# version 19
# baseline (speedup 1.0000x reference)
import contextlib
import numpy as np
import ml_dtypes
import concourse.bass as bass
import concourse.mybir as mybir
from concourse.bass_utils import run_bass_kernel_spmd

F32 = mybir.dt.float32
BF16 = mybir.dt.bfloat16
U8 = mybir.dt.uint8
I32 = mybir.dt.int32
AF = mybir.ActivationFunctionType
ALU = mybir.AluOpType
AX = mybir.AxisListType

E = 1024
H = 16
LT = 64
II = 2816
MAGIC = 12582912.0
NEG_LN8 = -2.0794415416798357
EPS = 1e-5
ENGS = ("sync", "gpsimd", "tensor", "vector", "scalar")


def bcast16(t, nb):
    a = t[:, 0:nb]
    return bass.AP(a.tensor, a.offset, [list(a.ap[0]), [1, nb], [0, 16]])


def view3(a, nb):
    return bass.AP(a.tensor, a.offset, [list(a.ap[0]), [16, nb], [1, 16]])


class Sem:
    __slots__ = ("h", "n", "name")

    def __init__(self, h, name):
        self.h = h
        self.n = 0
        self.name = name


NDMASEM = 8


class Bld:

    def __init__(self, nc, stack):
        self.nc = nc
        self.streams = {k: [] for k in ENGS}
        self.guar = {k: {} for k in ENGS}
        self.sems = {}
        for name in ("t", "v", "s", "bar"):
            h = stack.enter_context(nc.semaphore(name))
            self.sems[name] = Sem(h, name)
        self.dsems = {}
        for eng, pfx in (("sync", "ld"), ("gpsimd", "g")):
            lst = []
            for i in range(NDMASEM):
                nm = f"{pfx}{i}"
                h = stack.enter_context(nc.semaphore(nm))
                sm = Sem(h, nm)
                self.sems[nm] = sm
                lst.append(sm)
            self.dsems[eng] = lst
        self.drot = {"sync": 0, "gpsimd": 0}
        self.esem = {"tensor": self.sems["t"], "vector": self.sems["v"],
                     "scalar": self.sems["s"]}
        self.barno = 0

    def emit(self, eng, f, waits=(), inc=True):
        gmap = self.guar[eng]
        real = []
        for sm, val in waits:
            if val is None or val <= 0:
                continue
            if gmap.get(sm.name, 0) < val:
                real.append((sm.h, val))
                gmap[sm.name] = val
        if eng in ("sync", "gpsimd"):
            i = self.drot[eng]
            self.drot[eng] = (i + 1) % NDMASEM
            ds = self.dsems[eng][i]
            prev = ds.n
            ds.n += 16
            after = (ds, ds.n)

            def run(e, real=real, f=f, ds=ds, prev=prev):
                for h, vv in real:
                    e.wait_ge(h, vv)
                if prev:
                    e.wait_ge(ds.h, prev)
                f(e).then_inc(ds.h, 16)
            gmap[ds.name] = prev
        else:
            es = self.esem[eng]
            after = None
            if inc:
                es.n += 1
                after = es.n

            def run(e, real=real, f=f, inc=inc, es=es):
                for h, vv in real:
                    e.wait_ge(h, vv)
                r = f(e)
                if inc:
                    r.then_inc(es.h, 1)
        self.streams[eng].append(run)
        return after

    def barrier(self):
        bar = self.sems["bar"]
        self.barno += 1
        target = len(ENGS) * self.barno
        for eng in ENGS:
            if eng in ("sync", "gpsimd"):
                own = [(sm.h, sm.n) for sm in self.dsems[eng] if sm.n]

                def fin(e, own=own, bar=bar):
                    for h, vv in own:
                        e.wait_ge(h, vv)
                    e.nop().then_inc(bar.h, 1)
            else:
                es = self.esem[eng]

                def fin(e, es=es, n=es.n, bar=bar):
                    e.wait_ge(es.h, n)
                    e.nop().then_inc(bar.h, 1)
            self.streams[eng].append(fin)
        bar.n = target
        snap = {nm: sm.n for nm, sm in self.sems.items()}
        for eng in ENGS:
            self.streams[eng].append(
                lambda e, h=bar.h, v=target: e.wait_ge(h, v))
            g = self.guar[eng]
            for nm, vv in snap.items():
                g[nm] = max(g.get(nm, 0), vv)

    def flush(self, block):
        for eng in ENGS:
            ops = self.streams[eng]
            if not ops:
                continue
            getattr(block, eng)(
                lambda e, ops=tuple(ops): [op(e) for op in ops])
            self.streams[eng] = []


def quant_front(b, src, q, F, waits=()):
    nb = F // 16
    V, S = b.sems["v"], b.sems["s"]
    rot = 1 - getattr(b, "_qrot", 1)
    b._qrot = rot
    ps = getattr(b, "_qtok", None)
    if ps is None:
        ps = b._qtok = [
            {"sxn": None, "m2": 0, "cp1": 0, "cp2": 0, "f": 0}
            for _ in range(2)]
    p = ps[rot]
    src3 = bass.AP(src.tensor, src.offset,
                   [list(src.ap[0]), [16, nb], [1, 16]])
    qr = q[rot]
    am = qr["AM"][:, 0:nb]
    r6 = qr["R6"][:, 0:nb]
    c6 = qr["C6"][:, 0:nb]
    xn, ta = qr["XN"][:, 0:F], qr["TA"][:, 0:F]
    t1 = q[0]["T1"][:, 0:F]
    r1, r2, r3 = qr["R1"][:, 0:F], qr["R2"][:, 0:F], qr["R3"][:, 0:F]
    v_am = b.emit("vector", lambda e: e.tensor_reduce(
        am, src3, axis=AX.X, op=ALU.max, apply_absolute_value=True),
        waits=list(waits))
    v_cl = b.emit("vector", lambda e: e.tensor_scalar_max(am, am, 6e-12),
                  waits=[(V, v_am)])
    v_rc = b.emit("vector", lambda e: e.reciprocal(r6, am),
                  waits=[(V, v_cl)])
    v_r6 = b.emit("vector", lambda e: e.tensor_scalar_mul(r6, r6, 6.0),
                  waits=[(V, v_rc)])
    v_c6 = b.emit("vector", lambda e: e.tensor_scalar_mul(c6, am, 1.0 / 6.0),
                  waits=[(V, v_cl), (V, p["f"])])
    v_xn = b.emit("vector", lambda e: e.tensor_tensor(
        view3(xn, nb), src3, bcast16(qr["R6"], nb), op=ALU.mult),
        waits=[(V, v_r6)] + ([p["sxn"]] if p["sxn"] else []))
    s_ta = b.emit("scalar", lambda e: e.activation(ta, xn, AF.Abs),
                  waits=[(V, v_xn), (V, p["m2"])])
    g_a = b.emit("gpsimd", lambda e: e.tensor_scalar(
        t1, xn, 2.0, MAGIC, op0=ALU.mult, op1=ALU.add),
        waits=[(V, v_xn)])
    g_b = b.emit("gpsimd", lambda e: e.tensor_scalar(
        r1, t1, 0.5, -0.5 * MAGIC, op0=ALU.mult, op1=ALU.add),
        waits=[g_a, (V, p["cp2"])])
    g_c = b.emit("gpsimd", lambda e: e.tensor_scalar(
        t1, xn, 1.0, MAGIC, op0=ALU.mult, op1=ALU.add),
        waits=[(V, v_xn)])
    g_d = b.emit("gpsimd", lambda e: e.tensor_scalar(
        r2, t1, 1.0, -MAGIC, op0=ALU.mult, op1=ALU.add),
        waits=[g_c, (V, p["cp1"])])
    g_e = b.emit("gpsimd", lambda e: e.tensor_scalar(
        t1, xn, 0.5, MAGIC, op0=ALU.mult, op1=ALU.add),
        waits=[(V, v_xn)])
    g_f = b.emit("gpsimd", lambda e: e.tensor_scalar(
        r3, t1, 2.0, -2.0 * MAGIC, op0=ALU.mult, op1=ALU.add),
        waits=[g_e, (V, p["f"])])
    return {"rot": rot, "nb": nb, "F": F, "xn": v_xn, "qr": qr,
            "s_ta": s_ta, "g_e": g_e, "g_f": g_f}


def quant_back(b, ctx, dst, dst_waits=()):
    V, S = b.sems["v"], b.sems["s"]
    rot, nb, F = ctx["rot"], ctx["nb"], ctx["F"]
    p = b._qtok[rot]
    qr = ctx["qr"]
    ta = qr["TA"][:, 0:F]
    r1, r2, r3 = qr["R1"][:, 0:F], qr["R2"][:, 0:F], qr["R3"][:, 0:F]
    m1, m2 = qr["M1"][:, 0:F], qr["M2"][:, 0:F]
    dst3 = bass.AP(dst.tensor, dst.offset,
                   [list(dst.ap[0]), [16, nb], [1, 16]])
    v_m1 = b.emit("vector", lambda e: e.tensor_single_scalar(
        m1, ta, 2.0, op=ALU.is_le), waits=[(S, ctx["s_ta"])])
    v_m2 = b.emit("vector", lambda e: e.tensor_single_scalar(
        m2, ta, 4.0, op=ALU.is_le), waits=[(V, v_m1)])
    v_cp1 = b.emit("vector", lambda e: e.copy_predicated(r3, m2, r2),
                   waits=[(V, v_m2), ctx["g_f"]])
    v_cp2 = b.emit("vector", lambda e: e.copy_predicated(r3, m1, r1),
                   waits=[(V, v_cp1)])
    v_f = b.emit("vector", lambda e: e.tensor_tensor(
        dst3, view3(r3, nb), bcast16(qr["C6"], nb), op=ALU.mult),
        waits=[(V, v_cp2)] + list(dst_waits))
    p.update(f=v_f, sxn=ctx["g_e"], m2=v_m2, cp1=v_cp1, cp2=v_cp2)
    return v_f, ctx["xn"]


def quant_chunk(b, src, dst, q, F, waits=(), dst_waits=()):
    ctx = quant_front(b, src, q, F, waits=waits)
    return quant_back(b, ctx, dst, dst_waits=dst_waits)


def rsqrt_v(b, out, x, t1, t2, waits):
    V = b.sems["v"]
    xi = x.bitcast(I32)
    oi = out.bitcast(I32)
    v = b.emit("vector", lambda e: e.tensor_single_scalar(
        oi, xi, 1, op=ALU.arith_shift_right), waits=waits)
    v = b.emit("vector", lambda e: e.tensor_single_scalar(
        oi, oi, -1, op=ALU.mult), waits=[(V, v)])
    v = b.emit("vector", lambda e: e.tensor_single_scalar(
        oi, oi, 0x5F3759DF, op=ALU.add), waits=[(V, v)])
    for _ in range(3):
        v = b.emit("vector", lambda e: e.tensor_mul(t1, out, out),
                   waits=[(V, v)])
        v = b.emit("vector", lambda e: e.tensor_mul(t2, t1, x),
                   waits=[(V, v)])
        v = b.emit("vector", lambda e: e.tensor_scalar(
            t2, t2, -0.5, 1.5, op0=ALU.mult, op1=ALU.add), waits=[(V, v)])
        v = b.emit("vector", lambda e: e.tensor_mul(out, out, t2),
                   waits=[(V, v)])
    return v


def ln_tile(b, xs, out, w, src_waits):
    V, S = b.sems["v"], b.sems["s"]
    prev = getattr(b, "_lnprev", 0)
    prev_s = getattr(b, "_lnprev_s", 0)
    s1 = b.emit("scalar", lambda e: e.activation(
        w["DSQ"][:], xs, AF.Square, accum_out=w["VRR"][:]),
        waits=list(src_waits) + [(V, prev), (S, prev_s)])
    s2 = b.emit("scalar", lambda e: e.activation(
        w["DSQ"][:], xs, AF.Copy, accum_out=w["MUR"][:]),
        waits=[(S, s1)])
    b._lnprev_s = s2
    v2 = b.emit("vector", lambda e: e.tensor_scalar_mul(
        w["MU"][:], w["MUR"][:], 1.0 / 1024.0),
        waits=[(S, s2), (V, prev)])
    v3 = b.emit("vector", lambda e: e.tensor_mul(
        w["M2"][:], w["MU"][:], w["MU"][:]), waits=[(V, v2)])
    v5 = b.emit("vector", lambda e: e.scalar_tensor_tensor(
        w["VR"][:], w["VRR"][:], 1.0 / 1024.0, w["M2"][:],
        op0=ALU.mult, op1=ALU.subtract), waits=[(V, v3), (S, s1)])
    v5b = b.emit("vector", lambda e: e.tensor_scalar_add(
        w["VR"][:], w["VR"][:], EPS), waits=[(V, v5)])
    s3 = b.emit("scalar", lambda e: e.activation(
        w["LNV"][:], w["VR"][:], AF.Sqrt), waits=[(V, v5b), (S, prev_s)])
    b._lnprev_s = max(b._lnprev_s, s3)
    v6 = b.emit("vector", lambda e: e.reciprocal(
        w["RS"][:], w["LNV"][:]), waits=[(S, s3)])
    vf = b.emit("vector", lambda e: e.tensor_scalar(
        out, xs, w["MU"][:, 0:1], w["RS"][:, 0:1],
        op0=ALU.subtract, op1=ALU.mult), waits=[(V, v6)])
    b._lnprev = vf
    return vf


def build(T=4096, phases="WAB", acut=40):
    nc = bass.Bass("TRN2", target_bir_lowering=False, debug=False)
    NJ = T // 512
    NR = T // 128
    NK = T // 256

    dt_ = nc.dram_tensor
    x_d = dt_("x", [T, E], F32, kind="ExternalInput").ap()
    wq_d = dt_("wq", [E, E], F32, kind="ExternalInput").ap()
    wkd_d = dt_("wkd", [LT, E], F32, kind="ExternalInput").ap()
    wvd_d = dt_("wvd", [LT, E], F32, kind="ExternalInput").ap()
    q2l_d = dt_("q2l", [H * 64, 64], F32, kind="ExternalInput").ap()
    vup_d = dt_("vup", [H * 64, 64], F32, kind="ExternalInput").ap()
    wout_d = dt_("wout", [E, E], F32, kind="ExternalInput").ap()
    wg_d = dt_("wg", [II, E], F32, kind="ExternalInput").ap()
    wu_d = dt_("wu", [II, E], F32, kind="ExternalInput").ap()
    wd_d = dt_("wd", [E, II], F32, kind="ExternalInput").ap()
    id_d = dt_("ident", [128, 128], BF16, kind="ExternalInput").ap()
    mk_d = dt_("masks", [128, 4 * 512], BF16, kind="ExternalInput").ap()
    onr_d = dt_("onesr", [1, 128], BF16, kind="ExternalInput").ap()
    onc_d = dt_("onesc", [64, 1], BF16, kind="ExternalInput").ap()
    onc2_d = dt_("onesc2", [128, 33], BF16, kind="ExternalInput").ap()
    un4_d = dt_("unit4", [1, 16], BF16, kind="ExternalInput").ap()
    sel4_d = dt_("sel4", [4, 256], BF16, kind="ExternalInput").ap()
    out_d = dt_("out", [T, E], F32, kind="ExternalOutput").ap()
    sx1_d = dt_("s_x1", [T, E], F32, kind="Internal").ap()
    swg_d = dt_("s_wgT", [E, II], BF16, kind="Internal").ap()
    swu_d = dt_("s_wuT", [E, II], BF16, kind="Internal").ap()
    swd_d = dt_("s_wdT", [II, E], BF16, kind="Internal").ap()

    with contextlib.ExitStack() as top:
        en = top.enter_context
        b = Bld(nc, top)
        TS, V, S = (b.sems[k] for k in ("t", "v", "s"))

        IDN = en(nc.sbuf_tensor("IDN", [128, 128], BF16))
        ONR = en(nc.sbuf_tensor("ONR", [1, 128], BF16))
        ONC = en(nc.sbuf_tensor("ONC", [64, 1], BF16))
        ONC2 = en(nc.sbuf_tensor("ONC2", [128, 33], BF16))
        UN4 = en(nc.sbuf_tensor("UN4", [1, 16], BF16))
        SEL4 = en(nc.sbuf_tensor("SEL4", [4, 256], BF16))
        KM2 = en(nc.sbuf_tensor("KM2", [1, 1], F32))
        qsh = {}
        for nm, sh, d2 in (("AM", [128, 64], F32), ("R6", [128, 64], F32),
                           ("T1", [128, 1024], F32)):
            qsh[nm] = en(nc.sbuf_tensor("Q" + nm, sh, d2))
        q = []
        for qi in range(2):
            qc = dict(qsh)
            for nm, sh, d2 in (("C6", [128, 64], F32),
                               ("XN", [128, 1024], F32),
                               ("TA", [128, 1024], BF16),
                               ("R1", [128, 1024], BF16),
                               ("R2", [128, 1024], BF16),
                               ("R3", [128, 1024], BF16),
                               ("M1", [128, 1024], U8),
                               ("M2", [128, 1024], U8)):
                qc[nm] = en(nc.sbuf_tensor(f"Q{qi}{nm}", sh, d2))
            q.append(qc)
        w = {}
        for nm, sh in (("MU", [128, 1]), ("M2", [128, 1]), ("VR", [128, 1]),
                       ("LNV", [128, 1]), ("RS", [128, 1]), ("T2", [128, 1]),
                       ("MUR", [128, 1]), ("VRR", [128, 1])):
            w[nm] = en(nc.sbuf_tensor("W" + nm, sh, F32))
        w["DSQ"] = en(nc.sbuf_tensor("WDSQ", [128, E], BF16))

        c1 = b.emit("sync", lambda e: e.dma_start(out=IDN[:], in_=id_d))
        c3 = b.emit("sync", lambda e: e.dma_start(out=ONR[:], in_=onr_d))
        c4 = b.emit("sync", lambda e: e.dma_start(out=ONC[:], in_=onc_d))
        c5 = b.emit("sync", lambda e: e.dma_start(out=ONC2[:], in_=onc2_d))
        c6 = b.emit("sync", lambda e: e.dma_start(out=UN4[:], in_=un4_d))
        c7 = b.emit("sync", lambda e: e.dma_start(out=SEL4[:], in_=sel4_d))

        with contextlib.ExitStack() as wa:
            ea = wa.enter_context
            MSK = ea(nc.sbuf_tensor("MSK", [128, 4 * 512], BF16))
            KLT = ea(nc.sbuf_tensor("KLT", [128, T], BF16))
            VLA = ea(nc.sbuf_tensor("VLA", [128, NR * 65], BF16))
            WQLT = [ea(nc.sbuf_tensor(f"WQLT{c}", [128, E], BF16))
                    for c in range(8)]
            WOV2 = [ea(nc.sbuf_tensor(f"WOV2{i}", [128, E], BF16))
                    for i in range(H // 2)]
            WKDT = ea(nc.sbuf_tensor("WKDT", [128, 512], BF16))
            WVDT = ea(nc.sbuf_tensor("WVDT", [128, 512], BF16))
            XR = [ea(nc.sbuf_tensor(f"XR{i}", [128, E], F32))
                  for i in range(4)]
            LBB = ea(nc.sbuf_tensor("LBB", [128, E], BF16))

            cm = b.emit("sync", lambda e: e.dma_start(out=MSK[:], in_=mk_d))

            with contextlib.ExitStack() as pw:
                ew = pw.enter_context
                LB = ew(nc.sbuf_tensor("LB", [128, II], F32))
                QLB = ew(nc.sbuf_tensor("QLB", [128, 512], BF16))
                VUB = ew(nc.sbuf_tensor("VUB", [128, 512], BF16))
                VUPT = [ew(nc.sbuf_tensor(f"VUPT{r}", [128, 64], BF16))
                        for r in range(8)]
                VTMP = ew(nc.sbuf_tensor("VTMP", [64, 64], BF16))
                XTW = [ew(nc.sbuf_tensor(f"XTW{i}", [128, 8 * 128], BF16))
                       for i in range(2)]
                WOUTT = [ew(nc.sbuf_tensor(f"WOUTT{c}", [128, E], BF16))
                         for c in range(8)]
                KSQ = ew(nc.sbuf_tensor("KSQ", [64, T], BF16))
                KMT = ew(nc.sbuf_tensor("KMT", [1, 1], F32))
                PTW = [ew(nc.psum_tensor(f"PTW{i}", [128, 128], BF16))
                       for i in range(2)]
                PW1 = [ew(nc.psum_tensor(f"PW1{i}", [128, 512], F32))
                       for i in range(2)]
                PK = ew(nc.psum_tensor("PK", [64, 128], F32))
                PK2 = ew(nc.psum_tensor("PK2", [1, 512], F32))

                ptw_free = [0, 0]
                pw1_free = [0, 0]

                lw = b.emit("sync", lambda e: e.dma_start(
                    out=LB[0:64, 0:E], in_=wkd_d))
                lv = b.emit("sync", lambda e: e.dma_start(
                    out=LB[64:128, 0:E], in_=wvd_d))
                vc = b.emit("vector", lambda e: e.tensor_copy(
                    LBB[:, 0:E], LB[:, 0:E]), waits=[lw, lv])
                lbb_free = 0
                for src_b, dstt in ((0, WKDT), (64, WVDT)):
                    for c in range(8):
                        tt = b.emit("tensor", lambda e, c=c, src_b=src_b:
                                    e.transpose(
                                        PTW[c % 2][:, 0:64],
                                        LBB[src_b:src_b + 64,
                                            c * 128:(c + 1) * 128],
                                        IDN[src_b:src_b + 64,
                                            src_b:src_b + 64]),
                                    waits=[(V, vc), c1,
                                           (V, ptw_free[c % 2])])
                        lbb_free = tt
                        ptw_free[c % 2] = b.emit(
                            "vector", lambda e, c=c, dstt=dstt: e.tensor_copy(
                                dstt[:, c * 64:(c + 1) * 64],
                                PTW[c % 2][:, 0:64]), waits=[(TS, tt)])

                lqs = []
                for r in range(8):
                    lqs.append(b.emit("sync", lambda e, r=r: e.dma_start(
                        out=LB[:, r * 64:(r + 1) * 64],
                        in_=q2l_d[r * 128:(r + 1) * 128, :]),
                        waits=[(V, vc)]))
                vq = b.emit("vector", lambda e: e.tensor_copy(
                    QLB[:], LB[:, 0:512]), waits=lqs)
                lus = []
                for r in range(8):
                    lus.append(b.emit("sync", lambda e, r=r: e.dma_start(
                        out=LB[:, 512 + r * 64:512 + (r + 1) * 64],
                        in_=vup_d[r * 128:(r + 1) * 128, :]),
                        waits=[(V, vq)]))
                vu = b.emit("vector", lambda e: e.tensor_copy(
                    VUB[:], LB[:, 512:1024]), waits=lus)

                vupt_done = []
                for h in range(H):
                    r, base = h // 2, (h % 2) * 64
                    tt = b.emit("tensor", lambda e, r=r, base=base, h=h:
                                e.transpose(
                                    PTW[h % 2][0:64, 0:64],
                                    VUB[base:base + 64, r * 64:(r + 1) * 64],
                                    IDN[base:base + 64, base:base + 64]),
                                waits=[(V, vu), (V, ptw_free[h % 2])])
                    if base == 0:
                        dnv = b.emit("vector",
                                     lambda e, r=r, h=h: e.tensor_copy(
                                         VUPT[r][0:64, :],
                                         PTW[h % 2][0:64, 0:64]),
                                     waits=[(TS, tt)])
                        ptw_free[h % 2] = dnv
                        dn = (b.sems["v"], dnv)
                    else:
                        dn0 = b.emit("vector", lambda e, h=h: e.tensor_copy(
                            VTMP[:], PTW[h % 2][0:64, 0:64]),
                            waits=[(TS, tt)] +
                            ([vtmp_free] if h > 1 else []))
                        ptw_free[h % 2] = dn0
                        dn = b.emit("gpsimd", lambda e, r=r: e.dma_start(
                            out=VUPT[r][64:128, :], in_=VTMP[:]),
                            waits=[(V, dn0)])
                        vtmp_free = dn
                    vupt_done.append(dn)

                on_v = b.emit("vector", lambda e: e.memset(
                    bass.AP(VLA[:].tensor, VLA[:].offset + 64,
                            [list(VLA[:].ap[0]), [65, NR], [1, 1]]), 1.0))
                xr_free = {0: 0, 1: 0, 2: 0, 3: 0}
                xtw_free = [0, 0]
                pk_free = 0
                for r in range(NR):
                    ldr = b.emit("sync", lambda e, r=r: e.dma_start(
                        out=XR[r % 2][:], in_=x_d[r * 128:(r + 1) * 128, :]),
                        waits=[(V, xr_free[r % 2])])
                    cb = b.emit("vector", lambda e, r=r: e.tensor_copy(
                        LBB[:, 0:E], XR[r % 2][:]),
                        waits=[ldr, (TS, lbb_free)])
                    xr_free[r % 2] = cb
                    tcop = []
                    for c in range(8):
                        tt = b.emit("tensor", lambda e, c=c: e.transpose(
                            PTW[c % 2][:], LBB[:, c * 128:(c + 1) * 128],
                            IDN[:]),
                            waits=[(V, cb), (V, ptw_free[c % 2])])
                        if c == 7:
                            lbb_free = tt
                        ptw_free[c % 2] = b.emit(
                            "vector", lambda e, c=c, r=r: e.tensor_copy(
                                XTW[r % 2][:, c * 128:(c + 1) * 128],
                                PTW[c % 2][:]),
                            waits=[(TS, tt), (TS, xtw_free[r % 2])])
                        tcop.append(ptw_free[c % 2])
                    for c in range(8):
                        b.emit("tensor", lambda e, c=c, r=r: e.matmul(
                            PK[:, 0:128], WKDT[:, c * 64:(c + 1) * 64],
                            XTW[r % 2][:, c * 128:(c + 1) * 128],
                            start=(c == 0), stop=(c == 7)),
                            waits=[(V, tcop[c]), (V, pk_free)])
                    tk = b.sems["t"].n
                    pk_free = b.emit("vector", lambda e, r=r: e.tensor_copy(
                        KLT[0:64, r * 128:(r + 1) * 128],
                        PK[:, 0:128]),
                        waits=[(TS, tk)])
                    for c in range(8):
                        b.emit("tensor", lambda e, c=c, r=r: e.matmul(
                            PW1[r % 2][:, 0:64],
                            XTW[r % 2][:, c * 128:(c + 1) * 128],
                            WVDT[:, c * 64:(c + 1) * 64],
                            start=(c == 0), stop=(c == 7)),
                            waits=[(V, pw1_free[r % 2])])
                    tv = b.sems["t"].n
                    pw1_free[r % 2] = b.emit(
                        "vector", lambda e, r=r: e.tensor_copy(
                            VLA[:, r * 65:r * 65 + 64], PW1[r % 2][:, 0:64]),
                        waits=[(TS, tv), (V, on_v)])
                    xtw_free[r % 2] = b.sems["t"].n
                klat_done = pk_free
                kdup = b.emit("gpsimd", lambda e: e.dma_start(
                    out=KLT[64:128, 0:T], in_=KLT[0:64, 0:T]),
                    waits=[(V, klat_done)])

                sk = b.emit("scalar", lambda e: e.activation(
                    KSQ[:, 0:T], KLT[0:64, 0:T], AF.Square),
                    waits=[(V, klat_done)])
                km_prev = 0
                pk2_free = 0
                for blk in range(T // 512):
                    tk2 = b.emit("tensor", lambda e, blk=blk: e.matmul(
                        PK2[:], ONC[:, 0:1],
                        KSQ[:, blk * 512:(blk + 1) * 512],
                        start=True, stop=True),
                        waits=[(S, sk), c4, (V, pk2_free)])
                    vmx = b.emit("vector", lambda e: e.reduce_max(
                        KMT[0:1, 0:1], PK2[:], axis=AX.X),
                        waits=[(TS, tk2), (V, km_prev)])
                    pk2_free = vmx
                    if blk == 0:
                        km_prev = b.emit("vector", lambda e: e.tensor_copy(
                            KM2[:], KMT[0:1, 0:1]), waits=[(V, vmx)])
                    else:
                        km_prev = b.emit("vector", lambda e: e.tensor_max(
                            KM2[:], KM2[:], KMT[0:1, 0:1]), waits=[(V, vmx)])

                for r in range(8):
                    ldr = b.emit("sync", lambda e, r=r: e.dma_start(
                        out=XR[2][:], in_=wq_d[r * 128:(r + 1) * 128, :]),
                        waits=[(V, xr_free[2])])
                    cb = b.emit("vector", lambda e: e.tensor_copy(
                        LBB[:, 0:E], XR[2][:]),
                        waits=[ldr, (TS, lbb_free)])
                    xr_free[2] = cb
                    for hh in range(2):
                        h = 2 * r + hh
                        base = hh * 64
                        for c in range(8):
                            tt = b.emit(
                                "tensor",
                                lambda e, c=c, base=base, r=r: e.matmul(
                                    PW1[c % 2][:, 0:64],
                                    LBB[base:base + 64,
                                        c * 128:(c + 1) * 128],
                                    QLB[base:base + 64,
                                        r * 64:(r + 1) * 64],
                                    start=True, stop=True),
                                waits=[(V, cb), (V, vq),
                                       (V, pw1_free[c % 2])])
                            if c == 7:
                                lbb_free = tt
                            pw1_free[c % 2] = b.emit(
                                "vector", lambda e, c=c, h=h: e.tensor_copy(
                                    WQLT[c][:, h * 64:(h + 1) * 64],
                                    PW1[c % 2][:, 0:64]), waits=[(TS, tt)])

                for r in range(8):
                    ldr = b.emit("sync", lambda e, r=r: e.dma_start(
                        out=XR[2][:], in_=wout_d[r * 128:(r + 1) * 128, :]),
                        waits=[(V, xr_free[2])])
                    cb = b.emit("vector", lambda e: e.tensor_copy(
                        LBB[:, 0:E], XR[2][:]),
                        waits=[ldr, (TS, lbb_free)])
                    xr_free[2] = cb
                    for c in range(8):
                        tt = b.emit("tensor", lambda e, c=c: e.transpose(
                            PTW[c % 2][:], LBB[:, c * 128:(c + 1) * 128],
                            IDN[:]),
                            waits=[(V, cb), (V, ptw_free[c % 2])])
                        if c == 7:
                            lbb_free = tt
                        ptw_free[c % 2] = b.emit(
                            "vector", lambda e, c=c, r=r: e.tensor_copy(
                                WOUTT[c][:, r * 128:(r + 1) * 128],
                                PTW[c % 2][:]), waits=[(TS, tt)])
                woutt_done = ptw_free[1]

                for h in range(H):
                    r, base = h // 2, (h % 2) * 64
                    for nn in range(2):
                        tt = b.emit(
                            "tensor",
                            lambda e, r=r, base=base, h=h, nn=nn: e.matmul(
                                PW1[nn][0:64, 0:512],
                                VUPT[r][base:base + 64, :],
                                WOUTT[h // 2][base:base + 64,
                                              nn * 512:(nn + 1) * 512],
                                start=True, stop=True),
                            waits=[(V, woutt_done), (V, pw1_free[nn]),
                                   vupt_done[h]])
                        pw1_free[nn] = b.emit(
                            "vector", lambda e, h=h, nn=nn: e.tensor_copy(
                                WOV2[h // 2][(h % 2) * 64:
                                             (h % 2) * 64 + 64,
                                             nn * 512:(nn + 1) * 512],
                                PW1[nn][0:64, 0:512]), waits=[(TS, tt)])

                with contextlib.ExitStack() as ps1:
                    STGW = [ps1.enter_context(
                        nc.sbuf_tensor(f"STGW{c}", [128, II], BF16))
                        for c in range(8)]
                    stgw_tokens = []
                    ptwq_free = [(V, ptw_free[0]), (V, ptw_free[1])]
                    for w_d, s_d in ((wg_d, swg_d), (wu_d, swu_d)):
                        stg_free = list(stgw_tokens)
                        pend = [None]

                        def wflush(pr, ctx, stg_free=stg_free):
                            nonlocal lbb_free
                            qv, _ = quant_back(
                                b, ctx, LBB[:, 0:E],
                                dst_waits=[(TS, lbb_free)])
                            for c in range(8):
                                tt = b.emit(
                                    "tensor", lambda e, c=c: e.transpose(
                                        PTW[c % 2][:],
                                        LBB[:, c * 128:(c + 1) * 128],
                                        IDN[:]),
                                    waits=[(V, qv),
                                           ptwq_free[c % 2]])
                                if c == 7:
                                    lbb_free = tt
                                ptwq_free[c % 2] = (S, b.emit(
                                    "scalar",
                                    lambda e, c=c, pr=pr: e.activation(
                                        STGW[c][:, pr * 128:(pr + 1) * 128],
                                        PTW[c % 2][:], AF.Copy),
                                    waits=[(TS, tt)] + stg_free))

                        for r in range(22):
                            ldr = b.emit("sync", lambda e, r=r, w_d=w_d:
                                         e.dma_start(
                                             out=XR[2][:],
                                             in_=w_d[r * 128:(r + 1) * 128,
                                                     :]),
                                         waits=[(V, xr_free[2])])
                            ctx = quant_front(b, XR[2][:, 0:E], q, E,
                                              waits=[ldr])
                            xr_free[2] = ctx["xn"]
                            if pend[0] is not None:
                                wflush(*pend[0])
                            pend[0] = (r, ctx)
                        wflush(*pend[0])
                        pend[0] = None
                        stgw_tokens = []
                        for c in range(8):
                            stgw_tokens.append(b.emit(
                                "gpsimd", lambda e, c=c, s_d=s_d:
                                e.dma_start(
                                    out=s_d[c * 128:(c + 1) * 128, :],
                                    in_=STGW[c][:]),
                                waits=list(ptwq_free)))
                    stgw_store_done = list(stgw_tokens)

                with contextlib.ExitStack() as ps2:
                    STGD = [ps2.enter_context(
                        nc.sbuf_tensor(f"STGD{c}", [128, E], BF16))
                        for c in range(22)]
                    WDB = ps2.enter_context(
                        nc.sbuf_tensor("WDB", [128, II], BF16))
                    lb_free = vu
                    wdb_free = 0
                    pendd = [None]
                    rowq = {}

                    def dflushrow(r0):
                        nonlocal wdb_free
                        qvf = rowq.pop(r0)
                        for c in range(22):
                            tt = b.emit("tensor", lambda e, c=c: e.transpose(
                                PTW[c % 2][:],
                                WDB[:, c * 128:(c + 1) * 128],
                                IDN[:]),
                                waits=[(V, qvf[min(c // 8, 2)]),
                                       ptwq_free[c % 2]])
                            if c == 21:
                                wdb_free = tt
                            ptwq_free[c % 2] = (S, b.emit(
                                "scalar", lambda e, c=c, r0=r0: e.activation(
                                    STGD[c][:, r0 * 128:(r0 + 1) * 128],
                                    PTW[c % 2][:], AF.Copy),
                                waits=[(TS, tt)]))

                    def dflush(item):
                        r0, si, ctx, off, Fq = item
                        qv, _ = quant_back(b, ctx, WDB[:, off:off + Fq],
                                           dst_waits=[(TS, wdb_free)])
                        rowq.setdefault(r0, []).append(qv)
                        if si == 2:
                            dflushrow(r0)

                    for r in range(8):
                        ldr = b.emit("sync", lambda e, r=r: e.dma_start(
                            out=LB[:, 0:II], in_=wd_d[r * 128:(r + 1) * 128,
                                                      :]),
                            waits=[(V, lb_free)] + stgw_store_done)
                        for si, (off, Fq) in enumerate(
                                ((0, 1024), (1024, 1024), (2048, 768))):
                            ctx = quant_front(b, LB[:, off:off + Fq],
                                              q, Fq, waits=[ldr])
                            if pendd[0] is not None:
                                dflush(pendd[0])
                            pendd[0] = (r, si, ctx, off, Fq)
                        lb_free = ctx["xn"]
                    dflush(pendd[0])
                    for c in range(22):
                        b.emit("gpsimd", lambda e, c=c: e.dma_start(
                            out=swd_d[c * 128:(c + 1) * 128, :],
                            in_=STGD[c][:]), waits=list(ptwq_free))

            b.barrier()

            if phases == "W":
                stf = {}
                for r in range(T // 128):
                    lw_ = [stf[r % 4]] if (r % 4) in stf else []
                    tok = b.emit("sync", lambda e, r=r: e.dma_start(
                        out=XR[r % 4][:],
                        in_=x_d[r * 128:(r + 1) * 128, :]), waits=lw_)
                    stf[r % 4] = b.emit("gpsimd", lambda e, r=r: e.dma_start(
                        out=out_d[r * 128:(r + 1) * 128, :],
                        in_=XR[r % 4][:]), waits=[tok])
                b.barrier()

            if phases != "W":
             with contextlib.ExitStack() as pa:
                ep = pa.enter_context
                XTJ = [ep(nc.sbuf_tensor(f"XTJ{c}", [128, 512], BF16))
                       for c in range(8)]
                XP = [ep(nc.sbuf_tensor(f"XP{i}", [128, E], F32))
                      for i in range(2)]
                XRB = [ep(nc.sbuf_tensor(f"XRB{i}", [128, E], BF16))
                       for i in range(2)]
                QL2 = [ep(nc.sbuf_tensor(f"QL2{i}", [128, 512], BF16))
                       for i in range(H // 2)]
                SQ2 = [ep(nc.sbuf_tensor(f"SQ2{i}", [128, 512], BF16))
                       for i in range(2)]
                QMh = ep(nc.sbuf_tensor("QMh", [1, 16], F32))
                CJF = ep(nc.sbuf_tensor("CJF", [1, 16], F32))
                CJB2 = ep(nc.sbuf_tensor("CJB2", [1, 16], BF16))
                NCJH = ep(nc.sbuf_tensor("NCJH", [128, 16], F32))
                QM1 = ep(nc.sbuf_tensor("AQM1", [1, 1], F32))
                CJ = ep(nc.sbuf_tensor("CJ", [1, 1], F32))
                CJ2 = ep(nc.sbuf_tensor("CJ2", [1, 1], F32))
                CJT = ep(nc.sbuf_tensor("CJT", [1, 1], F32))
                CJN = ep(nc.sbuf_tensor("CJN", [1, 1], F32))
                CJB = ep(nc.sbuf_tensor("CJB", [1, 1], BF16))
                NCJ = ep(nc.sbuf_tensor("NCJ", [128, 1], F32))
                PBUF = [ep(nc.sbuf_tensor(f"PBUF{i}", [128, 1024], BF16))
                        for i in range(2)]
                CUB = [ep(nc.sbuf_tensor(f"CUB{i}", [64, 512], F32))
                       for i in range(4)]
                RDW = ep(nc.sbuf_tensor("RDW", [1, 2048], BF16))
                RD4R = ep(nc.sbuf_tensor("RD4R", [4, 512], F32))
                RDB4 = [ep(nc.sbuf_tensor(f"RDB4{i}", [4, 512], BF16))
                        for i in range(2)]
                CT2 = [ep(nc.sbuf_tensor(f"CT2{i}", [128, 512], BF16))
                       for i in range(H // 2)]
                XS = ep(nc.sbuf_tensor("XS", [128, E], F32))
                X1B = [ep(nc.sbuf_tensor(f"X1B{i}", [128, E], F32))
                       for i in range(2)]
                PQ = ep(nc.psum_tensor("PQ", [128, 512], F32))
                PSA = [ep(nc.psum_tensor(f"PSA{i}", [128, 1024], F32))
                       for i in range(2)]
                PC = ep(nc.psum_tensor("PC", [128, 512], F32))
                PC2 = ep(nc.psum_tensor("PC2", [128, 512], F32))
                PO0 = ep(nc.psum_tensor("PO0", [128, 512], F32))

                xr_free = {0: 0, 1: 0, 2: 0, 3: 0}
                xrb_free = {0: [], 1: []}
                x1b_free = [None, None]
                pq_free_v = 0
                sqb_free = [0, 0]
                exp_bank = [0, 0]
                pvp_bank = [0, 0]
                pcb_free = [0, 0]
                rdb_free = [0, 0]
                po_free_v = 0
                xs_free = 0

                xp_free = [0, 0]
                xtj_tfree = 0

                def emit_xtj(J):
                    toks = []
                    for tt in range(4):
                        ldp = b.emit("sync", lambda e, J=J, tt=tt:
                                     e.dma_start(
                                         out=XP[tt % 2][:],
                                         in_=x_d[J * 512 + tt * 128:
                                                 J * 512 + (tt + 1) * 128,
                                                 :]),
                                     waits=[(V, xp_free[tt % 2])])
                        cb = b.emit("vector", lambda e, tt=tt: e.tensor_copy(
                            XRB[tt % 2][:], XP[tt % 2][:]),
                            waits=[ldp] + xrb_free[tt % 2])
                        xp_free[tt % 2] = cb
                        xrb_free[tt % 2] = []
                        for c in range(8):
                            tok = b.emit("sync", lambda e, c=c, tt=tt:
                                         e.dma_start_transpose(
                                             XTJ[c][:, tt * 128:
                                                    (tt + 1) * 128],
                                             XRB[tt % 2][:, c * 128:
                                                         (c + 1) * 128]),
                                         waits=[(V, cb), (TS, xtj_tfree)])
                            toks.append(tok)
                            xrb_free[tt % 2].append(tok)
                    return toks

                pending_xtj = emit_xtj(0)
                for J in range(NJ):
                    xtj_toks = pending_xtj
                    ldr_toks = []
                    for tt in range(4):
                        ldr_toks.append(b.emit(
                            "sync", lambda e, J=J, tt=tt: e.dma_start(
                                out=XR[tt][:],
                                in_=x_d[J * 512 + tt * 128:
                                        J * 512 + (tt + 1) * 128, :]),
                            waits=[(V, xr_free[tt])]))

                    qlb_ready = [0] * (H // 2)
                    qm_last = 0
                    pend_qn = None
                    ncj_tok = {}
                    pend_bc = []

                    def emit_bc(hp, v4):
                        nonlocal pq_free_v
                        tb2 = b.emit("tensor", lambda e, hp=hp: e.matmul(
                            PQ[:, 0:2], ONR[0:1, :],
                            CJB2[0:1, 2 * hp:2 * hp + 2],
                            start=True, stop=True),
                            waits=[(V, v4), (V, pq_free_v), c3])
                        nv = b.emit("vector", lambda e, hp=hp: e.tensor_copy(
                            NCJH[:, 2 * hp:2 * hp + 2], PQ[:, 0:2]),
                            waits=[(TS, tb2)])
                        pq_free_v = nv
                        ncj_tok[2 * hp] = (V, nv)
                        ncj_tok[2 * hp + 1] = (V, nv)

                    def emit_qn(hp, sq):
                        nonlocal qm_last
                        tn = b.emit("tensor", lambda e, hp=hp: e.matmul(
                            PO0[0:33, 0:512], ONC2[:, 0:33], SQ2[hp % 2][:],
                            start=True, stop=True),
                            waits=[(S, sq), (V, qm_last), c5])
                        sqb_free[hp % 2] = tn
                        va = b.emit("vector", lambda e, hp=hp: e.reduce_max(
                            QMh[0:1, 2 * hp:2 * hp + 1], PO0[0:1, 0:512],
                            axis=AX.X), waits=[(TS, tn)])
                        vb = b.emit("vector", lambda e, hp=hp: e.reduce_max(
                            QMh[0:1, 2 * hp + 1:2 * hp + 2],
                            PO0[32:33, 0:512],
                            axis=AX.X), waits=[(V, va)])
                        qm_last = vb
                        vm = b.emit("vector", lambda e, hp=hp:
                                    e.tensor_scalar(
                                        QMh[0:1, 2 * hp:2 * hp + 2],
                                        QMh[0:1, 2 * hp:2 * hp + 2],
                                        KM2[0:1, 0:1], None, op0=ALU.mult),
                                    waits=[(V, vb)])
                        sqs = b.emit("scalar", lambda e, hp=hp: e.activation(
                            CJF[0:1, 2 * hp:2 * hp + 2],
                            QMh[0:1, 2 * hp:2 * hp + 2],
                            AF.Sqrt, scale=1.0 / 64.0), waits=[(V, vm)])
                        v4 = b.emit("vector", lambda e, hp=hp:
                                    e.tensor_scalar_mul(
                                        CJB2[0:1, 2 * hp:2 * hp + 2],
                                        CJF[0:1, 2 * hp:2 * hp + 2], -1.0),
                                    waits=[(S, sqs)])
                        if pend_bc:
                            emit_bc(*pend_bc.pop())
                        pend_bc.append((hp, v4))

                    for hp in range(H // 2):
                        for c in range(8):
                            b.emit("tensor", lambda e, c=c, hp=hp: e.matmul(
                                PQ[:], WQLT[c][:, hp * 128:(hp + 1) * 128],
                                XTJ[c][:], start=(c == 0), stop=(c == 7)),
                                waits=xtj_toks + [(V, pq_free_v)])
                        tq = b.sems["t"].n
                        qlb_ready[hp] = b.emit(
                            "vector", lambda e, hp=hp: e.tensor_copy(
                                QL2[hp][:], PQ[:]), waits=[(TS, tq)])
                        pq_free_v = qlb_ready[hp]
                        sq = b.emit("scalar", lambda e, hp=hp: e.activation(
                            SQ2[hp % 2][:], QL2[hp][:], AF.Square),
                            waits=[(V, qlb_ready[hp]),
                                   (TS, sqb_free[hp % 2])])
                        if pend_qn is not None:
                            emit_qn(*pend_qn)
                        pend_qn = (hp, sq)
                    emit_qn(*pend_qn)
                    while pend_bc:
                        emit_bc(*pend_bc.pop())
                    xtj_tfree = b.sems["t"].n
                    if J + 1 < NJ:
                        pending_xtj = emit_xtj(J + 1)

                    npairs = 2 * (J + 1)
                    steps = [(h, p) for h in range(H) for p in range(npairs)]
                    spair_t = {}
                    exp_tok = {}
                    pend_norm = []

                    def emit_spair(st):
                        h, p = st
                        bk = (h * npairs + p) % 2
                        for half in range(2):
                            i = 2 * p + half
                            b.emit("tensor",
                                   lambda e, i=i, h=h, bk=bk,
                                   half=half: e.matmul(
                                       PSA[bk][:, half * 512:
                                               half * 512 + 512],
                                       KLT[(h % 2) * 64:
                                           (h % 2) * 64 + 64,
                                           i * 128:(i + 1) * 128],
                                       QL2[h // 2][(h % 2) * 64:
                                                   (h % 2) * 64 + 64, :],
                                       start=True, stop=True),
                                   waits=[(V, qlb_ready[h // 2]),
                                          (S, exp_bank[bk])])
                        spair_t[st] = b.sems["t"].n

                    def emit_exp(st, J=J):
                        h, p = st
                        bk = (h * npairs + p) % 2
                        ex = b.emit("scalar", lambda e, bk=bk, h=h:
                                    e.activation(
                                        PBUF[bk][:], PSA[bk][:],
                                        AF.Exp, scale=0.125,
                                        bias=NCJH[:, h:h + 1]),
                                    waits=[(TS, spair_t[st]),
                                           (TS, pvp_bank[bk]),
                                           ncj_tok[h]])
                        exp_bank[bk] = ex
                        src = [(S, ex)]
                        if p >= 2 * J:
                            for half in range(2):
                                cdiag = 2 * p + half - 4 * J
                                mv = b.emit(
                                    "vector",
                                    lambda e, bk=bk, half=half,
                                    cdiag=cdiag: e.tensor_mul(
                                        PBUF[bk][:, half * 512:
                                                 half * 512 + 512],
                                        PBUF[bk][:, half * 512:
                                                 half * 512 + 512],
                                        MSK[:, cdiag * 512:
                                            (cdiag + 1) * 512]),
                                    waits=[(S, ex), cm])
                            src = [(V, mv)]
                        exp_tok[st] = src

                    def emit_pvpair(st, npairs=npairs):
                        h, p = st
                        bk = (h * npairs + p) % 2
                        pcb = PC if h % 2 == 0 else PC2
                        w8 = list(exp_tok[st])
                        if p == 0:
                            w8.append((V, pcb_free[h % 2]))
                        for half in range(2):
                            i = 2 * p + half
                            st_f = (p == 0 and half == 0)
                            sp_f = (p == npairs - 1 and half == 1)
                            b.emit("tensor",
                                   lambda e, i=i, bk=bk, half=half,
                                   st_f=st_f, sp_f=sp_f, pcb=pcb:
                                   e.matmul(
                                       pcb[0:65, :],
                                       VLA[:, i * 65:i * 65 + 65],
                                       PBUF[bk][:, half * 512:
                                                half * 512 + 512],
                                       start=st_f, stop=sp_f),
                                   waits=w8)
                            w8 = []
                        pvp_bank[bk] = b.sems["t"].n

                    cu_toks = [0, 0, 0, 0]

                    def norm_dve(h):
                        pcb = PC if h % 2 == 0 else PC2
                        g = h % 4
                        pv_last = b.sems["t"].n
                        r0 = b.emit("vector", lambda e, pcb=pcb, g=g:
                                    e.tensor_copy(
                                        RDW[0:1, g * 512:(g + 1) * 512],
                                        pcb[64:65, :]),
                                    waits=[(TS, pv_last)])
                        cu = b.emit("vector", lambda e, pcb=pcb, g=g:
                                    e.tensor_copy(CUB[g][:],
                                                  pcb[0:64, :]),
                                    waits=[(TS, pv_last)])
                        cu_toks[g] = cu
                        pcb_free[h % 2] = cu
                        return r0

                    def norm_v_group(h3, r0):
                        nonlocal po_free_v
                        gp = (h3 // 4) % 2
                        for g in range(4):
                            b.emit("tensor", lambda e, g=g: e.matmul(
                                PO0[0:4, 0:512], UN4[0:1, g * 4:g * 4 + 4],
                                RDW[0:1, g * 512:(g + 1) * 512],
                                start=(g == 0), stop=(g == 3)),
                                waits=[(V, r0), (V, po_free_v), c6])
                        tgl = b.sems["t"].n
                        gr = b.emit("vector", lambda e: e.reciprocal(
                            RD4R[:], PO0[0:4, 0:512]), waits=[(TS, tgl)])
                        gb = b.emit("vector", lambda e, gp=gp: e.tensor_copy(
                            RDB4[gp][:], RD4R[:]),
                            waits=[(V, gr), (TS, rdb_free[gp])])
                        po_free_v = gr
                        return gb

                    def norm_pe_group(h3, gb):
                        nonlocal po_free_v
                        gp = (h3 // 4) % 2
                        for hh in range(h3 - 3, h3 + 1):
                            tb2 = b.emit("tensor", lambda e, hh=hh, gp=gp:
                                         e.matmul(
                                             PO0[0:64, :],
                                             SEL4[:, (hh % 4) * 64:
                                                  (hh % 4) * 64 + 64],
                                             RDB4[gp][:],
                                             start=True, stop=True),
                                         waits=[(V, gb), (V, po_free_v),
                                                c7])
                            cn = b.emit("vector", lambda e, hh=hh:
                                        e.tensor_mul(
                                            CT2[hh // 2][(hh % 2) * 64:
                                                         (hh % 2) * 64 + 64,
                                                         :],
                                            CUB[hh % 4][:], PO0[0:64, :]),
                                        waits=[(V, cu_toks[hh % 4]),
                                               (TS, tb2)])
                            po_free_v = cn
                        rdb_free[gp] = b.sems["t"].n

                    emit_spair(steps[0])
                    if len(steps) > 1:
                        emit_spair(steps[1])
                    emit_exp(steps[0])
                    for idx, st in enumerate(steps):
                        h, p = st
                        emit_pvpair(st)
                        if idx + 2 < len(steps):
                            emit_spair(steps[idx + 2])
                        if idx + 1 < len(steps):
                            emit_exp(steps[idx + 1])
                        if p == npairs - 1:
                            if pend_norm:
                                norm_pe_group(*pend_norm.pop())
                            r0 = norm_dve(h)
                            if h % 4 == 3:
                                pend_norm.append((h, norm_v_group(h, r0)))
                    if pend_norm:
                        norm_pe_group(*pend_norm.pop())
                    ctxn_all = po_free_v

                    for tt in range(4):
                        for hp in range(H // 2):
                            b.emit("tensor", lambda e, hp=hp, tt=tt:
                                   e.matmul(
                                       PO0[:],
                                       CT2[hp][:, tt * 128:(tt + 1) * 128],
                                       WOV2[hp][:, 0:512],
                                       start=(hp == 0),
                                       stop=(hp == H // 2 - 1)),
                                   waits=[(V, ctxn_all), (V, po_free_v),
                                          (V, pcb_free[0]),
                                          (V, pcb_free[1])])
                            b.emit("tensor", lambda e, hp=hp, tt=tt:
                                   e.matmul(
                                       PC[:],
                                       CT2[hp][:, tt * 128:(tt + 1) * 128],
                                       WOV2[hp][:, 512:1024],
                                       start=(hp == 0),
                                       stop=(hp == H // 2 - 1)),
                                   waits=[])
                        t_o = b.sems["t"].n
                        a1 = b.emit("vector", lambda e, tt=tt: e.tensor_add(
                            XS[:, 0:512], PO0[:], XR[tt][:, 0:512]),
                            waits=[(TS, t_o), (V, xs_free), ldr_toks[tt]])
                        a2 = b.emit("vector", lambda e, tt=tt: e.tensor_add(
                            XS[:, 512:1024], PC[:], XR[tt][:, 512:1024]),
                            waits=[(TS, t_o)])
                        po_free_v = a2
                        pcb_free[0] = a2
                        xr_free[tt] = a2
                        lnw = [(V, a2)]
                        if x1b_free[tt % 2]:
                            lnw.append(x1b_free[tt % 2])
                        lnv = ln_tile(b, XS[:], X1B[tt % 2][:], w, lnw)
                        xs_free = lnv
                        x1b_free[tt % 2] = b.emit(
                            "gpsimd", lambda e, J=J, tt=tt: e.dma_start(
                                out=sx1_d[J * 512 + tt * 128:
                                          J * 512 + (tt + 1) * 128, :],
                                in_=X1B[tt % 2][:]), waits=[(V, lnv)])
                    b.barrier()
                    xr_free = {0: 0, 1: 0, 2: 0, 3: 0}
                    xrb_free = {0: [], 1: []}
                    x1b_free = [None, None]
                    pq_free_v = 0
                    sqb_free = [0, 0]
                    exp_bank = [0, 0]
                    pvp_bank = [0, 0]
                    pcb_free = [0, 0]
                    rdb_free = [0, 0]
                    po_free_v = 0
                    xs_free = 0

        if phases == "WA":
            stf2 = None
            for r in range(T // 128):
                tok = b.emit("sync", lambda e, r=r: e.dma_start(
                    out=w["SQ"][:], in_=sx1_d[r * 128:(r + 1) * 128, :]),
                    waits=[stf2] if stf2 else [])
                stf2 = b.emit("gpsimd", lambda e, r=r: e.dma_start(
                    out=out_d[r * 128:(r + 1) * 128, :], in_=w["SQ"][:]),
                    waits=[tok])
            b.barrier()

        if phases == "WAB":
         with contextlib.ExitStack() as pb:
            ep = pb.enter_context
            WGT = [ep(nc.sbuf_tensor(f"WGT{k}", [128, II], BF16))
                   for k in range(8)]
            WUT = [ep(nc.sbuf_tensor(f"WUT{k}", [128, II], BF16))
                   for k in range(8)]
            WDT = [ep(nc.sbuf_tensor(f"WDT{j}", [128, E], BF16))
                   for j in range(22)]
            X1R = [ep(nc.sbuf_tensor(f"X1R{u}", [128, E], F32))
                   for u in range(2)]
            X1QB = [ep(nc.sbuf_tensor(f"X1QB{u}", [128, E], BF16))
                    for u in range(2)]
            X1QT = ep(nc.sbuf_tensor("X1QT", [128, 2048], BF16))
            HCB = [ep(nc.sbuf_tensor(f"HCB{i}", [128, 1024], F32))
                   for i in range(2)]
            HQB = [ep(nc.sbuf_tensor(f"HQB{i}", [128, 1024], BF16))
                   for i in range(2)]
            HQT = ep(nc.sbuf_tensor("HQT", [128, 5632], BF16))
            PG = [ep(nc.psum_tensor(f"PG{i}", [128, 512], F32))
                  for i in range(2)]
            PU = [ep(nc.psum_tensor(f"PU{i}", [128, 512], F32))
                  for i in range(2)]
            PD = [ep(nc.psum_tensor(f"PD{i}", [128, 512], F32))
                  for i in range(2)]

            wload = []
            for k in range(8):
                wload.append(b.emit("sync", lambda e, k=k: e.dma_start(
                    out=WGT[k][:], in_=swg_d[k * 128:(k + 1) * 128, :])))
            for k in range(8):
                wload.append(b.emit("sync", lambda e, k=k: e.dma_start(
                    out=WUT[k][:], in_=swu_d[k * 128:(k + 1) * 128, :])))
            for j in range(22):
                wload.append(b.emit("sync", lambda e, j=j: e.dma_start(
                    out=WDT[j][:], in_=swd_d[j * 128:(j + 1) * 128, :])))

            st = {
                "outb_tok": [None, None],
                "gsf": [0, 0],
                "puf": [0, 0],
                "pd_free": 0,
                "hcb_free": [0, 0],
                "x1r_free": [0, 0],
                "x1q_toks": {0: [], 1: []},
                "hq_free": [[], []],
                "x1qt_toks": {0: [], 1: []},
                "x1qt_tfree": [0, 0],
                "hqt_tfree": [0, 0],
                "ldr_tok": [None, None],
                "pend_fin": None,
            }
            NRU = T // 128

            def stage1_front(ru):
                par = ru % 2
                s1w = [(V, st["x1r_free"][par])] + wload
                if st["outb_tok"][par] is not None:
                    s1w.append(st["outb_tok"][par])
                st["ldr_tok"][par] = b.emit(
                    "sync", lambda e, ru=ru, par=par: e.dma_start(
                        out=X1R[par][:],
                        in_=sx1_d[ru * 128:(ru + 1) * 128, :]),
                    waits=s1w)
                return quant_front(b, X1R[par][:, 0:E], q, E,
                                   waits=[st["ldr_tok"][par]])

            def stage1_back(ru, ctx):
                par = ru % 2
                qv, _ = quant_back(b, ctx, X1QB[par][:, 0:E],
                                   dst_waits=st["x1q_toks"][par])
                st["x1q_toks"][par] = []
                st["x1qt_toks"][par] = []
                for c in range(8):
                    tok = b.emit(
                        "sync", lambda e, c=c, par=par:
                        e.dma_start_transpose(
                            X1QT[:, c * 256 + par * 128:
                                 c * 256 + (par + 1) * 128],
                            X1QB[par][:, c * 128:(c + 1) * 128]),
                        waits=[(V, qv), (TS, st["x1qt_tfree"][par])])
                    st["x1qt_toks"][par].append(tok)
                    st["x1q_toks"][par].append(tok)

            def stage1(ru):
                stage1_back(ru, stage1_front(ru))

            def stage2(ru):
                par = ru % 2
                evs = {}
                hqv = {}
                hctx = {}

                def hback(sb):
                    ctx, hb, Fq = hctx[sb]
                    hv, hsf = quant_back(
                        b, ctx, HQB[hb][:, 0:Fq],
                        dst_waits=list(st["hq_free"][hb]))
                    hqv[sb] = hv
                    st["hcb_free"][hb] = hsf

                def emit_trans(sb):
                    Fq = 1024 if sb < 2 else 768
                    hb = sb % 2
                    toks = []
                    for cc in range(Fq // 128):
                        jj = sb * 8 + cc
                        tok = b.emit(
                            "sync", lambda e, cc=cc, jj=jj, hb=hb:
                            e.dma_start_transpose(
                                HQT[:, jj * 256 + par * 128:
                                    jj * 256 + (par + 1) * 128],
                                HQB[hb][:, cc * 128:(cc + 1) * 128]),
                            waits=[(V, hqv[sb]),
                                   (TS, st["hqt_tfree"][par])])
                        toks.append(tok)
                    st["hq_free"][hb] = list(toks)
                    evs[sb] = toks

                def emit_down(sb):
                    for j in range(sb * 8, min(sb * 8 + 8, 22)):
                        b.emit("tensor", lambda e, j=j: e.matmul(
                            PD[0][:], HQT[:, j * 256 + par * 128:
                                          j * 256 + (par + 1) * 128],
                            WDT[j][:, 0:512],
                            start=(j == 0), stop=(j == 21)),
                            waits=evs[sb] + [(V, st["pd_free"])])
                        b.emit("tensor", lambda e, j=j: e.matmul(
                            PD[1][:], HQT[:, j * 256 + par * 128:
                                          j * 256 + (par + 1) * 128],
                            WDT[j][:, 512:1024],
                            start=(j == 0), stop=(j == 21)), waits=[])

                for ic in range(6):
                    Fi = 512 if ic < 5 else 256
                    bk = ic % 2
                    hb = (ic // 2) % 2
                    off = (ic % 2) * 512
                    if ic == 2 and st["pend_fin"] is not None:
                        st["pend_fin"]()
                        st["pend_fin"] = None
                    for kk in range(8):
                        b.emit("tensor", lambda e, kk=kk, ic=ic, par=par,
                               Fi=Fi, bk=bk: e.matmul(
                                   PG[bk][:, 0:Fi],
                                   X1QT[:, kk * 256 + par * 128:
                                        kk * 256 + (par + 1) * 128],
                                   WGT[kk][:, ic * 512:ic * 512 + Fi],
                                   start=(kk == 0), stop=(kk == 7)),
                               waits=st["x1qt_toks"][par]
                               + [(S, st["gsf"][bk])])
                    tg = b.sems["t"].n
                    sg = b.emit("scalar", lambda e, bk=bk, Fi=Fi,
                                hb=hb, off=off:
                                e.activation(HCB[hb][:, off:off + Fi],
                                             PG[bk][:, 0:Fi], AF.Silu),
                                waits=[(TS, tg),
                                       (V, st["hcb_free"][hb])])
                    st["gsf"][bk] = sg
                    for kk in range(8):
                        b.emit("tensor", lambda e, kk=kk, ic=ic, par=par,
                               Fi=Fi, bk=bk: e.matmul(
                                   PU[bk][:, 0:Fi],
                                   X1QT[:, kk * 256 + par * 128:
                                        kk * 256 + (par + 1) * 128],
                                   WUT[kk][:, ic * 512:ic * 512 + Fi],
                                   start=(kk == 0), stop=(kk == 7)),
                               waits=[(V, st["puf"][bk])])
                    tu = b.sems["t"].n
                    if ic == 5:
                        st["x1qt_tfree"][par] = tu
                    hm = b.emit("vector", lambda e, bk=bk, Fi=Fi,
                                hb=hb, off=off:
                                e.tensor_mul(HCB[hb][:, off:off + Fi],
                                             HCB[hb][:, off:off + Fi],
                                             PU[bk][:, 0:Fi]),
                                waits=[(S, sg), (TS, tu)])
                    st["puf"][bk] = hm
                    if ic % 2 == 1 or ic == 5:
                        sb = ic // 2
                        Fq = off + Fi
                        hctx[sb] = (quant_front(
                            b, HCB[hb][:, 0:Fq], q, Fq,
                            waits=[(V, hm)]), hb, Fq)
                    if ic == 3:
                        hback(0)
                        emit_trans(0)
                    if ic == 5:
                        hback(1)
                        emit_trans(1)
                        if ru >= 1 and ru + 1 < NRU:
                            x1ctx = stage1_front(ru + 1)
                        hback(2)
                        emit_trans(2)
                        if ru >= 1 and ru + 1 < NRU:
                            stage1_back(ru + 1, x1ctx)

                def fin(par=par, ru=ru):
                    emit_down(0)
                    emit_down(1)
                    emit_down(2)
                    td = b.sems["t"].n
                    st["hqt_tfree"][par] = td
                    a1w = [(TS, td)]
                    if st["outb_tok"][par] is not None:
                        a1w.append(st["outb_tok"][par])
                    a1 = b.emit("vector", lambda e, par=par: e.tensor_add(
                        X1R[par][:, 0:512], PD[0][:], X1R[par][:, 0:512]),
                        waits=a1w)
                    a2 = b.emit("vector", lambda e, par=par: e.tensor_add(
                        X1R[par][:, 512:1024], PD[1][:],
                        X1R[par][:, 512:1024]),
                        waits=[(TS, td)])
                    st["pd_free"] = a2
                    lnv = ln_tile(b, X1R[par][:], X1R[par][:], w, [(V, a2)])
                    st["x1r_free"][par] = lnv
                    st["outb_tok"][par] = b.emit(
                        "gpsimd", lambda e, ru=ru, par=par: e.dma_start(
                            out=out_d[ru * 128:(ru + 1) * 128, :],
                            in_=X1R[par][:]), waits=[(V, lnv)])
                st["pend_fin"] = fin

            stage1(0)
            stage1(1)
            for ru in range(NRU):
                stage2(ru)
            st["pend_fin"]()
            st["pend_fin"] = None
            b.barrier()

            fin_waits = [(sm.h, sm.n) for sm in b.dsems["gpsimd"] if sm.n]
            b.streams["gpsimd"].append(
                lambda e, fw=tuple(fin_waits):
                [e.wait_ge(h, v) for h, v in fw])

        with nc.Block() as blk:
            b.flush(blk)

        tot = {}
        for ins in nc.all_instructions():
            si = ins.sync_info
            if si is None:
                continue
            for up in si.on_update:
                if up.update_mode in ("sem-inc", "sem-add-imm"):
                    tot[up.ant_name] = tot.get(up.ant_name, 0) + up.update_value
        for nm, sm in b.sems.items():
            got = tot.get(nm, 0)
            if got != sm.n:
                raise RuntimeError(
                    f"sem {nm}: program increments {got} != tracked {sm.n}")
    return nc


def make_consts():
    ident = np.eye(128).astype(ml_dtypes.bfloat16)
    masks = np.zeros((128, 4 * 512), np.float32)
    for c in range(4):
        s = np.arange(128)[:, None] + c * 128
        j = np.arange(512)[None, :]
        masks[:, c * 512:(c + 1) * 512] = (s <= j).astype(np.float32)
    masks = masks.astype(ml_dtypes.bfloat16)
    onesr = np.ones((1, 128), ml_dtypes.bfloat16)
    onesc = np.ones((64, 1), ml_dtypes.bfloat16)
    onesc2 = np.zeros((128, 33), ml_dtypes.bfloat16)
    onesc2[0:64, 0] = 1
    onesc2[64:128, 32] = 1
    unit4 = np.zeros((1, 16), ml_dtypes.bfloat16)
    for g in range(4):
        unit4[0, g * 4 + g] = 1
    sel4 = np.zeros((4, 256), ml_dtypes.bfloat16)
    for g in range(4):
        sel4[g, g * 64:(g + 1) * 64] = 1
    return ident, masks, onesr, onesc, onesc2, unit4, sel4


def kernel(L0_attn_hidden_states, L0_attn_Wq, L0_attn_Wk_down,
           L0_attn_Wv_down, L0_attn_q_to_latent, L0_attn_v_up, L0_attn_Wout,
           L0_ffn_w_gate, L0_ffn_w_up, L0_ffn_w_down):
    x = np.ascontiguousarray(np.asarray(L0_attn_hidden_states, np.float32))
    B, T, _ = x.shape
    ident, masks, onesr, onesc, onesc2, unit4, sel4 = make_consts()
    shared = {
        "wq": np.ascontiguousarray(np.asarray(L0_attn_Wq, np.float32)),
        "wkd": np.ascontiguousarray(np.asarray(L0_attn_Wk_down, np.float32)),
        "wvd": np.ascontiguousarray(np.asarray(L0_attn_Wv_down, np.float32)),
        "q2l": np.ascontiguousarray(
            np.asarray(L0_attn_q_to_latent, np.float32).reshape(H * 64, 64)),
        "vup": np.ascontiguousarray(
            np.asarray(L0_attn_v_up, np.float32).reshape(H * 64, 64)),
        "wout": np.ascontiguousarray(np.asarray(L0_attn_Wout, np.float32)),
        "wg": np.ascontiguousarray(np.asarray(L0_ffn_w_gate, np.float32)),
        "wu": np.ascontiguousarray(np.asarray(L0_ffn_w_up, np.float32)),
        "wd": np.ascontiguousarray(np.asarray(L0_ffn_w_down, np.float32)),
        "ident": ident, "masks": masks, "onesr": onesr, "onesc": onesc,
        "onesc2": onesc2, "unit4": unit4, "sel4": sel4,
    }
    nc = build(T)
    in_maps = [dict(shared, x=np.ascontiguousarray(x[c])) for c in range(B)]
    res = run_bass_kernel_spmd(nc, in_maps, core_ids=list(range(B)))
    return np.stack([res.results[c]["out"] for c in range(B)], axis=0)


if __name__ == "__main__":
    pass


# version 21
# speedup vs baseline: 1.0066x; 1.0066x over previous
import contextlib
import numpy as np
import ml_dtypes
import concourse.bass as bass
import concourse.mybir as mybir
from concourse.bass_utils import run_bass_kernel_spmd

F32 = mybir.dt.float32
BF16 = mybir.dt.bfloat16
U8 = mybir.dt.uint8
I32 = mybir.dt.int32
AF = mybir.ActivationFunctionType
ALU = mybir.AluOpType
AX = mybir.AxisListType

E = 1024
H = 16
LT = 64
II = 2816
MAGIC = 12582912.0
NEG_LN8 = -2.0794415416798357
EPS = 1e-5
ENGS = ("sync", "gpsimd", "tensor", "vector", "scalar")


def bcast16(t, nb):
    a = t[:, 0:nb]
    return bass.AP(a.tensor, a.offset, [list(a.ap[0]), [1, nb], [0, 16]])


def view3(a, nb):
    return bass.AP(a.tensor, a.offset, [list(a.ap[0]), [16, nb], [1, 16]])


class Sem:
    __slots__ = ("h", "n", "name")

    def __init__(self, h, name):
        self.h = h
        self.n = 0
        self.name = name


NDMASEM = 8


class Bld:

    def __init__(self, nc, stack):
        self.nc = nc
        self.streams = {k: [] for k in ENGS}
        self.guar = {k: {} for k in ENGS}
        self.sems = {}
        for name in ("t", "v", "s", "bar"):
            h = stack.enter_context(nc.semaphore(name))
            self.sems[name] = Sem(h, name)
        self.dsems = {}
        for eng, pfx in (("sync", "ld"), ("gpsimd", "g")):
            lst = []
            for i in range(NDMASEM):
                nm = f"{pfx}{i}"
                h = stack.enter_context(nc.semaphore(nm))
                sm = Sem(h, nm)
                self.sems[nm] = sm
                lst.append(sm)
            self.dsems[eng] = lst
        self.drot = {"sync": 0, "gpsimd": 0}
        self.esem = {"tensor": self.sems["t"], "vector": self.sems["v"],
                     "scalar": self.sems["s"]}
        self.barno = 0

    def emit(self, eng, f, waits=(), inc=True):
        gmap = self.guar[eng]
        real = []
        for sm, val in waits:
            if val is None or val <= 0:
                continue
            if gmap.get(sm.name, 0) < val:
                real.append((sm.h, val))
                gmap[sm.name] = val
        if eng in ("sync", "gpsimd"):
            i = self.drot[eng]
            self.drot[eng] = (i + 1) % NDMASEM
            ds = self.dsems[eng][i]
            prev = ds.n
            ds.n += 16
            after = (ds, ds.n)

            def run(e, real=real, f=f, ds=ds, prev=prev):
                for h, vv in real:
                    e.wait_ge(h, vv)
                if prev:
                    e.wait_ge(ds.h, prev)
                f(e).then_inc(ds.h, 16)
            gmap[ds.name] = prev
        else:
            es = self.esem[eng]
            after = None
            if inc:
                es.n += 1
                after = es.n

            def run(e, real=real, f=f, inc=inc, es=es):
                for h, vv in real:
                    e.wait_ge(h, vv)
                r = f(e)
                if inc:
                    r.then_inc(es.h, 1)
        self.streams[eng].append(run)
        return after

    def barrier(self):
        bar = self.sems["bar"]
        self.barno += 1
        target = len(ENGS) * self.barno
        for eng in ENGS:
            if eng in ("sync", "gpsimd"):
                own = [(sm.h, sm.n) for sm in self.dsems[eng] if sm.n]

                def fin(e, own=own, bar=bar):
                    for h, vv in own:
                        e.wait_ge(h, vv)
                    e.nop().then_inc(bar.h, 1)
            else:
                es = self.esem[eng]

                def fin(e, es=es, n=es.n, bar=bar):
                    e.wait_ge(es.h, n)
                    e.nop().then_inc(bar.h, 1)
            self.streams[eng].append(fin)
        bar.n = target
        snap = {nm: sm.n for nm, sm in self.sems.items()}
        for eng in ENGS:
            self.streams[eng].append(
                lambda e, h=bar.h, v=target: e.wait_ge(h, v))
            g = self.guar[eng]
            for nm, vv in snap.items():
                g[nm] = max(g.get(nm, 0), vv)

    def flush(self, block):
        for eng in ENGS:
            ops = self.streams[eng]
            if not ops:
                continue
            getattr(block, eng)(
                lambda e, ops=tuple(ops): [op(e) for op in ops])
            self.streams[eng] = []


def quant_front(b, src, q, F, waits=()):
    nb = F // 16
    V, S = b.sems["v"], b.sems["s"]
    rot = 1 - getattr(b, "_qrot", 1)
    b._qrot = rot
    ps = getattr(b, "_qtok", None)
    if ps is None:
        ps = b._qtok = [
            {"sxn": None, "m2": 0, "cp1": 0, "cp2": 0, "f": 0}
            for _ in range(2)]
    p = ps[rot]
    src3 = bass.AP(src.tensor, src.offset,
                   [list(src.ap[0]), [16, nb], [1, 16]])
    qr = q[rot]
    am = qr["AM"][:, 0:nb]
    r6 = qr["R6"][:, 0:nb]
    c6 = qr["C6"][:, 0:nb]
    xn, ta = qr["XN"][:, 0:F], qr["TA"][:, 0:F]
    t1 = q[0]["T1"][:, 0:F]
    r1, r2, r3 = qr["R1"][:, 0:F], qr["R2"][:, 0:F], qr["R3"][:, 0:F]
    v_am = b.emit("vector", lambda e: e.tensor_reduce(
        am, src3, axis=AX.X, op=ALU.max, apply_absolute_value=True),
        waits=list(waits))
    v_cl = b.emit("vector", lambda e: e.tensor_scalar_max(am, am, 6e-12),
                  waits=[(V, v_am)])
    v_rc = b.emit("vector", lambda e: e.reciprocal(r6, am),
                  waits=[(V, v_cl)])
    v_r6 = b.emit("vector", lambda e: e.tensor_scalar_mul(r6, r6, 6.0),
                  waits=[(V, v_rc)])
    v_c6 = b.emit("vector", lambda e: e.tensor_scalar_mul(c6, am, 1.0 / 6.0),
                  waits=[(V, v_cl), (V, p["f"])])
    v_xn = b.emit("vector", lambda e: e.tensor_tensor(
        view3(xn, nb), src3, bcast16(qr["R6"], nb), op=ALU.mult),
        waits=[(V, v_r6)] + ([p["sxn"]] if p["sxn"] else []))
    s_ta = b.emit("scalar", lambda e: e.activation(ta, xn, AF.Abs),
                  waits=[(V, v_xn), (V, p["m2"])])
    g_a = b.emit("gpsimd", lambda e: e.tensor_scalar(
        t1, xn, 2.0, MAGIC, op0=ALU.mult, op1=ALU.add),
        waits=[(V, v_xn)])
    g_b = b.emit("gpsimd", lambda e: e.tensor_scalar(
        r1, t1, 0.5, -0.5 * MAGIC, op0=ALU.mult, op1=ALU.add),
        waits=[g_a, (V, p["cp2"])])
    g_c = b.emit("gpsimd", lambda e: e.tensor_scalar(
        t1, xn, 1.0, MAGIC, op0=ALU.mult, op1=ALU.add),
        waits=[(V, v_xn)])
    g_d = b.emit("gpsimd", lambda e: e.tensor_scalar(
        r2, t1, 1.0, -MAGIC, op0=ALU.mult, op1=ALU.add),
        waits=[g_c, (V, p["cp1"])])
    g_e = b.emit("gpsimd", lambda e: e.tensor_scalar(
        t1, xn, 0.5, MAGIC, op0=ALU.mult, op1=ALU.add),
        waits=[(V, v_xn)])
    g_f = b.emit("gpsimd", lambda e: e.tensor_scalar(
        r3, t1, 2.0, -2.0 * MAGIC, op0=ALU.mult, op1=ALU.add),
        waits=[g_e, (V, p["f"])])
    return {"rot": rot, "nb": nb, "F": F, "xn": v_xn, "qr": qr,
            "s_ta": s_ta, "g_e": g_e, "g_f": g_f}


def quant_back(b, ctx, dst, dst_waits=()):
    V, S = b.sems["v"], b.sems["s"]
    rot, nb, F = ctx["rot"], ctx["nb"], ctx["F"]
    p = b._qtok[rot]
    qr = ctx["qr"]
    ta = qr["TA"][:, 0:F]
    r1, r2, r3 = qr["R1"][:, 0:F], qr["R2"][:, 0:F], qr["R3"][:, 0:F]
    m1, m2 = qr["M1"][:, 0:F], qr["M2"][:, 0:F]
    dst3 = bass.AP(dst.tensor, dst.offset,
                   [list(dst.ap[0]), [16, nb], [1, 16]])
    v_m1 = b.emit("vector", lambda e: e.tensor_single_scalar(
        m1, ta, 2.0, op=ALU.is_le), waits=[(S, ctx["s_ta"])])
    v_m2 = b.emit("vector", lambda e: e.tensor_single_scalar(
        m2, ta, 4.0, op=ALU.is_le), waits=[(V, v_m1)])
    v_cp1 = b.emit("vector", lambda e: e.copy_predicated(r3, m2, r2),
                   waits=[(V, v_m2), ctx["g_f"]])
    v_cp2 = b.emit("vector", lambda e: e.copy_predicated(r3, m1, r1),
                   waits=[(V, v_cp1)])
    v_f = b.emit("vector", lambda e: e.tensor_tensor(
        dst3, view3(r3, nb), bcast16(qr["C6"], nb), op=ALU.mult),
        waits=[(V, v_cp2)] + list(dst_waits))
    p.update(f=v_f, sxn=ctx["g_e"], m2=v_m2, cp1=v_cp1, cp2=v_cp2)
    return v_f, ctx["xn"]


def quant_chunk(b, src, dst, q, F, waits=(), dst_waits=()):
    ctx = quant_front(b, src, q, F, waits=waits)
    return quant_back(b, ctx, dst, dst_waits=dst_waits)


def rsqrt_v(b, out, x, t1, t2, waits):
    V = b.sems["v"]
    xi = x.bitcast(I32)
    oi = out.bitcast(I32)
    v = b.emit("vector", lambda e: e.tensor_single_scalar(
        oi, xi, 1, op=ALU.arith_shift_right), waits=waits)
    v = b.emit("vector", lambda e: e.tensor_single_scalar(
        oi, oi, -1, op=ALU.mult), waits=[(V, v)])
    v = b.emit("vector", lambda e: e.tensor_single_scalar(
        oi, oi, 0x5F3759DF, op=ALU.add), waits=[(V, v)])
    for _ in range(3):
        v = b.emit("vector", lambda e: e.tensor_mul(t1, out, out),
                   waits=[(V, v)])
        v = b.emit("vector", lambda e: e.tensor_mul(t2, t1, x),
                   waits=[(V, v)])
        v = b.emit("vector", lambda e: e.tensor_scalar(
            t2, t2, -0.5, 1.5, op0=ALU.mult, op1=ALU.add), waits=[(V, v)])
        v = b.emit("vector", lambda e: e.tensor_mul(out, out, t2),
                   waits=[(V, v)])
    return v


def ln_tile(b, xs, out, w, src_waits):
    V, S = b.sems["v"], b.sems["s"]
    prev = getattr(b, "_lnprev", 0)
    prev_s = getattr(b, "_lnprev_s", 0)
    s1 = b.emit("scalar", lambda e: e.activation(
        w["DSQ"][:], xs, AF.Square, accum_out=w["VRR"][:]),
        waits=list(src_waits) + [(V, prev), (S, prev_s)])
    s2 = b.emit("scalar", lambda e: e.activation(
        w["DSQ"][:], xs, AF.Copy, accum_out=w["MUR"][:]),
        waits=[(S, s1)])
    b._lnprev_s = s2
    v2 = b.emit("vector", lambda e: e.tensor_scalar_mul(
        w["MU"][:], w["MUR"][:], 1.0 / 1024.0),
        waits=[(S, s2), (V, prev)])
    v3 = b.emit("vector", lambda e: e.tensor_mul(
        w["M2"][:], w["MU"][:], w["MU"][:]), waits=[(V, v2)])
    v5 = b.emit("vector", lambda e: e.scalar_tensor_tensor(
        w["VR"][:], w["VRR"][:], 1.0 / 1024.0, w["M2"][:],
        op0=ALU.mult, op1=ALU.subtract), waits=[(V, v3), (S, s1)])
    v5b = b.emit("vector", lambda e: e.tensor_scalar_add(
        w["VR"][:], w["VR"][:], EPS), waits=[(V, v5)])
    s3 = b.emit("scalar", lambda e: e.activation(
        w["LNV"][:], w["VR"][:], AF.Sqrt), waits=[(V, v5b), (S, prev_s)])
    b._lnprev_s = max(b._lnprev_s, s3)
    v6 = b.emit("vector", lambda e: e.reciprocal(
        w["RS"][:], w["LNV"][:]), waits=[(S, s3)])
    vf = b.emit("vector", lambda e: e.tensor_scalar(
        out, xs, w["MU"][:, 0:1], w["RS"][:, 0:1],
        op0=ALU.subtract, op1=ALU.mult), waits=[(V, v6)])
    b._lnprev = vf
    return vf


def build(T=4096, phases="WAB", acut=40):
    nc = bass.Bass("TRN2", target_bir_lowering=False, debug=False)
    NJ = T // 512
    NR = T // 128
    NK = T // 256

    dt_ = nc.dram_tensor
    x_d = dt_("x", [T, E], F32, kind="ExternalInput").ap()
    wq_d = dt_("wq", [E, E], F32, kind="ExternalInput").ap()
    wkd_d = dt_("wkd", [LT, E], F32, kind="ExternalInput").ap()
    wvd_d = dt_("wvd", [LT, E], F32, kind="ExternalInput").ap()
    q2l_d = dt_("q2l", [H * 64, 64], F32, kind="ExternalInput").ap()
    vup_d = dt_("vup", [H * 64, 64], F32, kind="ExternalInput").ap()
    wout_d = dt_("wout", [E, E], F32, kind="ExternalInput").ap()
    wg_d = dt_("wg", [II, E], F32, kind="ExternalInput").ap()
    wu_d = dt_("wu", [II, E], F32, kind="ExternalInput").ap()
    wd_d = dt_("wd", [E, II], F32, kind="ExternalInput").ap()
    id_d = dt_("ident", [128, 128], BF16, kind="ExternalInput").ap()
    mk_d = dt_("masks", [128, 4 * 512], BF16, kind="ExternalInput").ap()
    onr_d = dt_("onesr", [1, 128], BF16, kind="ExternalInput").ap()
    onc_d = dt_("onesc", [64, 1], BF16, kind="ExternalInput").ap()
    onc2_d = dt_("onesc2", [128, 33], BF16, kind="ExternalInput").ap()
    un4_d = dt_("unit4", [1, 16], BF16, kind="ExternalInput").ap()
    sel4_d = dt_("sel4", [4, 256], BF16, kind="ExternalInput").ap()
    out_d = dt_("out", [T, E], F32, kind="ExternalOutput").ap()
    sx1_d = dt_("s_x1", [T, E], F32, kind="Internal").ap()
    swg_d = dt_("s_wgT", [E, II], BF16, kind="Internal").ap()
    swu_d = dt_("s_wuT", [E, II], BF16, kind="Internal").ap()
    swd_d = dt_("s_wdT", [II, E], BF16, kind="Internal").ap()

    with contextlib.ExitStack() as top:
        en = top.enter_context
        b = Bld(nc, top)
        TS, V, S = (b.sems[k] for k in ("t", "v", "s"))

        IDN = en(nc.sbuf_tensor("IDN", [128, 128], BF16))
        ONR = en(nc.sbuf_tensor("ONR", [1, 128], BF16))
        ONC = en(nc.sbuf_tensor("ONC", [64, 1], BF16))
        ONC2 = en(nc.sbuf_tensor("ONC2", [128, 33], BF16))
        UN4 = en(nc.sbuf_tensor("UN4", [1, 16], BF16))
        SEL4 = en(nc.sbuf_tensor("SEL4", [4, 256], BF16))
        KM2 = en(nc.sbuf_tensor("KM2", [1, 1], F32))
        qsh = {}
        for nm, sh, d2 in (("AM", [128, 64], F32), ("R6", [128, 64], F32),
                           ("T1", [128, 1024], F32)):
            qsh[nm] = en(nc.sbuf_tensor("Q" + nm, sh, d2))
        q = []
        for qi in range(2):
            qc = dict(qsh)
            for nm, sh, d2 in (("C6", [128, 64], F32),
                               ("XN", [128, 1024], F32),
                               ("TA", [128, 1024], BF16),
                               ("R1", [128, 1024], BF16),
                               ("R2", [128, 1024], BF16),
                               ("R3", [128, 1024], BF16),
                               ("M1", [128, 1024], U8),
                               ("M2", [128, 1024], U8)):
                qc[nm] = en(nc.sbuf_tensor(f"Q{qi}{nm}", sh, d2))
            q.append(qc)
        w = {}
        for nm, sh in (("MU", [128, 1]), ("M2", [128, 1]), ("VR", [128, 1]),
                       ("LNV", [128, 1]), ("RS", [128, 1]), ("T2", [128, 1]),
                       ("MUR", [128, 1]), ("VRR", [128, 1])):
            w[nm] = en(nc.sbuf_tensor("W" + nm, sh, F32))
        w["DSQ"] = en(nc.sbuf_tensor("WDSQ", [128, E], BF16))

        c1 = b.emit("sync", lambda e: e.dma_start(out=IDN[:], in_=id_d))
        c3 = b.emit("sync", lambda e: e.dma_start(out=ONR[:], in_=onr_d))
        c4 = b.emit("sync", lambda e: e.dma_start(out=ONC[:], in_=onc_d))
        c5 = b.emit("sync", lambda e: e.dma_start(out=ONC2[:], in_=onc2_d))
        c6 = b.emit("sync", lambda e: e.dma_start(out=UN4[:], in_=un4_d))
        c7 = b.emit("sync", lambda e: e.dma_start(out=SEL4[:], in_=sel4_d))

        with contextlib.ExitStack() as wa:
            ea = wa.enter_context
            MSK = ea(nc.sbuf_tensor("MSK", [128, 4 * 512], BF16))
            KLT = ea(nc.sbuf_tensor("KLT", [128, T], BF16))
            VLA = ea(nc.sbuf_tensor("VLA", [128, NR * 65], BF16))
            WQLT = [ea(nc.sbuf_tensor(f"WQLT{c}", [128, E], BF16))
                    for c in range(8)]
            WOV2 = [ea(nc.sbuf_tensor(f"WOV2{i}", [128, E], BF16))
                    for i in range(H // 2)]
            WKDT = ea(nc.sbuf_tensor("WKDT", [128, 512], BF16))
            WVDT = ea(nc.sbuf_tensor("WVDT", [128, 512], BF16))
            XR = [ea(nc.sbuf_tensor(f"XR{i}", [128, E], F32))
                  for i in range(4)]
            LBB = ea(nc.sbuf_tensor("LBB", [128, E], BF16))

            cm = b.emit("sync", lambda e: e.dma_start(out=MSK[:], in_=mk_d))

            with contextlib.ExitStack() as pw:
                ew = pw.enter_context
                LB = ew(nc.sbuf_tensor("LB", [128, II], F32))
                QLB = ew(nc.sbuf_tensor("QLB", [128, 512], BF16))
                VUB = ew(nc.sbuf_tensor("VUB", [128, 512], BF16))
                VUPT = [ew(nc.sbuf_tensor(f"VUPT{r}", [128, 64], BF16))
                        for r in range(8)]
                VTMP = ew(nc.sbuf_tensor("VTMP", [64, 64], BF16))
                XTW = [ew(nc.sbuf_tensor(f"XTW{i}", [128, 8 * 128], BF16))
                       for i in range(2)]
                WOUTT = [ew(nc.sbuf_tensor(f"WOUTT{c}", [128, E], BF16))
                         for c in range(8)]
                KSQ = ew(nc.sbuf_tensor("KSQ", [64, T], BF16))
                KMT = ew(nc.sbuf_tensor("KMT", [1, 1], F32))
                PTW = [ew(nc.psum_tensor(f"PTW{i}", [128, 128], BF16))
                       for i in range(2)]
                PW1 = [ew(nc.psum_tensor(f"PW1{i}", [128, 512], F32))
                       for i in range(2)]
                PK = ew(nc.psum_tensor("PK", [64, 128], F32))
                PK2 = ew(nc.psum_tensor("PK2", [1, 512], F32))

                ptw_free = [0, 0]
                pw1_free = [0, 0]

                lw = b.emit("sync", lambda e: e.dma_start(
                    out=LB[0:64, 0:E], in_=wkd_d))
                lv = b.emit("sync", lambda e: e.dma_start(
                    out=LB[64:128, 0:E], in_=wvd_d))
                vc = b.emit("vector", lambda e: e.tensor_copy(
                    LBB[:, 0:E], LB[:, 0:E]), waits=[lw, lv])
                lbb_free = 0
                for src_b, dstt in ((0, WKDT), (64, WVDT)):
                    for c in range(8):
                        tt = b.emit("tensor", lambda e, c=c, src_b=src_b:
                                    e.transpose(
                                        PTW[c % 2][:, 0:64],
                                        LBB[src_b:src_b + 64,
                                            c * 128:(c + 1) * 128],
                                        IDN[src_b:src_b + 64,
                                            src_b:src_b + 64]),
                                    waits=[(V, vc), c1,
                                           (V, ptw_free[c % 2])])
                        lbb_free = tt
                        ptw_free[c % 2] = b.emit(
                            "vector", lambda e, c=c, dstt=dstt: e.tensor_copy(
                                dstt[:, c * 64:(c + 1) * 64],
                                PTW[c % 2][:, 0:64]), waits=[(TS, tt)])

                lqs = []
                for r in range(8):
                    lqs.append(b.emit("sync", lambda e, r=r: e.dma_start(
                        out=LB[:, r * 64:(r + 1) * 64],
                        in_=q2l_d[r * 128:(r + 1) * 128, :]),
                        waits=[(V, vc)]))
                vq = b.emit("vector", lambda e: e.tensor_copy(
                    QLB[:], LB[:, 0:512]), waits=lqs)
                lus = []
                for r in range(8):
                    lus.append(b.emit("sync", lambda e, r=r: e.dma_start(
                        out=LB[:, 512 + r * 64:512 + (r + 1) * 64],
                        in_=vup_d[r * 128:(r + 1) * 128, :]),
                        waits=[(V, vq)]))
                vu = b.emit("vector", lambda e: e.tensor_copy(
                    VUB[:], LB[:, 512:1024]), waits=lus)

                vupt_done = []
                for h in range(H):
                    r, base = h // 2, (h % 2) * 64
                    tt = b.emit("tensor", lambda e, r=r, base=base, h=h:
                                e.transpose(
                                    PTW[h % 2][0:64, 0:64],
                                    VUB[base:base + 64, r * 64:(r + 1) * 64],
                                    IDN[base:base + 64, base:base + 64]),
                                waits=[(V, vu), (V, ptw_free[h % 2])])
                    if base == 0:
                        dnv = b.emit("vector",
                                     lambda e, r=r, h=h: e.tensor_copy(
                                         VUPT[r][0:64, :],
                                         PTW[h % 2][0:64, 0:64]),
                                     waits=[(TS, tt)])
                        ptw_free[h % 2] = dnv
                        dn = (b.sems["v"], dnv)
                    else:
                        dn0 = b.emit("vector", lambda e, h=h: e.tensor_copy(
                            VTMP[:], PTW[h % 2][0:64, 0:64]),
                            waits=[(TS, tt)] +
                            ([vtmp_free] if h > 1 else []))
                        ptw_free[h % 2] = dn0
                        dn = b.emit("gpsimd", lambda e, r=r: e.dma_start(
                            out=VUPT[r][64:128, :], in_=VTMP[:]),
                            waits=[(V, dn0)])
                        vtmp_free = dn
                    vupt_done.append(dn)

                on_v = b.emit("vector", lambda e: e.memset(
                    bass.AP(VLA[:].tensor, VLA[:].offset + 64,
                            [list(VLA[:].ap[0]), [65, NR], [1, 1]]), 1.0))
                xr_free = {0: 0, 1: 0, 2: 0, 3: 0}
                xtw_free = [0, 0]
                pk_free = 0
                for r in range(NR):
                    ldr = b.emit("sync", lambda e, r=r: e.dma_start(
                        out=XR[r % 2][:], in_=x_d[r * 128:(r + 1) * 128, :]),
                        waits=[(V, xr_free[r % 2])])
                    cb = b.emit("vector", lambda e, r=r: e.tensor_copy(
                        LBB[:, 0:E], XR[r % 2][:]),
                        waits=[ldr, (TS, lbb_free)])
                    xr_free[r % 2] = cb
                    tcop = []
                    for c in range(8):
                        tt = b.emit("tensor", lambda e, c=c: e.transpose(
                            PTW[c % 2][:], LBB[:, c * 128:(c + 1) * 128],
                            IDN[:]),
                            waits=[(V, cb), (V, ptw_free[c % 2])])
                        if c == 7:
                            lbb_free = tt
                        ptw_free[c % 2] = b.emit(
                            "vector", lambda e, c=c, r=r: e.tensor_copy(
                                XTW[r % 2][:, c * 128:(c + 1) * 128],
                                PTW[c % 2][:]),
                            waits=[(TS, tt), (TS, xtw_free[r % 2])])
                        tcop.append(ptw_free[c % 2])
                    for c in range(8):
                        b.emit("tensor", lambda e, c=c, r=r: e.matmul(
                            PK[:, 0:128], WKDT[:, c * 64:(c + 1) * 64],
                            XTW[r % 2][:, c * 128:(c + 1) * 128],
                            start=(c == 0), stop=(c == 7)),
                            waits=[(V, tcop[c]), (V, pk_free)])
                    tk = b.sems["t"].n
                    pk_free = b.emit("vector", lambda e, r=r: e.tensor_copy(
                        KLT[0:64, r * 128:(r + 1) * 128],
                        PK[:, 0:128]),
                        waits=[(TS, tk)])
                    for c in range(8):
                        b.emit("tensor", lambda e, c=c, r=r: e.matmul(
                            PW1[r % 2][:, 0:64],
                            XTW[r % 2][:, c * 128:(c + 1) * 128],
                            WVDT[:, c * 64:(c + 1) * 64],
                            start=(c == 0), stop=(c == 7)),
                            waits=[(V, pw1_free[r % 2])])
                    tv = b.sems["t"].n
                    pw1_free[r % 2] = b.emit(
                        "vector", lambda e, r=r: e.tensor_copy(
                            VLA[:, r * 65:r * 65 + 64], PW1[r % 2][:, 0:64]),
                        waits=[(TS, tv), (V, on_v)])
                    xtw_free[r % 2] = b.sems["t"].n
                klat_done = pk_free
                kdup = b.emit("gpsimd", lambda e: e.dma_start(
                    out=KLT[64:128, 0:T], in_=KLT[0:64, 0:T]),
                    waits=[(V, klat_done)])

                sk = b.emit("scalar", lambda e: e.activation(
                    KSQ[:, 0:T], KLT[0:64, 0:T], AF.Square),
                    waits=[(V, klat_done)])
                km_prev = 0
                pk2_free = 0
                for blk in range(T // 512):
                    tk2 = b.emit("tensor", lambda e, blk=blk: e.matmul(
                        PK2[:], ONC[:, 0:1],
                        KSQ[:, blk * 512:(blk + 1) * 512],
                        start=True, stop=True),
                        waits=[(S, sk), c4, (V, pk2_free)])
                    vmx = b.emit("vector", lambda e: e.reduce_max(
                        KMT[0:1, 0:1], PK2[:], axis=AX.X),
                        waits=[(TS, tk2), (V, km_prev)])
                    pk2_free = vmx
                    if blk == 0:
                        km_prev = b.emit("vector", lambda e: e.tensor_copy(
                            KM2[:], KMT[0:1, 0:1]), waits=[(V, vmx)])
                    else:
                        km_prev = b.emit("vector", lambda e: e.tensor_max(
                            KM2[:], KM2[:], KMT[0:1, 0:1]), waits=[(V, vmx)])

                for r in range(8):
                    ldr = b.emit("sync", lambda e, r=r: e.dma_start(
                        out=XR[2][:], in_=wq_d[r * 128:(r + 1) * 128, :]),
                        waits=[(V, xr_free[2])])
                    cb = b.emit("vector", lambda e: e.tensor_copy(
                        LBB[:, 0:E], XR[2][:]),
                        waits=[ldr, (TS, lbb_free)])
                    xr_free[2] = cb
                    for hh in range(2):
                        h = 2 * r + hh
                        base = hh * 64
                        for c in range(8):
                            tt = b.emit(
                                "tensor",
                                lambda e, c=c, base=base, r=r: e.matmul(
                                    PW1[c % 2][:, 0:64],
                                    LBB[base:base + 64,
                                        c * 128:(c + 1) * 128],
                                    QLB[base:base + 64,
                                        r * 64:(r + 1) * 64],
                                    start=True, stop=True),
                                waits=[(V, cb), (V, vq),
                                       (V, pw1_free[c % 2])])
                            if c == 7:
                                lbb_free = tt
                            pw1_free[c % 2] = b.emit(
                                "vector", lambda e, c=c, h=h: e.tensor_copy(
                                    WQLT[c][:, h * 64:(h + 1) * 64],
                                    PW1[c % 2][:, 0:64]), waits=[(TS, tt)])

                for r in range(8):
                    ldr = b.emit("sync", lambda e, r=r: e.dma_start(
                        out=XR[2][:], in_=wout_d[r * 128:(r + 1) * 128, :]),
                        waits=[(V, xr_free[2])])
                    cb = b.emit("vector", lambda e: e.tensor_copy(
                        LBB[:, 0:E], XR[2][:]),
                        waits=[ldr, (TS, lbb_free)])
                    xr_free[2] = cb
                    for c in range(8):
                        tt = b.emit("tensor", lambda e, c=c: e.transpose(
                            PTW[c % 2][:], LBB[:, c * 128:(c + 1) * 128],
                            IDN[:]),
                            waits=[(V, cb), (V, ptw_free[c % 2])])
                        if c == 7:
                            lbb_free = tt
                        ptw_free[c % 2] = b.emit(
                            "vector", lambda e, c=c, r=r: e.tensor_copy(
                                WOUTT[c][:, r * 128:(r + 1) * 128],
                                PTW[c % 2][:]), waits=[(TS, tt)])
                woutt_done = ptw_free[1]

                for h in range(H):
                    r, base = h // 2, (h % 2) * 64
                    for nn in range(2):
                        tt = b.emit(
                            "tensor",
                            lambda e, r=r, base=base, h=h, nn=nn: e.matmul(
                                PW1[nn][0:64, 0:512],
                                VUPT[r][base:base + 64, :],
                                WOUTT[h // 2][base:base + 64,
                                              nn * 512:(nn + 1) * 512],
                                start=True, stop=True),
                            waits=[(V, woutt_done), (V, pw1_free[nn]),
                                   vupt_done[h]])
                        pw1_free[nn] = b.emit(
                            "vector", lambda e, h=h, nn=nn: e.tensor_copy(
                                WOV2[h // 2][(h % 2) * 64:
                                             (h % 2) * 64 + 64,
                                             nn * 512:(nn + 1) * 512],
                                PW1[nn][0:64, 0:512]), waits=[(TS, tt)])

                with contextlib.ExitStack() as ps1:
                    STGW = [ps1.enter_context(
                        nc.sbuf_tensor(f"STGW{c}", [128, II], BF16))
                        for c in range(8)]
                    stgw_tokens = []
                    ptwq_free = [(V, ptw_free[0]), (V, ptw_free[1])]
                    for w_d, s_d in ((wg_d, swg_d), (wu_d, swu_d)):
                        stg_free = list(stgw_tokens)
                        pend = [None]

                        def wflush(pr, ctx, stg_free=stg_free):
                            nonlocal lbb_free
                            qv, _ = quant_back(
                                b, ctx, LBB[:, 0:E],
                                dst_waits=[(TS, lbb_free)])
                            for c in range(8):
                                tt = b.emit(
                                    "tensor", lambda e, c=c: e.transpose(
                                        PTW[c % 2][:],
                                        LBB[:, c * 128:(c + 1) * 128],
                                        IDN[:]),
                                    waits=[(V, qv),
                                           ptwq_free[c % 2]])
                                if c == 7:
                                    lbb_free = tt
                                ptwq_free[c % 2] = (S, b.emit(
                                    "scalar",
                                    lambda e, c=c, pr=pr: e.activation(
                                        STGW[c][:, pr * 128:(pr + 1) * 128],
                                        PTW[c % 2][:], AF.Copy),
                                    waits=[(TS, tt)] + stg_free))

                        for r in range(22):
                            ldr = b.emit("sync", lambda e, r=r, w_d=w_d:
                                         e.dma_start(
                                             out=XR[2][:],
                                             in_=w_d[r * 128:(r + 1) * 128,
                                                     :]),
                                         waits=[(V, xr_free[2])])
                            ctx = quant_front(b, XR[2][:, 0:E], q, E,
                                              waits=[ldr])
                            xr_free[2] = ctx["xn"]
                            if pend[0] is not None:
                                wflush(*pend[0])
                            pend[0] = (r, ctx)
                        wflush(*pend[0])
                        pend[0] = None
                        stgw_tokens = []
                        for c in range(8):
                            stgw_tokens.append(b.emit(
                                "gpsimd", lambda e, c=c, s_d=s_d:
                                e.dma_start(
                                    out=s_d[c * 128:(c + 1) * 128, :],
                                    in_=STGW[c][:]),
                                waits=list(ptwq_free)))
                    stgw_store_done = list(stgw_tokens)

                with contextlib.ExitStack() as ps2:
                    STGD = [ps2.enter_context(
                        nc.sbuf_tensor(f"STGD{c}", [128, E], BF16))
                        for c in range(22)]
                    WDB = ps2.enter_context(
                        nc.sbuf_tensor("WDB", [128, II], BF16))
                    lb_free = vu
                    wdb_free = 0
                    pendd = [None]
                    rowq = {}

                    def dflushrow(r0):
                        nonlocal wdb_free
                        qvf = rowq.pop(r0)
                        for c in range(22):
                            tt = b.emit("tensor", lambda e, c=c: e.transpose(
                                PTW[c % 2][:],
                                WDB[:, c * 128:(c + 1) * 128],
                                IDN[:]),
                                waits=[(V, qvf[min(c // 8, 2)]),
                                       ptwq_free[c % 2]])
                            if c == 21:
                                wdb_free = tt
                            ptwq_free[c % 2] = (S, b.emit(
                                "scalar", lambda e, c=c, r0=r0: e.activation(
                                    STGD[c][:, r0 * 128:(r0 + 1) * 128],
                                    PTW[c % 2][:], AF.Copy),
                                waits=[(TS, tt)]))

                    def dflush(item):
                        r0, si, ctx, off, Fq = item
                        qv, _ = quant_back(b, ctx, WDB[:, off:off + Fq],
                                           dst_waits=[(TS, wdb_free)])
                        rowq.setdefault(r0, []).append(qv)
                        if si == 2:
                            dflushrow(r0)

                    for r in range(8):
                        ldr = b.emit("sync", lambda e, r=r: e.dma_start(
                            out=LB[:, 0:II], in_=wd_d[r * 128:(r + 1) * 128,
                                                      :]),
                            waits=[(V, lb_free)] + stgw_store_done)
                        for si, (off, Fq) in enumerate(
                                ((0, 1024), (1024, 1024), (2048, 768))):
                            ctx = quant_front(b, LB[:, off:off + Fq],
                                              q, Fq, waits=[ldr])
                            if pendd[0] is not None:
                                dflush(pendd[0])
                            pendd[0] = (r, si, ctx, off, Fq)
                        lb_free = ctx["xn"]
                    dflush(pendd[0])
                    for c in range(22):
                        b.emit("gpsimd", lambda e, c=c: e.dma_start(
                            out=swd_d[c * 128:(c + 1) * 128, :],
                            in_=STGD[c][:]), waits=list(ptwq_free))

            b.barrier()

            if phases == "W":
                stf = {}
                for r in range(T // 128):
                    lw_ = [stf[r % 4]] if (r % 4) in stf else []
                    tok = b.emit("sync", lambda e, r=r: e.dma_start(
                        out=XR[r % 4][:],
                        in_=x_d[r * 128:(r + 1) * 128, :]), waits=lw_)
                    stf[r % 4] = b.emit("gpsimd", lambda e, r=r: e.dma_start(
                        out=out_d[r * 128:(r + 1) * 128, :],
                        in_=XR[r % 4][:]), waits=[tok])
                b.barrier()

            if phases != "W":
             with contextlib.ExitStack() as pa:
                ep = pa.enter_context
                XTJ = [ep(nc.sbuf_tensor(f"XTJ{c}", [128, 512], BF16))
                       for c in range(8)]
                XP = [ep(nc.sbuf_tensor(f"XP{i}", [128, E], F32))
                      for i in range(2)]
                XRB = [ep(nc.sbuf_tensor(f"XRB{i}", [128, E], BF16))
                       for i in range(2)]
                QL2 = [ep(nc.sbuf_tensor(f"QL2{i}", [128, 512], BF16))
                       for i in range(H // 2)]
                SQ2 = [ep(nc.sbuf_tensor(f"SQ2{i}", [128, 512], BF16))
                       for i in range(2)]
                QMh = ep(nc.sbuf_tensor("QMh", [1, 16], F32))
                CJF = ep(nc.sbuf_tensor("CJF", [1, 16], F32))
                CJB2 = ep(nc.sbuf_tensor("CJB2", [1, 16], BF16))
                NCJH = ep(nc.sbuf_tensor("NCJH", [128, 16], F32))
                QM1 = ep(nc.sbuf_tensor("AQM1", [1, 1], F32))
                CJ = ep(nc.sbuf_tensor("CJ", [1, 1], F32))
                CJ2 = ep(nc.sbuf_tensor("CJ2", [1, 1], F32))
                CJT = ep(nc.sbuf_tensor("CJT", [1, 1], F32))
                CJN = ep(nc.sbuf_tensor("CJN", [1, 1], F32))
                CJB = ep(nc.sbuf_tensor("CJB", [1, 1], BF16))
                NCJ = ep(nc.sbuf_tensor("NCJ", [128, 1], F32))
                PBUF = [ep(nc.sbuf_tensor(f"PBUF{i}", [128, 1024], BF16))
                        for i in range(2)]
                CUB = [ep(nc.sbuf_tensor(f"CUB{i}", [64, 512], F32))
                       for i in range(4)]
                RDW = ep(nc.sbuf_tensor("RDW", [1, 2048], BF16))
                RD4R = ep(nc.sbuf_tensor("RD4R", [4, 512], F32))
                RDB4 = [ep(nc.sbuf_tensor(f"RDB4{i}", [4, 512], BF16))
                        for i in range(2)]
                CT2 = [ep(nc.sbuf_tensor(f"CT2{i}", [128, 512], BF16))
                       for i in range(H // 2)]
                XS = ep(nc.sbuf_tensor("XS", [128, E], F32))
                X1B = [ep(nc.sbuf_tensor(f"X1B{i}", [128, E], F32))
                       for i in range(2)]
                PQ = ep(nc.psum_tensor("PQ", [128, 512], F32))
                PSA = [ep(nc.psum_tensor(f"PSA{i}", [128, 1024], F32))
                       for i in range(2)]
                PC = ep(nc.psum_tensor("PC", [128, 512], F32))
                PC2 = ep(nc.psum_tensor("PC2", [128, 512], F32))
                PO0 = ep(nc.psum_tensor("PO0", [128, 512], F32))

                xr_free = {0: 0, 1: 0, 2: 0, 3: 0}
                xrb_free = {0: [], 1: []}
                x1b_free = [None, None]
                pq_free_v = 0
                sqb_free = [0, 0]
                exp_bank = [0, 0]
                pvp_bank = [0, 0]
                pcb_free = [0, 0]
                rdb_free = [0, 0]
                po_free_v = 0
                xs_free = 0

                xp_free = [0, 0]
                xtj_tfree = 0
                bc_tfree = 0

                def emit_xtj(J):
                    toks = []
                    for tt in range(4):
                        ldp = b.emit("sync", lambda e, J=J, tt=tt:
                                     e.dma_start(
                                         out=XP[tt % 2][:],
                                         in_=x_d[J * 512 + tt * 128:
                                                 J * 512 + (tt + 1) * 128,
                                                 :]),
                                     waits=[(V, xp_free[tt % 2])])
                        cb = b.emit("vector", lambda e, tt=tt: e.tensor_copy(
                            XRB[tt % 2][:], XP[tt % 2][:]),
                            waits=[ldp] + xrb_free[tt % 2])
                        xp_free[tt % 2] = cb
                        xrb_free[tt % 2] = []
                        for c in range(8):
                            tok = b.emit("sync", lambda e, c=c, tt=tt:
                                         e.dma_start_transpose(
                                             XTJ[c][:, tt * 128:
                                                    (tt + 1) * 128],
                                             XRB[tt % 2][:, c * 128:
                                                         (c + 1) * 128]),
                                         waits=[(V, cb), (TS, xtj_tfree)])
                            toks.append(tok)
                            xrb_free[tt % 2].append(tok)
                    return toks

                pending_xtj = emit_xtj(0)
                for J in range(NJ):
                    xtj_toks = pending_xtj
                    ldr_toks = []
                    for tt in range(4):
                        ldr_toks.append(b.emit(
                            "sync", lambda e, J=J, tt=tt: e.dma_start(
                                out=XR[tt][:],
                                in_=x_d[J * 512 + tt * 128:
                                        J * 512 + (tt + 1) * 128, :]),
                            waits=[(V, xr_free[tt])]))

                    qlb_ready = [0] * (H // 2)
                    qm_last = 0
                    pend_qn = None
                    ncj_tok = {}
                    pend_bc = []

                    def emit_bc(hp, v4):
                        nonlocal pq_free_v, bc_tfree
                        tb2 = b.emit("tensor", lambda e, hp=hp: e.matmul(
                            PQ[:, 0:2], ONR[0:1, :],
                            CJB2[0:1, 2 * hp:2 * hp + 2],
                            start=True, stop=True),
                            waits=[(V, v4), (V, pq_free_v), c3])
                        bc_tfree = b.sems["t"].n
                        nv = b.emit("vector", lambda e, hp=hp: e.tensor_copy(
                            NCJH[:, 2 * hp:2 * hp + 2], PQ[:, 0:2]),
                            waits=[(TS, tb2), (S, exp_bank[0]),
                                   (S, exp_bank[1])])
                        pq_free_v = nv
                        ncj_tok[2 * hp] = (V, nv)
                        ncj_tok[2 * hp + 1] = (V, nv)

                    def emit_qn(hp, sq):
                        nonlocal qm_last
                        tn = b.emit("tensor", lambda e, hp=hp: e.matmul(
                            PO0[0:33, 0:512], ONC2[:, 0:33], SQ2[hp % 2][:],
                            start=True, stop=True),
                            waits=[(S, sq), (V, qm_last),
                                   (V, po_free_v), c5])
                        sqb_free[hp % 2] = tn
                        va = b.emit("vector", lambda e, hp=hp: e.reduce_max(
                            QMh[0:1, 2 * hp:2 * hp + 1], PO0[0:1, 0:512],
                            axis=AX.X), waits=[(TS, tn), (S, exp_bank[0]),
                                               (S, exp_bank[1])])
                        vb = b.emit("vector", lambda e, hp=hp: e.reduce_max(
                            QMh[0:1, 2 * hp + 1:2 * hp + 2],
                            PO0[32:33, 0:512],
                            axis=AX.X), waits=[(V, va)])
                        qm_last = vb
                        vm = b.emit("vector", lambda e, hp=hp:
                                    e.tensor_scalar(
                                        QMh[0:1, 2 * hp:2 * hp + 2],
                                        QMh[0:1, 2 * hp:2 * hp + 2],
                                        KM2[0:1, 0:1], None, op0=ALU.mult),
                                    waits=[(V, vb)])
                        sqs = b.emit("scalar", lambda e, hp=hp: e.activation(
                            CJF[0:1, 2 * hp:2 * hp + 2],
                            QMh[0:1, 2 * hp:2 * hp + 2],
                            AF.Sqrt, scale=1.0 / 64.0), waits=[(V, vm)])
                        v4 = b.emit("vector", lambda e, hp=hp:
                                    e.tensor_scalar_mul(
                                        CJB2[0:1, 2 * hp:2 * hp + 2],
                                        CJF[0:1, 2 * hp:2 * hp + 2], -1.0),
                                    waits=[(S, sqs), (TS, bc_tfree)])
                        if pend_bc:
                            emit_bc(*pend_bc.pop())
                        pend_bc.append((hp, v4))

                    for hp in range(H // 2):
                        for c in range(8):
                            b.emit("tensor", lambda e, c=c, hp=hp: e.matmul(
                                PQ[:], WQLT[c][:, hp * 128:(hp + 1) * 128],
                                XTJ[c][:], start=(c == 0), stop=(c == 7)),
                                waits=xtj_toks + [(V, pq_free_v)])
                        tq = b.sems["t"].n
                        qlb_ready[hp] = b.emit(
                            "vector", lambda e, hp=hp: e.tensor_copy(
                                QL2[hp][:], PQ[:]), waits=[(TS, tq)])
                        pq_free_v = qlb_ready[hp]
                        sq = b.emit("scalar", lambda e, hp=hp: e.activation(
                            SQ2[hp % 2][:], QL2[hp][:], AF.Square),
                            waits=[(V, qlb_ready[hp]),
                                   (TS, sqb_free[hp % 2])])
                        if pend_qn is not None:
                            emit_qn(*pend_qn)
                        pend_qn = (hp, sq)
                    emit_qn(*pend_qn)
                    while pend_bc:
                        emit_bc(*pend_bc.pop())
                    xtj_tfree = b.sems["t"].n
                    if J + 1 < NJ:
                        pending_xtj = emit_xtj(J + 1)

                    npairs = 2 * (J + 1)
                    steps = [(h, p) for h in range(H) for p in range(npairs)]
                    spair_t = {}
                    exp_tok = {}
                    pend_norm = []

                    def emit_spair(st):
                        h, p = st
                        bk = (h * npairs + p) % 2
                        for half in range(2):
                            i = 2 * p + half
                            b.emit("tensor",
                                   lambda e, i=i, h=h, bk=bk,
                                   half=half: e.matmul(
                                       PSA[bk][:, half * 512:
                                               half * 512 + 512],
                                       KLT[(h % 2) * 64:
                                           (h % 2) * 64 + 64,
                                           i * 128:(i + 1) * 128],
                                       QL2[h // 2][(h % 2) * 64:
                                                   (h % 2) * 64 + 64, :],
                                       start=True, stop=True),
                                   waits=[(V, qlb_ready[h // 2]),
                                          (S, exp_bank[bk])])
                        spair_t[st] = b.sems["t"].n

                    def emit_exp(st, J=J):
                        h, p = st
                        bk = (h * npairs + p) % 2
                        ex = b.emit("scalar", lambda e, bk=bk, h=h:
                                    e.activation(
                                        PBUF[bk][:], PSA[bk][:],
                                        AF.Exp, scale=0.125,
                                        bias=NCJH[:, h:h + 1]),
                                    waits=[(TS, spair_t[st]),
                                           (TS, pvp_bank[bk]),
                                           ncj_tok[h]])
                        exp_bank[bk] = ex
                        src = [(S, ex)]
                        if p >= 2 * J:
                            for half in range(2):
                                cdiag = 2 * p + half - 4 * J
                                mv = b.emit(
                                    "vector",
                                    lambda e, bk=bk, half=half,
                                    cdiag=cdiag: e.tensor_mul(
                                        PBUF[bk][:, half * 512:
                                                 half * 512 + 512],
                                        PBUF[bk][:, half * 512:
                                                 half * 512 + 512],
                                        MSK[:, cdiag * 512:
                                            (cdiag + 1) * 512]),
                                    waits=[(S, ex), cm])
                            src = [(V, mv)]
                        exp_tok[st] = src

                    def emit_pvpair(st, npairs=npairs):
                        h, p = st
                        bk = (h * npairs + p) % 2
                        pcb = PC if h % 2 == 0 else PC2
                        w8 = list(exp_tok[st])
                        if p == 0:
                            w8.append((V, pcb_free[h % 2]))
                        for half in range(2):
                            i = 2 * p + half
                            st_f = (p == 0 and half == 0)
                            sp_f = (p == npairs - 1 and half == 1)
                            b.emit("tensor",
                                   lambda e, i=i, bk=bk, half=half,
                                   st_f=st_f, sp_f=sp_f, pcb=pcb:
                                   e.matmul(
                                       pcb[0:65, :],
                                       VLA[:, i * 65:i * 65 + 65],
                                       PBUF[bk][:, half * 512:
                                                half * 512 + 512],
                                       start=st_f, stop=sp_f),
                                   waits=w8)
                            w8 = []
                        pvp_bank[bk] = b.sems["t"].n

                    cu_toks = [0, 0, 0, 0]

                    def norm_dve(h):
                        pcb = PC if h % 2 == 0 else PC2
                        g = h % 4
                        pv_last = b.sems["t"].n
                        r0 = b.emit("vector", lambda e, pcb=pcb, g=g:
                                    e.tensor_copy(
                                        RDW[0:1, g * 512:(g + 1) * 512],
                                        pcb[64:65, :]),
                                    waits=[(TS, pv_last)])
                        cu = b.emit("vector", lambda e, pcb=pcb, g=g:
                                    e.tensor_copy(CUB[g][:],
                                                  pcb[0:64, :]),
                                    waits=[(TS, pv_last)])
                        cu_toks[g] = cu
                        pcb_free[h % 2] = cu
                        return r0

                    def norm_v_group(h3, r0):
                        nonlocal po_free_v
                        gp = (h3 // 4) % 2
                        for g in range(4):
                            b.emit("tensor", lambda e, g=g: e.matmul(
                                PO0[0:4, 0:512], UN4[0:1, g * 4:g * 4 + 4],
                                RDW[0:1, g * 512:(g + 1) * 512],
                                start=(g == 0), stop=(g == 3)),
                                waits=[(V, r0), (V, po_free_v), c6])
                        tgl = b.sems["t"].n
                        gr = b.emit("vector", lambda e: e.reciprocal(
                            RD4R[:], PO0[0:4, 0:512]), waits=[(TS, tgl)])
                        gb = b.emit("vector", lambda e, gp=gp: e.tensor_copy(
                            RDB4[gp][:], RD4R[:]),
                            waits=[(V, gr), (TS, rdb_free[gp])])
                        po_free_v = gr
                        return gb

                    def norm_pe_group(h3, gb):
                        nonlocal po_free_v
                        gp = (h3 // 4) % 2
                        for hh in range(h3 - 3, h3 + 1):
                            tb2 = b.emit("tensor", lambda e, hh=hh, gp=gp:
                                         e.matmul(
                                             PO0[0:64, :],
                                             SEL4[:, (hh % 4) * 64:
                                                  (hh % 4) * 64 + 64],
                                             RDB4[gp][:],
                                             start=True, stop=True),
                                         waits=[(V, gb), (V, po_free_v),
                                                c7])
                            cn = b.emit("vector", lambda e, hh=hh:
                                        e.tensor_mul(
                                            CT2[hh // 2][(hh % 2) * 64:
                                                         (hh % 2) * 64 + 64,
                                                         :],
                                            CUB[hh % 4][:], PO0[0:64, :]),
                                        waits=[(V, cu_toks[hh % 4]),
                                               (TS, tb2)])
                            po_free_v = cn
                        rdb_free[gp] = b.sems["t"].n

                    emit_spair(steps[0])
                    if len(steps) > 1:
                        emit_spair(steps[1])
                    emit_exp(steps[0])
                    for idx, st in enumerate(steps):
                        h, p = st
                        emit_pvpair(st)
                        if idx + 2 < len(steps):
                            emit_spair(steps[idx + 2])
                        if idx + 1 < len(steps):
                            emit_exp(steps[idx + 1])
                        if p == npairs - 1:
                            if pend_norm:
                                norm_pe_group(*pend_norm.pop())
                            r0 = norm_dve(h)
                            if h % 4 == 3:
                                pend_norm.append((h, norm_v_group(h, r0)))
                    if pend_norm:
                        norm_pe_group(*pend_norm.pop())
                    ctxn_all = po_free_v

                    for tt in range(4):
                        for hp in range(H // 2):
                            b.emit("tensor", lambda e, hp=hp, tt=tt:
                                   e.matmul(
                                       PO0[:],
                                       CT2[hp][:, tt * 128:(tt + 1) * 128],
                                       WOV2[hp][:, 0:512],
                                       start=(hp == 0),
                                       stop=(hp == H // 2 - 1)),
                                   waits=[(V, ctxn_all), (V, po_free_v),
                                          (V, pcb_free[0]),
                                          (V, pcb_free[1])])
                            b.emit("tensor", lambda e, hp=hp, tt=tt:
                                   e.matmul(
                                       PC[:],
                                       CT2[hp][:, tt * 128:(tt + 1) * 128],
                                       WOV2[hp][:, 512:1024],
                                       start=(hp == 0),
                                       stop=(hp == H // 2 - 1)),
                                   waits=[])
                        t_o = b.sems["t"].n
                        a1 = b.emit("vector", lambda e, tt=tt: e.tensor_add(
                            XS[:, 0:512], PO0[:], XR[tt][:, 0:512]),
                            waits=[(TS, t_o), (V, xs_free), ldr_toks[tt]])
                        a2 = b.emit("vector", lambda e, tt=tt: e.tensor_add(
                            XS[:, 512:1024], PC[:], XR[tt][:, 512:1024]),
                            waits=[(TS, t_o)])
                        po_free_v = a2
                        pcb_free[0] = a2
                        xr_free[tt] = a2
                        lnw = [(V, a2)]
                        if x1b_free[tt % 2]:
                            lnw.append(x1b_free[tt % 2])
                        lnv = ln_tile(b, XS[:], X1B[tt % 2][:], w, lnw)
                        xs_free = lnv
                        x1b_free[tt % 2] = b.emit(
                            "gpsimd", lambda e, J=J, tt=tt: e.dma_start(
                                out=sx1_d[J * 512 + tt * 128:
                                          J * 512 + (tt + 1) * 128, :],
                                in_=X1B[tt % 2][:]), waits=[(V, lnv)])
                if J == NJ - 1:
                    b.barrier()

        if phases == "WA":
            stf2 = None
            for r in range(T // 128):
                tok = b.emit("sync", lambda e, r=r: e.dma_start(
                    out=w["SQ"][:], in_=sx1_d[r * 128:(r + 1) * 128, :]),
                    waits=[stf2] if stf2 else [])
                stf2 = b.emit("gpsimd", lambda e, r=r: e.dma_start(
                    out=out_d[r * 128:(r + 1) * 128, :], in_=w["SQ"][:]),
                    waits=[tok])
            b.barrier()

        if phases == "WAB":
         with contextlib.ExitStack() as pb:
            ep = pb.enter_context
            WGT = [ep(nc.sbuf_tensor(f"WGT{k}", [128, II], BF16))
                   for k in range(8)]
            WUT = [ep(nc.sbuf_tensor(f"WUT{k}", [128, II], BF16))
                   for k in range(8)]
            WDT = [ep(nc.sbuf_tensor(f"WDT{j}", [128, E], BF16))
                   for j in range(22)]
            X1R = [ep(nc.sbuf_tensor(f"X1R{u}", [128, E], F32))
                   for u in range(2)]
            X1QB = [ep(nc.sbuf_tensor(f"X1QB{u}", [128, E], BF16))
                    for u in range(2)]
            X1QT = ep(nc.sbuf_tensor("X1QT", [128, 2048], BF16))
            HCB = [ep(nc.sbuf_tensor(f"HCB{i}", [128, 1024], F32))
                   for i in range(2)]
            HQB = [ep(nc.sbuf_tensor(f"HQB{i}", [128, 1024], BF16))
                   for i in range(2)]
            HQT = ep(nc.sbuf_tensor("HQT", [128, 5632], BF16))
            PG = [ep(nc.psum_tensor(f"PG{i}", [128, 512], F32))
                  for i in range(2)]
            PU = [ep(nc.psum_tensor(f"PU{i}", [128, 512], F32))
                  for i in range(2)]
            PD = [ep(nc.psum_tensor(f"PD{i}", [128, 512], F32))
                  for i in range(2)]

            wload = []
            for k in range(8):
                wload.append(b.emit("sync", lambda e, k=k: e.dma_start(
                    out=WGT[k][:], in_=swg_d[k * 128:(k + 1) * 128, :])))
            for k in range(8):
                wload.append(b.emit("sync", lambda e, k=k: e.dma_start(
                    out=WUT[k][:], in_=swu_d[k * 128:(k + 1) * 128, :])))
            for j in range(22):
                wload.append(b.emit("sync", lambda e, j=j: e.dma_start(
                    out=WDT[j][:], in_=swd_d[j * 128:(j + 1) * 128, :])))

            st = {
                "outb_tok": [None, None],
                "gsf": [0, 0],
                "puf": [0, 0],
                "pd_free": 0,
                "hcb_free": [0, 0],
                "x1r_free": [0, 0],
                "x1q_toks": {0: [], 1: []},
                "hq_free": [[], []],
                "x1qt_toks": {0: [], 1: []},
                "x1qt_tfree": [0, 0],
                "hqt_tfree": [0, 0],
                "ldr_tok": [None, None],
                "pend_fin": None,
            }
            NRU = T // 128

            def stage1_front(ru):
                par = ru % 2
                s1w = [(V, st["x1r_free"][par])] + wload
                if st["outb_tok"][par] is not None:
                    s1w.append(st["outb_tok"][par])
                st["ldr_tok"][par] = b.emit(
                    "sync", lambda e, ru=ru, par=par: e.dma_start(
                        out=X1R[par][:],
                        in_=sx1_d[ru * 128:(ru + 1) * 128, :]),
                    waits=s1w)
                return quant_front(b, X1R[par][:, 0:E], q, E,
                                   waits=[st["ldr_tok"][par]])

            def stage1_back(ru, ctx):
                par = ru % 2
                qv, _ = quant_back(b, ctx, X1QB[par][:, 0:E],
                                   dst_waits=st["x1q_toks"][par])
                st["x1q_toks"][par] = []
                st["x1qt_toks"][par] = []
                for c in range(8):
                    tok = b.emit(
                        "sync", lambda e, c=c, par=par:
                        e.dma_start_transpose(
                            X1QT[:, c * 256 + par * 128:
                                 c * 256 + (par + 1) * 128],
                            X1QB[par][:, c * 128:(c + 1) * 128]),
                        waits=[(V, qv), (TS, st["x1qt_tfree"][par])])
                    st["x1qt_toks"][par].append(tok)
                    st["x1q_toks"][par].append(tok)

            def stage1(ru):
                stage1_back(ru, stage1_front(ru))

            def stage2(ru):
                par = ru % 2
                evs = {}
                hqv = {}
                hctx = {}

                def hback(sb):
                    ctx, hb, Fq = hctx[sb]
                    hv, hsf = quant_back(
                        b, ctx, HQB[hb][:, 0:Fq],
                        dst_waits=list(st["hq_free"][hb]))
                    hqv[sb] = hv
                    st["hcb_free"][hb] = hsf

                def emit_trans(sb):
                    Fq = 1024 if sb < 2 else 768
                    hb = sb % 2
                    toks = []
                    for cc in range(Fq // 128):
                        jj = sb * 8 + cc
                        tok = b.emit(
                            "sync", lambda e, cc=cc, jj=jj, hb=hb:
                            e.dma_start_transpose(
                                HQT[:, jj * 256 + par * 128:
                                    jj * 256 + (par + 1) * 128],
                                HQB[hb][:, cc * 128:(cc + 1) * 128]),
                            waits=[(V, hqv[sb]),
                                   (TS, st["hqt_tfree"][par])])
                        toks.append(tok)
                    st["hq_free"][hb] = list(toks)
                    evs[sb] = toks

                def emit_down(sb):
                    for j in range(sb * 8, min(sb * 8 + 8, 22)):
                        b.emit("tensor", lambda e, j=j: e.matmul(
                            PD[0][:], HQT[:, j * 256 + par * 128:
                                          j * 256 + (par + 1) * 128],
                            WDT[j][:, 0:512],
                            start=(j == 0), stop=(j == 21)),
                            waits=evs[sb] + [(V, st["pd_free"])])
                        b.emit("tensor", lambda e, j=j: e.matmul(
                            PD[1][:], HQT[:, j * 256 + par * 128:
                                          j * 256 + (par + 1) * 128],
                            WDT[j][:, 512:1024],
                            start=(j == 0), stop=(j == 21)), waits=[])

                for ic in range(6):
                    Fi = 512 if ic < 5 else 256
                    bk = ic % 2
                    hb = (ic // 2) % 2
                    off = (ic % 2) * 512
                    if ic == 2 and st["pend_fin"] is not None:
                        st["pend_fin"]()
                        st["pend_fin"] = None
                    for kk in range(8):
                        b.emit("tensor", lambda e, kk=kk, ic=ic, par=par,
                               Fi=Fi, bk=bk: e.matmul(
                                   PG[bk][:, 0:Fi],
                                   X1QT[:, kk * 256 + par * 128:
                                        kk * 256 + (par + 1) * 128],
                                   WGT[kk][:, ic * 512:ic * 512 + Fi],
                                   start=(kk == 0), stop=(kk == 7)),
                               waits=st["x1qt_toks"][par]
                               + [(S, st["gsf"][bk])])
                    tg = b.sems["t"].n
                    sg = b.emit("scalar", lambda e, bk=bk, Fi=Fi,
                                hb=hb, off=off:
                                e.activation(HCB[hb][:, off:off + Fi],
                                             PG[bk][:, 0:Fi], AF.Silu),
                                waits=[(TS, tg),
                                       (V, st["hcb_free"][hb])])
                    st["gsf"][bk] = sg
                    for kk in range(8):
                        b.emit("tensor", lambda e, kk=kk, ic=ic, par=par,
                               Fi=Fi, bk=bk: e.matmul(
                                   PU[bk][:, 0:Fi],
                                   X1QT[:, kk * 256 + par * 128:
                                        kk * 256 + (par + 1) * 128],
                                   WUT[kk][:, ic * 512:ic * 512 + Fi],
                                   start=(kk == 0), stop=(kk == 7)),
                               waits=[(V, st["puf"][bk])])
                    tu = b.sems["t"].n
                    if ic == 5:
                        st["x1qt_tfree"][par] = tu
                    hm = b.emit("vector", lambda e, bk=bk, Fi=Fi,
                                hb=hb, off=off:
                                e.tensor_mul(HCB[hb][:, off:off + Fi],
                                             HCB[hb][:, off:off + Fi],
                                             PU[bk][:, 0:Fi]),
                                waits=[(S, sg), (TS, tu)])
                    st["puf"][bk] = hm
                    if ic % 2 == 1 or ic == 5:
                        sb = ic // 2
                        Fq = off + Fi
                        hctx[sb] = (quant_front(
                            b, HCB[hb][:, 0:Fq], q, Fq,
                            waits=[(V, hm)]), hb, Fq)
                    if ic == 3:
                        hback(0)
                        emit_trans(0)
                    if ic == 5:
                        hback(1)
                        emit_trans(1)
                        if ru >= 1 and ru + 1 < NRU:
                            x1ctx = stage1_front(ru + 1)
                        hback(2)
                        emit_trans(2)
                        if ru >= 1 and ru + 1 < NRU:
                            stage1_back(ru + 1, x1ctx)

                def fin(par=par, ru=ru):
                    emit_down(0)
                    emit_down(1)
                    emit_down(2)
                    td = b.sems["t"].n
                    st["hqt_tfree"][par] = td
                    a1w = [(TS, td)]
                    if st["outb_tok"][par] is not None:
                        a1w.append(st["outb_tok"][par])
                    a1 = b.emit("vector", lambda e, par=par: e.tensor_add(
                        X1R[par][:, 0:512], PD[0][:], X1R[par][:, 0:512]),
                        waits=a1w)
                    a2 = b.emit("vector", lambda e, par=par: e.tensor_add(
                        X1R[par][:, 512:1024], PD[1][:],
                        X1R[par][:, 512:1024]),
                        waits=[(TS, td)])
                    st["pd_free"] = a2
                    lnv = ln_tile(b, X1R[par][:], X1R[par][:], w, [(V, a2)])
                    st["x1r_free"][par] = lnv
                    st["outb_tok"][par] = b.emit(
                        "gpsimd", lambda e, ru=ru, par=par: e.dma_start(
                            out=out_d[ru * 128:(ru + 1) * 128, :],
                            in_=X1R[par][:]), waits=[(V, lnv)])
                st["pend_fin"] = fin

            stage1(0)
            stage1(1)
            for ru in range(NRU):
                stage2(ru)
            st["pend_fin"]()
            st["pend_fin"] = None
            b.barrier()

            fin_waits = [(sm.h, sm.n) for sm in b.dsems["gpsimd"] if sm.n]
            b.streams["gpsimd"].append(
                lambda e, fw=tuple(fin_waits):
                [e.wait_ge(h, v) for h, v in fw])

        with nc.Block() as blk:
            b.flush(blk)

        tot = {}
        for ins in nc.all_instructions():
            si = ins.sync_info
            if si is None:
                continue
            for up in si.on_update:
                if up.update_mode in ("sem-inc", "sem-add-imm"):
                    tot[up.ant_name] = tot.get(up.ant_name, 0) + up.update_value
        for nm, sm in b.sems.items():
            got = tot.get(nm, 0)
            if got != sm.n:
                raise RuntimeError(
                    f"sem {nm}: program increments {got} != tracked {sm.n}")
    return nc


def make_consts():
    ident = np.eye(128).astype(ml_dtypes.bfloat16)
    masks = np.zeros((128, 4 * 512), np.float32)
    for c in range(4):
        s = np.arange(128)[:, None] + c * 128
        j = np.arange(512)[None, :]
        masks[:, c * 512:(c + 1) * 512] = (s <= j).astype(np.float32)
    masks = masks.astype(ml_dtypes.bfloat16)
    onesr = np.ones((1, 128), ml_dtypes.bfloat16)
    onesc = np.ones((64, 1), ml_dtypes.bfloat16)
    onesc2 = np.zeros((128, 33), ml_dtypes.bfloat16)
    onesc2[0:64, 0] = 1
    onesc2[64:128, 32] = 1
    unit4 = np.zeros((1, 16), ml_dtypes.bfloat16)
    for g in range(4):
        unit4[0, g * 4 + g] = 1
    sel4 = np.zeros((4, 256), ml_dtypes.bfloat16)
    for g in range(4):
        sel4[g, g * 64:(g + 1) * 64] = 1
    return ident, masks, onesr, onesc, onesc2, unit4, sel4


def kernel(L0_attn_hidden_states, L0_attn_Wq, L0_attn_Wk_down,
           L0_attn_Wv_down, L0_attn_q_to_latent, L0_attn_v_up, L0_attn_Wout,
           L0_ffn_w_gate, L0_ffn_w_up, L0_ffn_w_down):
    x = np.ascontiguousarray(np.asarray(L0_attn_hidden_states, np.float32))
    B, T, _ = x.shape
    ident, masks, onesr, onesc, onesc2, unit4, sel4 = make_consts()
    shared = {
        "wq": np.ascontiguousarray(np.asarray(L0_attn_Wq, np.float32)),
        "wkd": np.ascontiguousarray(np.asarray(L0_attn_Wk_down, np.float32)),
        "wvd": np.ascontiguousarray(np.asarray(L0_attn_Wv_down, np.float32)),
        "q2l": np.ascontiguousarray(
            np.asarray(L0_attn_q_to_latent, np.float32).reshape(H * 64, 64)),
        "vup": np.ascontiguousarray(
            np.asarray(L0_attn_v_up, np.float32).reshape(H * 64, 64)),
        "wout": np.ascontiguousarray(np.asarray(L0_attn_Wout, np.float32)),
        "wg": np.ascontiguousarray(np.asarray(L0_ffn_w_gate, np.float32)),
        "wu": np.ascontiguousarray(np.asarray(L0_ffn_w_up, np.float32)),
        "wd": np.ascontiguousarray(np.asarray(L0_ffn_w_down, np.float32)),
        "ident": ident, "masks": masks, "onesr": onesr, "onesc": onesc,
        "onesc2": onesc2, "unit4": unit4, "sel4": sel4,
    }
    nc = build(T)
    in_maps = [dict(shared, x=np.ascontiguousarray(x[c])) for c in range(B)]
    res = run_bass_kernel_spmd(nc, in_maps, core_ids=list(range(B)))
    return np.stack([res.results[c]["out"] for c in range(B)], axis=0)


if __name__ == "__main__":
    pass


# version 22 (final)
# speedup vs baseline: 1.0187x; 1.0120x over previous
import contextlib
import numpy as np
import ml_dtypes
import concourse.bass as bass
import concourse.mybir as mybir
from concourse.bass_utils import run_bass_kernel_spmd

F32 = mybir.dt.float32
BF16 = mybir.dt.bfloat16
U8 = mybir.dt.uint8
I32 = mybir.dt.int32
AF = mybir.ActivationFunctionType
ALU = mybir.AluOpType
AX = mybir.AxisListType

E = 1024
H = 16
LT = 64
II = 2816
MAGIC = 12582912.0
NEG_LN8 = -2.0794415416798357
EPS = 1e-5
ENGS = ("sync", "gpsimd", "tensor", "vector", "scalar")


def bcast16(t, nb):
    a = t[:, 0:nb]
    return bass.AP(a.tensor, a.offset, [list(a.ap[0]), [1, nb], [0, 16]])


def view3(a, nb):
    return bass.AP(a.tensor, a.offset, [list(a.ap[0]), [16, nb], [1, 16]])


class Sem:
    __slots__ = ("h", "n", "name")

    def __init__(self, h, name):
        self.h = h
        self.n = 0
        self.name = name


NDMASEM = 8


class Bld:

    def __init__(self, nc, stack):
        self.nc = nc
        self.streams = {k: [] for k in ENGS}
        self.guar = {k: {} for k in ENGS}
        self.sems = {}
        for name in ("t", "v", "s", "bar"):
            h = stack.enter_context(nc.semaphore(name))
            self.sems[name] = Sem(h, name)
        self.dsems = {}
        for eng, pfx in (("sync", "ld"), ("gpsimd", "g")):
            lst = []
            for i in range(NDMASEM):
                nm = f"{pfx}{i}"
                h = stack.enter_context(nc.semaphore(nm))
                sm = Sem(h, nm)
                self.sems[nm] = sm
                lst.append(sm)
            self.dsems[eng] = lst
        self.drot = {"sync": 0, "gpsimd": 0}
        self.esem = {"tensor": self.sems["t"], "vector": self.sems["v"],
                     "scalar": self.sems["s"]}
        self.barno = 0

    def emit(self, eng, f, waits=(), inc=True):
        gmap = self.guar[eng]
        real = []
        for sm, val in waits:
            if val is None or val <= 0:
                continue
            if gmap.get(sm.name, 0) < val:
                real.append((sm.h, val))
                gmap[sm.name] = val
        if eng in ("sync", "gpsimd"):
            i = self.drot[eng]
            self.drot[eng] = (i + 1) % NDMASEM
            ds = self.dsems[eng][i]
            prev = ds.n
            ds.n += 16
            after = (ds, ds.n)

            def run(e, real=real, f=f, ds=ds, prev=prev):
                for h, vv in real:
                    e.wait_ge(h, vv)
                if prev:
                    e.wait_ge(ds.h, prev)
                f(e).then_inc(ds.h, 16)
            gmap[ds.name] = prev
        else:
            es = self.esem[eng]
            after = None
            if inc:
                es.n += 1
                after = es.n

            def run(e, real=real, f=f, inc=inc, es=es):
                for h, vv in real:
                    e.wait_ge(h, vv)
                r = f(e)
                if inc:
                    r.then_inc(es.h, 1)
        self.streams[eng].append(run)
        return after

    def barrier(self):
        bar = self.sems["bar"]
        self.barno += 1
        target = len(ENGS) * self.barno
        for eng in ENGS:
            if eng in ("sync", "gpsimd"):
                own = [(sm.h, sm.n) for sm in self.dsems[eng] if sm.n]

                def fin(e, own=own, bar=bar):
                    for h, vv in own:
                        e.wait_ge(h, vv)
                    e.nop().then_inc(bar.h, 1)
            else:
                es = self.esem[eng]

                def fin(e, es=es, n=es.n, bar=bar):
                    e.wait_ge(es.h, n)
                    e.nop().then_inc(bar.h, 1)
            self.streams[eng].append(fin)
        bar.n = target
        snap = {nm: sm.n for nm, sm in self.sems.items()}
        for eng in ENGS:
            self.streams[eng].append(
                lambda e, h=bar.h, v=target: e.wait_ge(h, v))
            g = self.guar[eng]
            for nm, vv in snap.items():
                g[nm] = max(g.get(nm, 0), vv)

    def flush(self, block):
        for eng in ENGS:
            ops = self.streams[eng]
            if not ops:
                continue
            getattr(block, eng)(
                lambda e, ops=tuple(ops): [op(e) for op in ops])
            self.streams[eng] = []


def quant_front(b, src, q, F, waits=()):
    nb = F // 16
    V, S = b.sems["v"], b.sems["s"]
    rot = 1 - getattr(b, "_qrot", 1)
    b._qrot = rot
    ps = getattr(b, "_qtok", None)
    if ps is None:
        ps = b._qtok = [
            {"sxn": 0, "m2": 0, "cp1": 0, "cp2": 0, "f": 0}
            for _ in range(2)]
    p = ps[rot]
    src3 = bass.AP(src.tensor, src.offset,
                   [list(src.ap[0]), [16, nb], [1, 16]])
    qr = q[rot]
    am = qr["AM"][:, 0:nb]
    r6 = qr["R6"][:, 0:nb]
    c6 = qr["C6"][:, 0:nb]
    xn, t1, ta = qr["XN"][:, 0:F], qr["T1"][:, 0:F], qr["TA"][:, 0:F]
    r1, r2, r3 = qr["R1"][:, 0:F], qr["R2"][:, 0:F], qr["R3"][:, 0:F]
    v_am = b.emit("vector", lambda e: e.tensor_reduce(
        am, src3, axis=AX.X, op=ALU.max, apply_absolute_value=True),
        waits=list(waits))
    v_cl = b.emit("vector", lambda e: e.tensor_scalar_max(am, am, 6e-12),
                  waits=[(V, v_am)])
    v_rc = b.emit("vector", lambda e: e.reciprocal(r6, am),
                  waits=[(V, v_cl)])
    v_c6 = b.emit("vector", lambda e: e.tensor_scalar_mul(c6, am, 1.0 / 6.0),
                  waits=[(V, v_cl), (V, p["f"])])
    v_xn = b.emit("vector", lambda e: e.tensor_tensor(
        view3(xn, nb), src3, bcast16(qr["R6"], nb), op=ALU.mult),
        waits=[(V, v_rc), (S, p["sxn"])])
    s_ta = b.emit("scalar", lambda e: e.activation(
        ta, xn, AF.Abs, scale=6.0),
        waits=[(V, v_xn), (V, p["m2"])])
    s_a = b.emit("scalar", lambda e: e.activation(
        t1, xn, AF.Copy, scale=12.0, bias=MAGIC), waits=[(S, s_ta)])
    s_b = b.emit("scalar", lambda e: e.activation(
        r1, t1, AF.Copy, scale=0.5, bias=-0.5 * MAGIC),
        waits=[(S, s_a), (V, p["cp2"])])
    s_c = b.emit("scalar", lambda e: e.activation(
        t1, xn, AF.Copy, scale=6.0, bias=MAGIC), waits=[(S, s_b)])
    s_d = b.emit("scalar", lambda e: e.activation(
        r2, t1, AF.Copy, scale=1.0, bias=-MAGIC),
        waits=[(S, s_c), (V, p["cp1"])])
    s_e = b.emit("scalar", lambda e: e.activation(
        t1, xn, AF.Copy, scale=3.0, bias=MAGIC), waits=[(S, s_d)])
    s_f = b.emit("scalar", lambda e: e.activation(
        r3, t1, AF.Copy, scale=2.0, bias=-2.0 * MAGIC),
        waits=[(S, s_e), (V, p["f"])])
    return {"rot": rot, "nb": nb, "F": F, "xn": v_xn, "qr": qr,
            "s_ta": s_ta, "s_e": s_e, "s_f": s_f}


def quant_back(b, ctx, dst, dst_waits=()):
    V, S = b.sems["v"], b.sems["s"]
    rot, nb, F = ctx["rot"], ctx["nb"], ctx["F"]
    p = b._qtok[rot]
    qr = ctx["qr"]
    ta = qr["TA"][:, 0:F]
    r1, r2, r3 = qr["R1"][:, 0:F], qr["R2"][:, 0:F], qr["R3"][:, 0:F]
    m1, m2 = qr["M1"][:, 0:F], qr["M2"][:, 0:F]
    dst3 = bass.AP(dst.tensor, dst.offset,
                   [list(dst.ap[0]), [16, nb], [1, 16]])
    v_m1 = b.emit("vector", lambda e: e.tensor_single_scalar(
        m1, ta, 2.0, op=ALU.is_le), waits=[(S, ctx["s_ta"])])
    v_m2 = b.emit("vector", lambda e: e.tensor_single_scalar(
        m2, ta, 4.0, op=ALU.is_le), waits=[(V, v_m1)])
    v_cp1 = b.emit("vector", lambda e: e.copy_predicated(r3, m2, r2),
                   waits=[(V, v_m2), (S, ctx["s_f"])])
    v_cp2 = b.emit("vector", lambda e: e.copy_predicated(r3, m1, r1),
                   waits=[(V, v_cp1)])
    v_f = b.emit("vector", lambda e: e.tensor_tensor(
        dst3, view3(r3, nb), bcast16(qr["C6"], nb), op=ALU.mult),
        waits=[(V, v_cp2)] + list(dst_waits))
    p.update(f=v_f, sxn=ctx["s_e"], m2=v_m2, cp1=v_cp1, cp2=v_cp2)
    return v_f, ctx["xn"]


def quant_chunk(b, src, dst, q, F, waits=(), dst_waits=()):
    ctx = quant_front(b, src, q, F, waits=waits)
    return quant_back(b, ctx, dst, dst_waits=dst_waits)


def rsqrt_v(b, out, x, t1, t2, waits):
    V = b.sems["v"]
    xi = x.bitcast(I32)
    oi = out.bitcast(I32)
    v = b.emit("vector", lambda e: e.tensor_single_scalar(
        oi, xi, 1, op=ALU.arith_shift_right), waits=waits)
    v = b.emit("vector", lambda e: e.tensor_single_scalar(
        oi, oi, -1, op=ALU.mult), waits=[(V, v)])
    v = b.emit("vector", lambda e: e.tensor_single_scalar(
        oi, oi, 0x5F3759DF, op=ALU.add), waits=[(V, v)])
    for _ in range(3):
        v = b.emit("vector", lambda e: e.tensor_mul(t1, out, out),
                   waits=[(V, v)])
        v = b.emit("vector", lambda e: e.tensor_mul(t2, t1, x),
                   waits=[(V, v)])
        v = b.emit("vector", lambda e: e.tensor_scalar(
            t2, t2, -0.5, 1.5, op0=ALU.mult, op1=ALU.add), waits=[(V, v)])
        v = b.emit("vector", lambda e: e.tensor_mul(out, out, t2),
                   waits=[(V, v)])
    return v


def ln_tile(b, xs, out, w, src_waits):
    V, S = b.sems["v"], b.sems["s"]
    prev = getattr(b, "_lnprev", 0)
    prev_s = getattr(b, "_lnprev_s", 0)
    s1 = b.emit("scalar", lambda e: e.activation(
        w["DSQ"][:], xs, AF.Square, accum_out=w["VRR"][:]),
        waits=list(src_waits) + [(V, prev), (S, prev_s)])
    s2 = b.emit("scalar", lambda e: e.activation(
        w["DSQ"][:], xs, AF.Copy, accum_out=w["MUR"][:]),
        waits=[(S, s1)])
    b._lnprev_s = s2
    v2 = b.emit("vector", lambda e: e.tensor_scalar_mul(
        w["MU"][:], w["MUR"][:], 1.0 / 1024.0),
        waits=[(S, s2), (V, prev)])
    v3 = b.emit("vector", lambda e: e.tensor_mul(
        w["M2"][:], w["MU"][:], w["MU"][:]), waits=[(V, v2)])
    v5 = b.emit("vector", lambda e: e.scalar_tensor_tensor(
        w["VR"][:], w["VRR"][:], 1.0 / 1024.0, w["M2"][:],
        op0=ALU.mult, op1=ALU.subtract), waits=[(V, v3), (S, s1)])
    v5b = b.emit("vector", lambda e: e.tensor_scalar_add(
        w["VR"][:], w["VR"][:], EPS), waits=[(V, v5)])
    s3 = b.emit("scalar", lambda e: e.activation(
        w["LNV"][:], w["VR"][:], AF.Sqrt), waits=[(V, v5b), (S, prev_s)])
    b._lnprev_s = max(b._lnprev_s, s3)
    v6 = b.emit("vector", lambda e: e.reciprocal(
        w["RS"][:], w["LNV"][:]), waits=[(S, s3)])
    vf = b.emit("vector", lambda e: e.tensor_scalar(
        out, xs, w["MU"][:, 0:1], w["RS"][:, 0:1],
        op0=ALU.subtract, op1=ALU.mult), waits=[(V, v6)])
    b._lnprev = vf
    return vf


def build(T=4096, phases="WAB", acut=40):
    nc = bass.Bass("TRN2", target_bir_lowering=False, debug=False)
    NJ = T // 512
    NR = T // 128
    NK = T // 256

    dt_ = nc.dram_tensor
    x_d = dt_("x", [T, E], F32, kind="ExternalInput").ap()
    wq_d = dt_("wq", [E, E], F32, kind="ExternalInput").ap()
    wkd_d = dt_("wkd", [LT, E], F32, kind="ExternalInput").ap()
    wvd_d = dt_("wvd", [LT, E], F32, kind="ExternalInput").ap()
    q2l_d = dt_("q2l", [H * 64, 64], F32, kind="ExternalInput").ap()
    vup_d = dt_("vup", [H * 64, 64], F32, kind="ExternalInput").ap()
    wout_d = dt_("wout", [E, E], F32, kind="ExternalInput").ap()
    wg_d = dt_("wg", [II, E], F32, kind="ExternalInput").ap()
    wu_d = dt_("wu", [II, E], F32, kind="ExternalInput").ap()
    wd_d = dt_("wd", [E, II], F32, kind="ExternalInput").ap()
    id_d = dt_("ident", [128, 128], BF16, kind="ExternalInput").ap()
    mk_d = dt_("masks", [128, 4 * 512], BF16, kind="ExternalInput").ap()
    onr_d = dt_("onesr", [1, 128], BF16, kind="ExternalInput").ap()
    onc_d = dt_("onesc", [64, 1], BF16, kind="ExternalInput").ap()
    onc2_d = dt_("onesc2", [128, 33], BF16, kind="ExternalInput").ap()
    un4_d = dt_("unit4", [1, 16], BF16, kind="ExternalInput").ap()
    sel4_d = dt_("sel4", [4, 256], BF16, kind="ExternalInput").ap()
    out_d = dt_("out", [T, E], F32, kind="ExternalOutput").ap()
    sx1_d = dt_("s_x1", [T, E], F32, kind="Internal").ap()
    swg_d = dt_("s_wgT", [E, II], BF16, kind="Internal").ap()
    swu_d = dt_("s_wuT", [E, II], BF16, kind="Internal").ap()
    swd_d = dt_("s_wdT", [II, E], BF16, kind="Internal").ap()

    with contextlib.ExitStack() as top:
        en = top.enter_context
        b = Bld(nc, top)
        TS, V, S = (b.sems[k] for k in ("t", "v", "s"))

        IDN = en(nc.sbuf_tensor("IDN", [128, 128], BF16))
        ONR = en(nc.sbuf_tensor("ONR", [1, 128], BF16))
        ONC = en(nc.sbuf_tensor("ONC", [64, 1], BF16))
        ONC2 = en(nc.sbuf_tensor("ONC2", [128, 33], BF16))
        UN4 = en(nc.sbuf_tensor("UN4", [1, 16], BF16))
        SEL4 = en(nc.sbuf_tensor("SEL4", [4, 256], BF16))
        KM2 = en(nc.sbuf_tensor("KM2", [1, 1], F32))
        qsh = {}
        for nm, sh, d2 in (("AM", [128, 64], F32), ("R6", [128, 64], F32),
                           ("T1", [128, 1024], F32)):
            qsh[nm] = en(nc.sbuf_tensor("Q" + nm, sh, d2))
        q = []
        for qi in range(2):
            qc = dict(qsh)
            for nm, sh, d2 in (("C6", [128, 64], F32),
                               ("XN", [128, 1024], F32),
                               ("TA", [128, 1024], BF16),
                               ("R1", [128, 1024], BF16),
                               ("R2", [128, 1024], BF16),
                               ("R3", [128, 1024], BF16),
                               ("M1", [128, 1024], U8),
                               ("M2", [128, 1024], U8)):
                qc[nm] = en(nc.sbuf_tensor(f"Q{qi}{nm}", sh, d2))
            q.append(qc)
        w = {}
        for nm, sh in (("MU", [128, 1]), ("M2", [128, 1]), ("VR", [128, 1]),
                       ("LNV", [128, 1]), ("RS", [128, 1]), ("T2", [128, 1]),
                       ("MUR", [128, 1]), ("VRR", [128, 1])):
            w[nm] = en(nc.sbuf_tensor("W" + nm, sh, F32))
        w["DSQ"] = en(nc.sbuf_tensor("WDSQ", [128, E], BF16))

        c1 = b.emit("sync", lambda e: e.dma_start(out=IDN[:], in_=id_d))
        c3 = b.emit("sync", lambda e: e.dma_start(out=ONR[:], in_=onr_d))
        c4 = b.emit("sync", lambda e: e.dma_start(out=ONC[:], in_=onc_d))
        c5 = b.emit("sync", lambda e: e.dma_start(out=ONC2[:], in_=onc2_d))
        c6 = b.emit("sync", lambda e: e.dma_start(out=UN4[:], in_=un4_d))
        c7 = b.emit("sync", lambda e: e.dma_start(out=SEL4[:], in_=sel4_d))

        with contextlib.ExitStack() as wa:
            ea = wa.enter_context
            MSK = ea(nc.sbuf_tensor("MSK", [128, 4 * 512], BF16))
            KLT = ea(nc.sbuf_tensor("KLT", [128, T], BF16))
            VLA = ea(nc.sbuf_tensor("VLA", [128, NR * 65], BF16))
            WQLT = [ea(nc.sbuf_tensor(f"WQLT{c}", [128, E], BF16))
                    for c in range(8)]
            WOV2 = [ea(nc.sbuf_tensor(f"WOV2{i}", [128, E], BF16))
                    for i in range(H // 2)]
            WKDT = ea(nc.sbuf_tensor("WKDT", [128, 512], BF16))
            WVDT = ea(nc.sbuf_tensor("WVDT", [128, 512], BF16))
            XR = [ea(nc.sbuf_tensor(f"XR{i}", [128, E], F32))
                  for i in range(4)]
            LBB = ea(nc.sbuf_tensor("LBB", [128, E], BF16))

            cm = b.emit("sync", lambda e: e.dma_start(out=MSK[:], in_=mk_d))

            with contextlib.ExitStack() as pw:
                ew = pw.enter_context
                LB = ew(nc.sbuf_tensor("LB", [128, II], F32))
                QLB = ew(nc.sbuf_tensor("QLB", [128, 512], BF16))
                VUB = ew(nc.sbuf_tensor("VUB", [128, 512], BF16))
                VUPT = [ew(nc.sbuf_tensor(f"VUPT{r}", [128, 64], BF16))
                        for r in range(8)]
                VTMP = ew(nc.sbuf_tensor("VTMP", [64, 64], BF16))
                XTW = [ew(nc.sbuf_tensor(f"XTW{i}", [128, 8 * 128], BF16))
                       for i in range(2)]
                WOUTT = [ew(nc.sbuf_tensor(f"WOUTT{c}", [128, E], BF16))
                         for c in range(8)]
                KSQ = ew(nc.sbuf_tensor("KSQ", [64, T], BF16))
                KMT = ew(nc.sbuf_tensor("KMT", [1, 1], F32))
                PTW = [ew(nc.psum_tensor(f"PTW{i}", [128, 128], BF16))
                       for i in range(2)]
                PW1 = [ew(nc.psum_tensor(f"PW1{i}", [128, 512], F32))
                       for i in range(2)]
                PK = ew(nc.psum_tensor("PK", [64, 128], F32))
                PK2 = ew(nc.psum_tensor("PK2", [1, 512], F32))

                ptw_free = [0, 0]
                pw1_free = [0, 0]

                lw = b.emit("sync", lambda e: e.dma_start(
                    out=LB[0:64, 0:E], in_=wkd_d))
                lv = b.emit("sync", lambda e: e.dma_start(
                    out=LB[64:128, 0:E], in_=wvd_d))
                vc = b.emit("vector", lambda e: e.tensor_copy(
                    LBB[:, 0:E], LB[:, 0:E]), waits=[lw, lv])
                lbb_free = 0
                for src_b, dstt in ((0, WKDT), (64, WVDT)):
                    for c in range(8):
                        tt = b.emit("tensor", lambda e, c=c, src_b=src_b:
                                    e.transpose(
                                        PTW[c % 2][:, 0:64],
                                        LBB[src_b:src_b + 64,
                                            c * 128:(c + 1) * 128],
                                        IDN[src_b:src_b + 64,
                                            src_b:src_b + 64]),
                                    waits=[(V, vc), c1,
                                           (V, ptw_free[c % 2])])
                        lbb_free = tt
                        ptw_free[c % 2] = b.emit(
                            "vector", lambda e, c=c, dstt=dstt: e.tensor_copy(
                                dstt[:, c * 64:(c + 1) * 64],
                                PTW[c % 2][:, 0:64]), waits=[(TS, tt)])

                lqs = []
                for r in range(8):
                    lqs.append(b.emit("sync", lambda e, r=r: e.dma_start(
                        out=LB[:, r * 64:(r + 1) * 64],
                        in_=q2l_d[r * 128:(r + 1) * 128, :]),
                        waits=[(V, vc)]))
                vq = b.emit("vector", lambda e: e.tensor_copy(
                    QLB[:], LB[:, 0:512]), waits=lqs)
                lus = []
                for r in range(8):
                    lus.append(b.emit("sync", lambda e, r=r: e.dma_start(
                        out=LB[:, 512 + r * 64:512 + (r + 1) * 64],
                        in_=vup_d[r * 128:(r + 1) * 128, :]),
                        waits=[(V, vq)]))
                vu = b.emit("vector", lambda e: e.tensor_copy(
                    VUB[:], LB[:, 512:1024]), waits=lus)

                vupt_done = []
                for h in range(H):
                    r, base = h // 2, (h % 2) * 64
                    tt = b.emit("tensor", lambda e, r=r, base=base, h=h:
                                e.transpose(
                                    PTW[h % 2][0:64, 0:64],
                                    VUB[base:base + 64, r * 64:(r + 1) * 64],
                                    IDN[base:base + 64, base:base + 64]),
                                waits=[(V, vu), (V, ptw_free[h % 2])])
                    if base == 0:
                        dnv = b.emit("vector",
                                     lambda e, r=r, h=h: e.tensor_copy(
                                         VUPT[r][0:64, :],
                                         PTW[h % 2][0:64, 0:64]),
                                     waits=[(TS, tt)])
                        ptw_free[h % 2] = dnv
                        dn = (b.sems["v"], dnv)
                    else:
                        dn0 = b.emit("vector", lambda e, h=h: e.tensor_copy(
                            VTMP[:], PTW[h % 2][0:64, 0:64]),
                            waits=[(TS, tt)] +
                            ([vtmp_free] if h > 1 else []))
                        ptw_free[h % 2] = dn0
                        dn = b.emit("gpsimd", lambda e, r=r: e.dma_start(
                            out=VUPT[r][64:128, :], in_=VTMP[:]),
                            waits=[(V, dn0)])
                        vtmp_free = dn
                    vupt_done.append(dn)

                on_v = b.emit("vector", lambda e: e.memset(
                    bass.AP(VLA[:].tensor, VLA[:].offset + 64,
                            [list(VLA[:].ap[0]), [65, NR], [1, 1]]), 1.0))
                xr_free = {0: 0, 1: 0, 2: 0, 3: 0}
                xtw_free = [0, 0]
                pk_free = 0
                for r in range(NR):
                    ldr = b.emit("sync", lambda e, r=r: e.dma_start(
                        out=XR[r % 2][:], in_=x_d[r * 128:(r + 1) * 128, :]),
                        waits=[(V, xr_free[r % 2])])
                    cb = b.emit("vector", lambda e, r=r: e.tensor_copy(
                        LBB[:, 0:E], XR[r % 2][:]),
                        waits=[ldr, (TS, lbb_free)])
                    xr_free[r % 2] = cb
                    tcop = []
                    for c in range(8):
                        tt = b.emit("tensor", lambda e, c=c: e.transpose(
                            PTW[c % 2][:], LBB[:, c * 128:(c + 1) * 128],
                            IDN[:]),
                            waits=[(V, cb), (V, ptw_free[c % 2])])
                        if c == 7:
                            lbb_free = tt
                        ptw_free[c % 2] = b.emit(
                            "vector", lambda e, c=c, r=r: e.tensor_copy(
                                XTW[r % 2][:, c * 128:(c + 1) * 128],
                                PTW[c % 2][:]),
                            waits=[(TS, tt), (TS, xtw_free[r % 2])])
                        tcop.append(ptw_free[c % 2])
                    for c in range(8):
                        b.emit("tensor", lambda e, c=c, r=r: e.matmul(
                            PK[:, 0:128], WKDT[:, c * 64:(c + 1) * 64],
                            XTW[r % 2][:, c * 128:(c + 1) * 128],
                            start=(c == 0), stop=(c == 7)),
                            waits=[(V, tcop[c]), (V, pk_free)])
                    tk = b.sems["t"].n
                    pk_free = b.emit("vector", lambda e, r=r: e.tensor_copy(
                        KLT[0:64, r * 128:(r + 1) * 128],
                        PK[:, 0:128]),
                        waits=[(TS, tk)])
                    for c in range(8):
                        b.emit("tensor", lambda e, c=c, r=r: e.matmul(
                            PW1[r % 2][:, 0:64],
                            XTW[r % 2][:, c * 128:(c + 1) * 128],
                            WVDT[:, c * 64:(c + 1) * 64],
                            start=(c == 0), stop=(c == 7)),
                            waits=[(V, pw1_free[r % 2])])
                    tv = b.sems["t"].n
                    pw1_free[r % 2] = b.emit(
                        "vector", lambda e, r=r: e.tensor_copy(
                            VLA[:, r * 65:r * 65 + 64], PW1[r % 2][:, 0:64]),
                        waits=[(TS, tv), (V, on_v)])
                    xtw_free[r % 2] = b.sems["t"].n
                klat_done = pk_free
                kdup = b.emit("gpsimd", lambda e: e.dma_start(
                    out=KLT[64:128, 0:T], in_=KLT[0:64, 0:T]),
                    waits=[(V, klat_done)])

                sk = b.emit("scalar", lambda e: e.activation(
                    KSQ[:, 0:T], KLT[0:64, 0:T], AF.Square),
                    waits=[(V, klat_done)])
                km_prev = 0
                pk2_free = 0
                for blk in range(T // 512):
                    tk2 = b.emit("tensor", lambda e, blk=blk: e.matmul(
                        PK2[:], ONC[:, 0:1],
                        KSQ[:, blk * 512:(blk + 1) * 512],
                        start=True, stop=True),
                        waits=[(S, sk), c4, (V, pk2_free)])
                    vmx = b.emit("vector", lambda e: e.reduce_max(
                        KMT[0:1, 0:1], PK2[:], axis=AX.X),
                        waits=[(TS, tk2), (V, km_prev)])
                    pk2_free = vmx
                    if blk == 0:
                        km_prev = b.emit("vector", lambda e: e.tensor_copy(
                            KM2[:], KMT[0:1, 0:1]), waits=[(V, vmx)])
                    else:
                        km_prev = b.emit("vector", lambda e: e.tensor_max(
                            KM2[:], KM2[:], KMT[0:1, 0:1]), waits=[(V, vmx)])

                for r in range(8):
                    ldr = b.emit("sync", lambda e, r=r: e.dma_start(
                        out=XR[2][:], in_=wq_d[r * 128:(r + 1) * 128, :]),
                        waits=[(V, xr_free[2])])
                    cb = b.emit("vector", lambda e: e.tensor_copy(
                        LBB[:, 0:E], XR[2][:]),
                        waits=[ldr, (TS, lbb_free)])
                    xr_free[2] = cb
                    for hh in range(2):
                        h = 2 * r + hh
                        base = hh * 64
                        for c in range(8):
                            tt = b.emit(
                                "tensor",
                                lambda e, c=c, base=base, r=r: e.matmul(
                                    PW1[c % 2][:, 0:64],
                                    LBB[base:base + 64,
                                        c * 128:(c + 1) * 128],
                                    QLB[base:base + 64,
                                        r * 64:(r + 1) * 64],
                                    start=True, stop=True),
                                waits=[(V, cb), (V, vq),
                                       (V, pw1_free[c % 2])])
                            if c == 7:
                                lbb_free = tt
                            pw1_free[c % 2] = b.emit(
                                "vector", lambda e, c=c, h=h: e.tensor_copy(
                                    WQLT[c][:, h * 64:(h + 1) * 64],
                                    PW1[c % 2][:, 0:64]), waits=[(TS, tt)])

                for r in range(8):
                    ldr = b.emit("sync", lambda e, r=r: e.dma_start(
                        out=XR[2][:], in_=wout_d[r * 128:(r + 1) * 128, :]),
                        waits=[(V, xr_free[2])])
                    cb = b.emit("vector", lambda e: e.tensor_copy(
                        LBB[:, 0:E], XR[2][:]),
                        waits=[ldr, (TS, lbb_free)])
                    xr_free[2] = cb
                    for c in range(8):
                        tt = b.emit("tensor", lambda e, c=c: e.transpose(
                            PTW[c % 2][:], LBB[:, c * 128:(c + 1) * 128],
                            IDN[:]),
                            waits=[(V, cb), (V, ptw_free[c % 2])])
                        if c == 7:
                            lbb_free = tt
                        ptw_free[c % 2] = b.emit(
                            "vector", lambda e, c=c, r=r: e.tensor_copy(
                                WOUTT[c][:, r * 128:(r + 1) * 128],
                                PTW[c % 2][:]), waits=[(TS, tt)])
                woutt_done = ptw_free[1]

                for h in range(H):
                    r, base = h // 2, (h % 2) * 64
                    for nn in range(2):
                        tt = b.emit(
                            "tensor",
                            lambda e, r=r, base=base, h=h, nn=nn: e.matmul(
                                PW1[nn][0:64, 0:512],
                                VUPT[r][base:base + 64, :],
                                WOUTT[h // 2][base:base + 64,
                                              nn * 512:(nn + 1) * 512],
                                start=True, stop=True),
                            waits=[(V, woutt_done), (V, pw1_free[nn]),
                                   vupt_done[h]])
                        pw1_free[nn] = b.emit(
                            "vector", lambda e, h=h, nn=nn: e.tensor_copy(
                                WOV2[h // 2][(h % 2) * 64:
                                             (h % 2) * 64 + 64,
                                             nn * 512:(nn + 1) * 512],
                                PW1[nn][0:64, 0:512]), waits=[(TS, tt)])

                with contextlib.ExitStack() as ps1:
                    STGW = [ps1.enter_context(
                        nc.sbuf_tensor(f"STGW{c}", [128, II], BF16))
                        for c in range(8)]
                    stgw_tokens = []
                    ptwq_free = [(V, ptw_free[0]), (V, ptw_free[1])]
                    for w_d, s_d in ((wg_d, swg_d), (wu_d, swu_d)):
                        stg_free = list(stgw_tokens)
                        pend = [None]

                        def wflush(pr, ctx, stg_free=stg_free):
                            nonlocal lbb_free
                            qv, _ = quant_back(
                                b, ctx, LBB[:, 0:E],
                                dst_waits=[(TS, lbb_free)])
                            for c in range(8):
                                tt = b.emit(
                                    "tensor", lambda e, c=c: e.transpose(
                                        PTW[c % 2][:],
                                        LBB[:, c * 128:(c + 1) * 128],
                                        IDN[:]),
                                    waits=[(V, qv),
                                           ptwq_free[c % 2]])
                                if c == 7:
                                    lbb_free = tt
                                ptwq_free[c % 2] = (S, b.emit(
                                    "scalar",
                                    lambda e, c=c, pr=pr: e.activation(
                                        STGW[c][:, pr * 128:(pr + 1) * 128],
                                        PTW[c % 2][:], AF.Copy),
                                    waits=[(TS, tt)] + stg_free))

                        for r in range(22):
                            ldr = b.emit("sync", lambda e, r=r, w_d=w_d:
                                         e.dma_start(
                                             out=XR[2][:],
                                             in_=w_d[r * 128:(r + 1) * 128,
                                                     :]),
                                         waits=[(V, xr_free[2])])
                            ctx = quant_front(b, XR[2][:, 0:E], q, E,
                                              waits=[ldr])
                            xr_free[2] = ctx["xn"]
                            if pend[0] is not None:
                                wflush(*pend[0])
                            pend[0] = (r, ctx)
                        wflush(*pend[0])
                        pend[0] = None
                        stgw_tokens = []
                        for c in range(8):
                            stgw_tokens.append(b.emit(
                                "gpsimd", lambda e, c=c, s_d=s_d:
                                e.dma_start(
                                    out=s_d[c * 128:(c + 1) * 128, :],
                                    in_=STGW[c][:]),
                                waits=list(ptwq_free)))
                    stgw_store_done = list(stgw_tokens)

                with contextlib.ExitStack() as ps2:
                    STGD = [ps2.enter_context(
                        nc.sbuf_tensor(f"STGD{c}", [128, E], BF16))
                        for c in range(22)]
                    WDB = ps2.enter_context(
                        nc.sbuf_tensor("WDB", [128, II], BF16))
                    lb_free = vu
                    wdb_free = 0
                    pendd = [None]
                    rowq = {}

                    def dflushrow(r0):
                        nonlocal wdb_free
                        qvf = rowq.pop(r0)
                        for c in range(22):
                            tt = b.emit("tensor", lambda e, c=c: e.transpose(
                                PTW[c % 2][:],
                                WDB[:, c * 128:(c + 1) * 128],
                                IDN[:]),
                                waits=[(V, qvf[min(c // 8, 2)]),
                                       ptwq_free[c % 2]])
                            if c == 21:
                                wdb_free = tt
                            ptwq_free[c % 2] = (S, b.emit(
                                "scalar", lambda e, c=c, r0=r0: e.activation(
                                    STGD[c][:, r0 * 128:(r0 + 1) * 128],
                                    PTW[c % 2][:], AF.Copy),
                                waits=[(TS, tt)]))

                    def dflush(item):
                        r0, si, ctx, off, Fq = item
                        qv, _ = quant_back(b, ctx, WDB[:, off:off + Fq],
                                           dst_waits=[(TS, wdb_free)])
                        rowq.setdefault(r0, []).append(qv)
                        if si == 2:
                            dflushrow(r0)

                    for r in range(8):
                        ldr = b.emit("sync", lambda e, r=r: e.dma_start(
                            out=LB[:, 0:II], in_=wd_d[r * 128:(r + 1) * 128,
                                                      :]),
                            waits=[(V, lb_free)] + stgw_store_done)
                        for si, (off, Fq) in enumerate(
                                ((0, 1024), (1024, 1024), (2048, 768))):
                            ctx = quant_front(b, LB[:, off:off + Fq],
                                              q, Fq, waits=[ldr])
                            if pendd[0] is not None:
                                dflush(pendd[0])
                            pendd[0] = (r, si, ctx, off, Fq)
                        lb_free = ctx["xn"]
                    dflush(pendd[0])
                    for c in range(22):
                        b.emit("gpsimd", lambda e, c=c: e.dma_start(
                            out=swd_d[c * 128:(c + 1) * 128, :],
                            in_=STGD[c][:]), waits=list(ptwq_free))

            b.barrier()

            if phases == "W":
                stf = {}
                for r in range(T // 128):
                    lw_ = [stf[r % 4]] if (r % 4) in stf else []
                    tok = b.emit("sync", lambda e, r=r: e.dma_start(
                        out=XR[r % 4][:],
                        in_=x_d[r * 128:(r + 1) * 128, :]), waits=lw_)
                    stf[r % 4] = b.emit("gpsimd", lambda e, r=r: e.dma_start(
                        out=out_d[r * 128:(r + 1) * 128, :],
                        in_=XR[r % 4][:]), waits=[tok])
                b.barrier()

            if phases != "W":
             with contextlib.ExitStack() as pa:
                ep = pa.enter_context
                XTJ = [ep(nc.sbuf_tensor(f"XTJ{c}", [128, 512], BF16))
                       for c in range(8)]
                XP = [ep(nc.sbuf_tensor(f"XP{i}", [128, E], F32))
                      for i in range(2)]
                XRB = [ep(nc.sbuf_tensor(f"XRB{i}", [128, E], BF16))
                       for i in range(2)]
                QL2 = [ep(nc.sbuf_tensor(f"QL2{i}", [128, 512], BF16))
                       for i in range(H // 2)]
                SQ2 = [ep(nc.sbuf_tensor(f"SQ2{i}", [128, 512], BF16))
                       for i in range(2)]
                QM = ep(nc.sbuf_tensor("AQM", [1, 512], F32))
                QM1 = ep(nc.sbuf_tensor("AQM1", [1, 1], F32))
                CJ = ep(nc.sbuf_tensor("CJ", [1, 1], F32))
                CJ2 = ep(nc.sbuf_tensor("CJ2", [1, 1], F32))
                CJT = ep(nc.sbuf_tensor("CJT", [1, 1], F32))
                CJN = ep(nc.sbuf_tensor("CJN", [1, 1], F32))
                CJB = ep(nc.sbuf_tensor("CJB", [1, 1], BF16))
                NCJ = ep(nc.sbuf_tensor("NCJ", [128, 1], F32))
                PBUF = [ep(nc.sbuf_tensor(f"PBUF{i}", [128, 1024], BF16))
                        for i in range(2)]
                CUB = [ep(nc.sbuf_tensor(f"CUB{i}", [64, 512], F32))
                       for i in range(4)]
                RDW = ep(nc.sbuf_tensor("RDW", [1, 2048], BF16))
                RD4R = ep(nc.sbuf_tensor("RD4R", [4, 512], F32))
                RDB4 = [ep(nc.sbuf_tensor(f"RDB4{i}", [4, 512], BF16))
                        for i in range(2)]
                CT2 = [ep(nc.sbuf_tensor(f"CT2{i}", [128, 512], BF16))
                       for i in range(H // 2)]
                XS = ep(nc.sbuf_tensor("XS", [128, E], F32))
                X1B = [ep(nc.sbuf_tensor(f"X1B{i}", [128, E], F32))
                       for i in range(2)]
                PQ = ep(nc.psum_tensor("PQ", [128, 512], F32))
                PSA = [ep(nc.psum_tensor(f"PSA{i}", [128, 1024], F32))
                       for i in range(2)]
                PC = ep(nc.psum_tensor("PC", [128, 512], F32))
                PC2 = ep(nc.psum_tensor("PC2", [128, 512], F32))
                PO0 = ep(nc.psum_tensor("PO0", [128, 512], F32))

                xr_free = {0: 0, 1: 0, 2: 0, 3: 0}
                xrb_free = {0: [], 1: []}
                x1b_free = [None, None]
                pq_free_v = 0
                sqb_free = [0, 0]
                exp_bank = [0, 0]
                pvp_bank = [0, 0]
                pcb_free = [0, 0]
                rdb_free = [0, 0]
                po_free_v = 0
                xs_free = 0

                xp_free = [0, 0]
                xtj_tfree = 0

                def emit_xtj(J):
                    toks = []
                    for tt in range(4):
                        ldp = b.emit("sync", lambda e, J=J, tt=tt:
                                     e.dma_start(
                                         out=XP[tt % 2][:],
                                         in_=x_d[J * 512 + tt * 128:
                                                 J * 512 + (tt + 1) * 128,
                                                 :]),
                                     waits=[(V, xp_free[tt % 2])])
                        cb = b.emit("vector", lambda e, tt=tt: e.tensor_copy(
                            XRB[tt % 2][:], XP[tt % 2][:]),
                            waits=[ldp] + xrb_free[tt % 2])
                        xp_free[tt % 2] = cb
                        xrb_free[tt % 2] = []
                        for c in range(8):
                            tok = b.emit("sync", lambda e, c=c, tt=tt:
                                         e.dma_start_transpose(
                                             XTJ[c][:, tt * 128:
                                                    (tt + 1) * 128],
                                             XRB[tt % 2][:, c * 128:
                                                         (c + 1) * 128]),
                                         waits=[(V, cb), (TS, xtj_tfree)])
                            toks.append(tok)
                            xrb_free[tt % 2].append(tok)
                    return toks

                pending_xtj = emit_xtj(0)
                for J in range(NJ):
                    xtj_toks = pending_xtj
                    ldr_toks = []
                    for tt in range(4):
                        ldr_toks.append(b.emit(
                            "sync", lambda e, J=J, tt=tt: e.dma_start(
                                out=XR[tt][:],
                                in_=x_d[J * 512 + tt * 128:
                                        J * 512 + (tt + 1) * 128, :]),
                            waits=[(V, xr_free[tt])]))

                    qlb_ready = [0] * (H // 2)
                    qm_last = 0
                    pend_qn = None

                    def emit_qn(hp, sq):
                        nonlocal qm_last
                        tn = b.emit("tensor", lambda e, hp=hp: e.matmul(
                            PO0[0:33, 0:512], ONC2[:, 0:33], SQ2[hp % 2][:],
                            start=True, stop=True),
                            waits=[(S, sq), (V, qm_last), c5])
                        sqb_free[hp % 2] = tn
                        if hp == 0:
                            qm_last = b.emit(
                                "vector", lambda e: e.tensor_copy(
                                    QM[:], PO0[0:1, 0:512]),
                                waits=[(TS, tn)])
                        else:
                            qm_last = b.emit(
                                "vector", lambda e: e.tensor_max(
                                    QM[:], QM[:], PO0[0:1, 0:512]),
                                waits=[(TS, tn)])
                        qm_last = b.emit(
                            "vector", lambda e: e.tensor_max(
                                QM[:], QM[:], PO0[32:33, 0:512]),
                            waits=[(V, qm_last)])

                    for hp in range(H // 2):
                        for c in range(8):
                            b.emit("tensor", lambda e, c=c, hp=hp: e.matmul(
                                PQ[:], WQLT[c][:, hp * 128:(hp + 1) * 128],
                                XTJ[c][:], start=(c == 0), stop=(c == 7)),
                                waits=xtj_toks + [(V, pq_free_v)])
                        tq = b.sems["t"].n
                        qlb_ready[hp] = b.emit(
                            "vector", lambda e, hp=hp: e.tensor_copy(
                                QL2[hp][:], PQ[:]), waits=[(TS, tq)])
                        pq_free_v = qlb_ready[hp]
                        sq = b.emit("scalar", lambda e, hp=hp: e.activation(
                            SQ2[hp % 2][:], QL2[hp][:], AF.Square),
                            waits=[(V, qlb_ready[hp]),
                                   (TS, sqb_free[hp % 2])])
                        if pend_qn is not None:
                            emit_qn(*pend_qn)
                        pend_qn = (hp, sq)
                    emit_qn(*pend_qn)
                    xtj_tfree = b.sems["t"].n
                    if J + 1 < NJ:
                        pending_xtj = emit_xtj(J + 1)
                    v1 = b.emit("vector", lambda e: e.reduce_max(
                        QM1[:], QM[:], axis=AX.X), waits=[(V, qm_last)])
                    v2 = b.emit("vector", lambda e: e.tensor_mul(
                        QM1[:], QM1[:], KM2[:]), waits=[(V, v1)])
                    sqs = b.emit("scalar", lambda e: e.activation(
                        CJN[:], QM1[:], AF.Sqrt, scale=1.0 / 64.0),
                        waits=[(V, v2)])
                    v4 = b.emit("vector", lambda e: e.tensor_scalar_mul(
                        CJB[:], CJN[:], -1.0), waits=[(S, sqs)])
                    tb = b.emit("tensor", lambda e: e.matmul(
                        PO0[:, 0:1], ONR[0:1, :], CJB[:],
                        start=True, stop=True), waits=[(V, v4)])
                    ncj_v = b.emit("vector", lambda e: e.tensor_copy(
                        NCJ[:], PO0[:, 0:1]), waits=[(TS, tb)])
                    po_free_v = ncj_v

                    npairs = 2 * (J + 1)
                    steps = [(h, p) for h in range(H) for p in range(npairs)]
                    spair_t = {}
                    exp_tok = {}
                    pend_norm = []

                    def emit_spair(st):
                        h, p = st
                        bk = (h * npairs + p) % 2
                        for half in range(2):
                            i = 2 * p + half
                            b.emit("tensor",
                                   lambda e, i=i, h=h, bk=bk,
                                   half=half: e.matmul(
                                       PSA[bk][:, half * 512:
                                               half * 512 + 512],
                                       KLT[(h % 2) * 64:
                                           (h % 2) * 64 + 64,
                                           i * 128:(i + 1) * 128],
                                       QL2[h // 2][(h % 2) * 64:
                                                   (h % 2) * 64 + 64, :],
                                       start=True, stop=True),
                                   waits=[(V, qlb_ready[h // 2]),
                                          (S, exp_bank[bk])])
                        spair_t[st] = b.sems["t"].n

                    def emit_exp(st, J=J):
                        h, p = st
                        bk = (h * npairs + p) % 2
                        ex = b.emit("scalar", lambda e, bk=bk:
                                    e.activation(
                                        PBUF[bk][:], PSA[bk][:],
                                        AF.Exp, scale=0.125,
                                        bias=NCJ[:, 0:1]),
                                    waits=[(TS, spair_t[st]),
                                           (TS, pvp_bank[bk]),
                                           (V, ncj_v)])
                        exp_bank[bk] = ex
                        src = [(S, ex)]
                        if p >= 2 * J:
                            for half in range(2):
                                cdiag = 2 * p + half - 4 * J
                                mv = b.emit(
                                    "vector",
                                    lambda e, bk=bk, half=half,
                                    cdiag=cdiag: e.tensor_mul(
                                        PBUF[bk][:, half * 512:
                                                 half * 512 + 512],
                                        PBUF[bk][:, half * 512:
                                                 half * 512 + 512],
                                        MSK[:, cdiag * 512:
                                            (cdiag + 1) * 512]),
                                    waits=[(S, ex), cm])
                            src = [(V, mv)]
                        exp_tok[st] = src

                    def emit_pvpair(st, npairs=npairs):
                        h, p = st
                        bk = (h * npairs + p) % 2
                        pcb = PC if h % 2 == 0 else PC2
                        w8 = list(exp_tok[st])
                        if p == 0:
                            w8.append((V, pcb_free[h % 2]))
                        for half in range(2):
                            i = 2 * p + half
                            st_f = (p == 0 and half == 0)
                            sp_f = (p == npairs - 1 and half == 1)
                            b.emit("tensor",
                                   lambda e, i=i, bk=bk, half=half,
                                   st_f=st_f, sp_f=sp_f, pcb=pcb:
                                   e.matmul(
                                       pcb[0:65, :],
                                       VLA[:, i * 65:i * 65 + 65],
                                       PBUF[bk][:, half * 512:
                                                half * 512 + 512],
                                       start=st_f, stop=sp_f),
                                   waits=w8)
                            w8 = []
                        pvp_bank[bk] = b.sems["t"].n

                    cu_toks = [0, 0, 0, 0]

                    def norm_dve(h):
                        pcb = PC if h % 2 == 0 else PC2
                        g = h % 4
                        pv_last = b.sems["t"].n
                        r0 = b.emit("vector", lambda e, pcb=pcb, g=g:
                                    e.tensor_copy(
                                        RDW[0:1, g * 512:(g + 1) * 512],
                                        pcb[64:65, :]),
                                    waits=[(TS, pv_last)])
                        cu = b.emit("vector", lambda e, pcb=pcb, g=g:
                                    e.tensor_copy(CUB[g][:],
                                                  pcb[0:64, :]),
                                    waits=[(TS, pv_last)])
                        cu_toks[g] = cu
                        pcb_free[h % 2] = cu
                        return r0

                    def norm_v_group(h3, r0):
                        nonlocal po_free_v
                        gp = (h3 // 4) % 2
                        for g in range(4):
                            b.emit("tensor", lambda e, g=g: e.matmul(
                                PO0[0:4, 0:512], UN4[0:1, g * 4:g * 4 + 4],
                                RDW[0:1, g * 512:(g + 1) * 512],
                                start=(g == 0), stop=(g == 3)),
                                waits=[(V, r0), (V, po_free_v), c6])
                        tgl = b.sems["t"].n
                        gr = b.emit("vector", lambda e: e.reciprocal(
                            RD4R[:], PO0[0:4, 0:512]), waits=[(TS, tgl)])
                        gb = b.emit("vector", lambda e, gp=gp: e.tensor_copy(
                            RDB4[gp][:], RD4R[:]),
                            waits=[(V, gr), (TS, rdb_free[gp])])
                        po_free_v = gr
                        return gb

                    def norm_pe_group(h3, gb):
                        nonlocal po_free_v
                        gp = (h3 // 4) % 2
                        for hh in range(h3 - 3, h3 + 1):
                            tb2 = b.emit("tensor", lambda e, hh=hh, gp=gp:
                                         e.matmul(
                                             PO0[0:64, :],
                                             SEL4[:, (hh % 4) * 64:
                                                  (hh % 4) * 64 + 64],
                                             RDB4[gp][:],
                                             start=True, stop=True),
                                         waits=[(V, gb), (V, po_free_v),
                                                c7])
                            cn = b.emit("vector", lambda e, hh=hh:
                                        e.tensor_mul(
                                            CT2[hh // 2][(hh % 2) * 64:
                                                         (hh % 2) * 64 + 64,
                                                         :],
                                            CUB[hh % 4][:], PO0[0:64, :]),
                                        waits=[(V, cu_toks[hh % 4]),
                                               (TS, tb2)])
                            po_free_v = cn
                        rdb_free[gp] = b.sems["t"].n

                    emit_spair(steps[0])
                    if len(steps) > 1:
                        emit_spair(steps[1])
                    emit_exp(steps[0])
                    for idx, st in enumerate(steps):
                        h, p = st
                        emit_pvpair(st)
                        if idx + 2 < len(steps):
                            emit_spair(steps[idx + 2])
                        if idx + 1 < len(steps):
                            emit_exp(steps[idx + 1])
                        if p == npairs - 1:
                            if pend_norm:
                                norm_pe_group(*pend_norm.pop())
                            r0 = norm_dve(h)
                            if h % 4 == 3:
                                pend_norm.append((h, norm_v_group(h, r0)))
                    if pend_norm:
                        norm_pe_group(*pend_norm.pop())
                    ctxn_all = po_free_v

                    for tt in range(4):
                        for hp in range(H // 2):
                            b.emit("tensor", lambda e, hp=hp, tt=tt:
                                   e.matmul(
                                       PO0[:],
                                       CT2[hp][:, tt * 128:(tt + 1) * 128],
                                       WOV2[hp][:, 0:512],
                                       start=(hp == 0),
                                       stop=(hp == H // 2 - 1)),
                                   waits=[(V, ctxn_all), (V, po_free_v),
                                          (V, pcb_free[0]),
                                          (V, pcb_free[1])])
                            b.emit("tensor", lambda e, hp=hp, tt=tt:
                                   e.matmul(
                                       PC[:],
                                       CT2[hp][:, tt * 128:(tt + 1) * 128],
                                       WOV2[hp][:, 512:1024],
                                       start=(hp == 0),
                                       stop=(hp == H // 2 - 1)),
                                   waits=[])
                        t_o = b.sems["t"].n
                        a1 = b.emit("vector", lambda e, tt=tt: e.tensor_add(
                            XS[:, 0:512], PO0[:], XR[tt][:, 0:512]),
                            waits=[(TS, t_o), (V, xs_free), ldr_toks[tt]])
                        a2 = b.emit("vector", lambda e, tt=tt: e.tensor_add(
                            XS[:, 512:1024], PC[:], XR[tt][:, 512:1024]),
                            waits=[(TS, t_o)])
                        po_free_v = a2
                        pcb_free[0] = a2
                        xr_free[tt] = a2
                        lnw = [(V, a2)]
                        if x1b_free[tt % 2]:
                            lnw.append(x1b_free[tt % 2])
                        lnv = ln_tile(b, XS[:], X1B[tt % 2][:], w, lnw)
                        xs_free = lnv
                        x1b_free[tt % 2] = b.emit(
                            "gpsimd", lambda e, J=J, tt=tt: e.dma_start(
                                out=sx1_d[J * 512 + tt * 128:
                                          J * 512 + (tt + 1) * 128, :],
                                in_=X1B[tt % 2][:]), waits=[(V, lnv)])
                    b.barrier()
                    xr_free = {0: 0, 1: 0, 2: 0, 3: 0}
                    xrb_free = {0: [], 1: []}
                    x1b_free = [None, None]
                    pq_free_v = 0
                    sqb_free = [0, 0]
                    exp_bank = [0, 0]
                    pvp_bank = [0, 0]
                    pcb_free = [0, 0]
                    rdb_free = [0, 0]
                    po_free_v = 0
                    xs_free = 0

        if phases == "WA":
            stf2 = None
            for r in range(T // 128):
                tok = b.emit("sync", lambda e, r=r: e.dma_start(
                    out=w["SQ"][:], in_=sx1_d[r * 128:(r + 1) * 128, :]),
                    waits=[stf2] if stf2 else [])
                stf2 = b.emit("gpsimd", lambda e, r=r: e.dma_start(
                    out=out_d[r * 128:(r + 1) * 128, :], in_=w["SQ"][:]),
                    waits=[tok])
            b.barrier()

        if phases == "WAB":
         with contextlib.ExitStack() as pb:
            ep = pb.enter_context
            WGT = [ep(nc.sbuf_tensor(f"WGT{k}", [128, II], BF16))
                   for k in range(8)]
            WUT = [ep(nc.sbuf_tensor(f"WUT{k}", [128, II], BF16))
                   for k in range(8)]
            WDT = [ep(nc.sbuf_tensor(f"WDT{j}", [128, E], BF16))
                   for j in range(22)]
            X1R = [ep(nc.sbuf_tensor(f"X1R{u}", [128, E], F32))
                   for u in range(2)]
            X1QB = [ep(nc.sbuf_tensor(f"X1QB{u}", [128, E], BF16))
                    for u in range(2)]
            X1QT = ep(nc.sbuf_tensor("X1QT", [128, 2048], BF16))
            HCB = [ep(nc.sbuf_tensor(f"HCB{i}", [128, 1024], F32))
                   for i in range(2)]
            HQB = [ep(nc.sbuf_tensor(f"HQB{i}", [128, 1024], BF16))
                   for i in range(2)]
            HQT = ep(nc.sbuf_tensor("HQT", [128, 5632], BF16))
            PG = [ep(nc.psum_tensor(f"PG{i}", [128, 512], F32))
                  for i in range(2)]
            PU = [ep(nc.psum_tensor(f"PU{i}", [128, 512], F32))
                  for i in range(2)]
            PD = [ep(nc.psum_tensor(f"PD{i}", [128, 512], F32))
                  for i in range(2)]
            PT = [ep(nc.psum_tensor(f"PT{i}", [128, 128], BF16))
                  for i in range(2)]

            wload = []
            for k in range(8):
                wload.append(b.emit("sync", lambda e, k=k: e.dma_start(
                    out=WGT[k][:], in_=swg_d[k * 128:(k + 1) * 128, :])))
            for k in range(8):
                wload.append(b.emit("sync", lambda e, k=k: e.dma_start(
                    out=WUT[k][:], in_=swu_d[k * 128:(k + 1) * 128, :])))
            for j in range(22):
                wload.append(b.emit("sync", lambda e, j=j: e.dma_start(
                    out=WDT[j][:], in_=swd_d[j * 128:(j + 1) * 128, :])))

            st = {
                "outb_tok": [None, None],
                "gsf": [0, 0],
                "puf": [0, 0],
                "pd_free": 0,
                "hcb_free": [0, 0],
                "x1r_free": [0, 0],
                "x1q_toks": {0: [], 1: []},
                "hq_free": [0, 0],
                "x1qt_toks": {0: [], 1: []},
                "x1qt_tfree": [0, 0],
                "hqt_tfree": [0, 0],
                "pt_free": [0, 0],
                "ldr_tok": [None, None],
                "pend_fin": None,
            }
            NRU = T // 128

            def stage1_front(ru):
                par = ru % 2
                s1w = [(V, st["x1r_free"][par])] + wload
                if st["outb_tok"][par] is not None:
                    s1w.append(st["outb_tok"][par])
                st["ldr_tok"][par] = b.emit(
                    "sync", lambda e, ru=ru, par=par: e.dma_start(
                        out=X1R[par][:],
                        in_=sx1_d[ru * 128:(ru + 1) * 128, :]),
                    waits=s1w)
                return quant_front(b, X1R[par][:, 0:E], q, E,
                                   waits=[st["ldr_tok"][par]])

            def stage1_back(ru, ctx):
                par = ru % 2
                qv, _ = quant_back(b, ctx, X1QB[par][:, 0:E],
                                   dst_waits=st["x1q_toks"][par])
                st["x1q_toks"][par] = []
                st["x1qt_toks"][par] = []
                for c in range(8):
                    tok = b.emit(
                        "sync", lambda e, c=c, par=par:
                        e.dma_start_transpose(
                            X1QT[:, c * 256 + par * 128:
                                 c * 256 + (par + 1) * 128],
                            X1QB[par][:, c * 128:(c + 1) * 128]),
                        waits=[(V, qv), (TS, st["x1qt_tfree"][par])])
                    st["x1qt_toks"][par].append(tok)
                    st["x1q_toks"][par].append(tok)

            def stage1(ru):
                stage1_back(ru, stage1_front(ru))

            def stage2(ru):
                par = ru % 2
                evs = {}
                hqv = {}
                hctx = {}

                def hback(sb):
                    ctx, hb, Fq = hctx[sb]
                    hv, hsf = quant_back(
                        b, ctx, HQB[hb][:, 0:Fq],
                        dst_waits=[(TS, st["hq_free"][hb])])
                    hqv[sb] = hv
                    st["hcb_free"][hb] = hsf

                def emit_trans(sb):
                    Fq = 1024 if sb < 2 else 768
                    hb = sb % 2
                    ev = 0
                    for cc in range(Fq // 128):
                        jj = sb * 8 + cc
                        tt = b.emit("tensor", lambda e, cc=cc, hb=hb:
                                    e.transpose(
                                        PT[cc % 2][:],
                                        HQB[hb][:, cc * 128:(cc + 1) * 128],
                                        IDN[:]),
                                    waits=[(V, hqv[sb]),
                                           (S, st["pt_free"][cc % 2])])
                        ev = b.emit("scalar", lambda e, cc=cc, jj=jj:
                                    e.activation(
                                        HQT[:, jj * 256 + par * 128:
                                            jj * 256 + (par + 1) * 128],
                                        PT[cc % 2][:], AF.Copy),
                                    waits=[(TS, tt),
                                           (TS, st["hqt_tfree"][par])])
                        st["pt_free"][cc % 2] = ev
                    st["hq_free"][hb] = b.sems["t"].n
                    evs[sb] = ev

                def emit_down(sb):
                    for j in range(sb * 8, min(sb * 8 + 8, 22)):
                        b.emit("tensor", lambda e, j=j: e.matmul(
                            PD[0][:], HQT[:, j * 256 + par * 128:
                                          j * 256 + (par + 1) * 128],
                            WDT[j][:, 0:512],
                            start=(j == 0), stop=(j == 21)),
                            waits=[(S, evs[sb]), (V, st["pd_free"])])
                        b.emit("tensor", lambda e, j=j: e.matmul(
                            PD[1][:], HQT[:, j * 256 + par * 128:
                                          j * 256 + (par + 1) * 128],
                            WDT[j][:, 512:1024],
                            start=(j == 0), stop=(j == 21)), waits=[])

                for ic in range(6):
                    Fi = 512 if ic < 5 else 256
                    bk = ic % 2
                    hb = (ic // 2) % 2
                    off = (ic % 2) * 512
                    if ic == 2 and st["pend_fin"] is not None:
                        st["pend_fin"]()
                        st["pend_fin"] = None
                    for kk in range(8):
                        b.emit("tensor", lambda e, kk=kk, ic=ic, par=par,
                               Fi=Fi, bk=bk: e.matmul(
                                   PG[bk][:, 0:Fi],
                                   X1QT[:, kk * 256 + par * 128:
                                        kk * 256 + (par + 1) * 128],
                                   WGT[kk][:, ic * 512:ic * 512 + Fi],
                                   start=(kk == 0), stop=(kk == 7)),
                               waits=st["x1qt_toks"][par]
                               + [(S, st["gsf"][bk])])
                    tg = b.sems["t"].n
                    sg = b.emit("scalar", lambda e, bk=bk, Fi=Fi,
                                hb=hb, off=off:
                                e.activation(HCB[hb][:, off:off + Fi],
                                             PG[bk][:, 0:Fi], AF.Silu),
                                waits=[(TS, tg),
                                       (V, st["hcb_free"][hb])])
                    st["gsf"][bk] = sg
                    for kk in range(8):
                        b.emit("tensor", lambda e, kk=kk, ic=ic, par=par,
                               Fi=Fi, bk=bk: e.matmul(
                                   PU[bk][:, 0:Fi],
                                   X1QT[:, kk * 256 + par * 128:
                                        kk * 256 + (par + 1) * 128],
                                   WUT[kk][:, ic * 512:ic * 512 + Fi],
                                   start=(kk == 0), stop=(kk == 7)),
                               waits=[(V, st["puf"][bk])])
                    tu = b.sems["t"].n
                    if ic == 5:
                        st["x1qt_tfree"][par] = tu
                    hm = b.emit("vector", lambda e, bk=bk, Fi=Fi,
                                hb=hb, off=off:
                                e.tensor_mul(HCB[hb][:, off:off + Fi],
                                             HCB[hb][:, off:off + Fi],
                                             PU[bk][:, 0:Fi]),
                                waits=[(S, sg), (TS, tu)])
                    st["puf"][bk] = hm
                    if ic % 2 == 1 or ic == 5:
                        sb = ic // 2
                        Fq = off + Fi
                        hctx[sb] = (quant_front(
                            b, HCB[hb][:, 0:Fq], q, Fq,
                            waits=[(V, hm)]), hb, Fq)
                    if ic == 3:
                        hback(0)
                        emit_trans(0)
                        emit_down(0)
                    if ic == 5:
                        hback(1)
                        emit_trans(1)
                        emit_down(1)
                        if ru >= 1 and ru + 1 < NRU:
                            x1ctx = stage1_front(ru + 1)
                        hback(2)
                        emit_trans(2)
                        if ru >= 1 and ru + 1 < NRU:
                            stage1_back(ru + 1, x1ctx)

                def fin(par=par, ru=ru):
                    emit_down(2)
                    td = b.sems["t"].n
                    st["hqt_tfree"][par] = td
                    a1w = [(TS, td)]
                    if st["outb_tok"][par] is not None:
                        a1w.append(st["outb_tok"][par])
                    a1 = b.emit("vector", lambda e, par=par: e.tensor_add(
                        X1R[par][:, 0:512], PD[0][:], X1R[par][:, 0:512]),
                        waits=a1w)
                    a2 = b.emit("vector", lambda e, par=par: e.tensor_add(
                        X1R[par][:, 512:1024], PD[1][:],
                        X1R[par][:, 512:1024]),
                        waits=[(TS, td)])
                    st["pd_free"] = a2
                    lnv = ln_tile(b, X1R[par][:], X1R[par][:], w, [(V, a2)])
                    st["x1r_free"][par] = lnv
                    st["outb_tok"][par] = b.emit(
                        "gpsimd", lambda e, ru=ru, par=par: e.dma_start(
                            out=out_d[ru * 128:(ru + 1) * 128, :],
                            in_=X1R[par][:]), waits=[(V, lnv)])
                st["pend_fin"] = fin

            stage1(0)
            stage1(1)
            for ru in range(NRU):
                stage2(ru)
            st["pend_fin"]()
            st["pend_fin"] = None
            b.barrier()

            fin_waits = [(sm.h, sm.n) for sm in b.dsems["gpsimd"] if sm.n]
            b.streams["gpsimd"].append(
                lambda e, fw=tuple(fin_waits):
                [e.wait_ge(h, v) for h, v in fw])

        with nc.Block() as blk:
            b.flush(blk)

        tot = {}
        for ins in nc.all_instructions():
            si = ins.sync_info
            if si is None:
                continue
            for up in si.on_update:
                if up.update_mode in ("sem-inc", "sem-add-imm"):
                    tot[up.ant_name] = tot.get(up.ant_name, 0) + up.update_value
        for nm, sm in b.sems.items():
            got = tot.get(nm, 0)
            if got != sm.n:
                raise RuntimeError(
                    f"sem {nm}: program increments {got} != tracked {sm.n}")
    return nc


def make_consts():
    ident = np.eye(128).astype(ml_dtypes.bfloat16)
    masks = np.zeros((128, 4 * 512), np.float32)
    for c in range(4):
        s = np.arange(128)[:, None] + c * 128
        j = np.arange(512)[None, :]
        masks[:, c * 512:(c + 1) * 512] = (s <= j).astype(np.float32)
    masks = masks.astype(ml_dtypes.bfloat16)
    onesr = np.ones((1, 128), ml_dtypes.bfloat16)
    onesc = np.ones((64, 1), ml_dtypes.bfloat16)
    onesc2 = np.zeros((128, 33), ml_dtypes.bfloat16)
    onesc2[0:64, 0] = 1
    onesc2[64:128, 32] = 1
    unit4 = np.zeros((1, 16), ml_dtypes.bfloat16)
    for g in range(4):
        unit4[0, g * 4 + g] = 1
    sel4 = np.zeros((4, 256), ml_dtypes.bfloat16)
    for g in range(4):
        sel4[g, g * 64:(g + 1) * 64] = 1
    return ident, masks, onesr, onesc, onesc2, unit4, sel4


def kernel(L0_attn_hidden_states, L0_attn_Wq, L0_attn_Wk_down,
           L0_attn_Wv_down, L0_attn_q_to_latent, L0_attn_v_up, L0_attn_Wout,
           L0_ffn_w_gate, L0_ffn_w_up, L0_ffn_w_down):
    x = np.ascontiguousarray(np.asarray(L0_attn_hidden_states, np.float32))
    B, T, _ = x.shape
    ident, masks, onesr, onesc, onesc2, unit4, sel4 = make_consts()
    shared = {
        "wq": np.ascontiguousarray(np.asarray(L0_attn_Wq, np.float32)),
        "wkd": np.ascontiguousarray(np.asarray(L0_attn_Wk_down, np.float32)),
        "wvd": np.ascontiguousarray(np.asarray(L0_attn_Wv_down, np.float32)),
        "q2l": np.ascontiguousarray(
            np.asarray(L0_attn_q_to_latent, np.float32).reshape(H * 64, 64)),
        "vup": np.ascontiguousarray(
            np.asarray(L0_attn_v_up, np.float32).reshape(H * 64, 64)),
        "wout": np.ascontiguousarray(np.asarray(L0_attn_Wout, np.float32)),
        "wg": np.ascontiguousarray(np.asarray(L0_ffn_w_gate, np.float32)),
        "wu": np.ascontiguousarray(np.asarray(L0_ffn_w_up, np.float32)),
        "wd": np.ascontiguousarray(np.asarray(L0_ffn_w_down, np.float32)),
        "ident": ident, "masks": masks, "onesr": onesr, "onesc": onesc,
        "onesc2": onesc2, "unit4": unit4, "sel4": sel4,
    }
    nc = build(T)
    in_maps = [dict(shared, x=np.ascontiguousarray(x[c])) for c in range(B)]
    res = run_bass_kernel_spmd(nc, in_maps, core_ids=list(range(B)))
    return np.stack([res.results[c]["out"] for c in range(B)], axis=0)


if __name__ == "__main__":
    pass
